# version 08
# baseline (speedup 1.0000x reference)
import numpy as np
from contextlib import ExitStack

import concourse.bass as bass
import concourse.mybir as mybir
from concourse.bass_utils import run_bass_kernel_spmd

F32 = mybir.dt.float32
BF16 = mybir.dt.bfloat16
AF = mybir.ActivationFunctionType
ALU = mybir.AluOpType

D = 1024
NKC = D // 128
NH = 16
HD = 64
DL = 256
NE = 8
FF = 4096
FCB = 1024
NFC = FF // FCB
NFT = FCB // 128
QB = 512
EXPB = 4

LN_EPS = 1e-5
GATE_EPS = 1e-6
DECAY_SCALE = -(0.01 * 0.01)
ATT_SCALE_LNBIAS = float(np.log(1.0 / (np.sqrt(np.float64(1024.0)) + 1e-8)))


class Chan:

    def __init__(self, stack, nc, name):
        self.sem = stack.enter_context(nc.semaphore(name))
        self.n = 0

    def inc(self, inst, k=1):
        inst.then_inc(self.sem, k)
        self.n += k
        return self.n


def nsplit(total, maxn=512):
    return [(o, min(maxn, total - o)) for o in range(0, total, maxn)]


def dma_ser(eng, chan, out, in_):
    v = chan.inc(eng.dma_start(out, in_), 16)
    eng.wait_ge(chan.sem, v)
    return v


def build(S=4096, debug=False):
    NT = S // 128
    NQC = S // QB
    KT = NT
    NKG = KT // EXPB
    TH = S // 2
    NTTH = TH // 128
    TC = 1024 if TH % 1024 == 0 else TH
    NTC = TH // TC
    NTI = TC // 128

    nc = bass.Bass("TRN2", target_bir_lowering=False, debug=False)

    x_h = nc.dram_tensor("x", [S, D], F32, kind="ExternalInput").ap()
    wq_h = nc.dram_tensor("wq", [D, D], F32, kind="ExternalInput").ap()
    wk_h = nc.dram_tensor("wk", [D, D], F32, kind="ExternalInput").ap()
    wv_h = nc.dram_tensor("wv", [D, D], F32, kind="ExternalInput").ap()
    wo_h = nc.dram_tensor("wo", [D, D], F32, kind="ExternalInput").ap()
    wdn_h = nc.dram_tensor("wdn", [D, DL], F32, kind="ExternalInput").ap()
    wupk_h = nc.dram_tensor("wupk", [DL, D], F32, kind="ExternalInput").ap()
    wupv_h = nc.dram_tensor("wupv", [DL, D], F32, kind="ExternalInput").ap()
    gwt_h = nc.dram_tensor("gwt", [D, NE], F32, kind="ExternalInput").ap()
    gbias_h = nc.dram_tensor("gbias", [1, NE], F32, kind="ExternalInput").ap()
    wut_h = nc.dram_tensor("wut", [NE, D, FF], F32, kind="ExternalInput").ap()
    bu2_h = nc.dram_tensor("bu2", [NE, 128, FF // 128], F32, kind="ExternalInput").ap()
    wdt_h = nc.dram_tensor("wdt", [NE, FF, D], F32, kind="ExternalInput").ap()
    bd_h = nc.dram_tensor("bd", [NE, D], F32, kind="ExternalInput").ap()
    swut_h = nc.dram_tensor("swut", [D, FF], F32, kind="ExternalInput").ap()
    sbu2_h = nc.dram_tensor("sbu2", [128, FF // 128], F32, kind="ExternalInput").ap()
    swdt_h = nc.dram_tensor("swdt", [FF, D], F32, kind="ExternalInput").ap()
    sbd_h = nc.dram_tensor("sbd", [1, D], F32, kind="ExternalInput").ap()
    out_h = nc.dram_tensor("out", [S, D], F32, kind="ExternalOutput").ap()

    scratch_kind = "ExternalOutput" if debug else "Internal"
    xnt_sp = nc.dram_tensor("xnt_sp", [128, NKC * S], BF16, kind=scratch_kind).ap()
    pv_sp = nc.dram_tensor("pv_sp", [128, NKC * S], BF16, kind=scratch_kind).ap()
    q_sp = nc.dram_tensor("q_sp", [128, NKC * S], BF16, kind=scratch_kind).ap()
    comb_sp = nc.dram_tensor("comb_sp", [S, D], F32, kind=scratch_kind).ap()
    if debug:
        dbg_lat = nc.dram_tensor("dbg_lat", [128, 2 * S], BF16, kind="ExternalOutput").ap()
        dbg_q = nc.dram_tensor("dbg_q", [128, NKC * S], BF16, kind="ExternalOutput").ap()
        dbg_gsig = nc.dram_tensor("dbg_gsig", [128, NE * NT], F32, kind="ExternalOutput").ap()
        dbg_w = nc.dram_tensor("dbg_w", [128, NE * NT], F32, kind="ExternalOutput").ap()

    st = ExitStack()
    with st:
        def sbuf(name, shape, dtype, **kw):
            return st.enter_context(nc.sbuf_tensor(name, shape, dtype, **kw))

        ident = sbuf("ident", [128, 128], BF16)
        onesf = sbuf("onesf", [128, 128], BF16)
        consts = sbuf("consts", [128, 4], F32)
        mrv = sbuf("mrv", [128, 8], F32)
        gsig = sbuf("gsig", [128, NE * NT], F32)
        wgt = sbuf("wgt", [128, NE * NT], F32)
        wupk_b = sbuf("wupk_b", [128, 2 * D], BF16)
        wupv_b = sbuf("wupv_b", [128, 2 * D], BF16)
        gwt_b = sbuf("gwt_b", [128, NKC * NE], BF16)
        gbias_b = sbuf("gbias_b", [1, NE], BF16)
        bd_aug = sbuf("bd_aug", [32, D], BF16)
        wT_q = sbuf("wT_q", [32, (S // 512) * 128], BF16)
        wbf = sbuf("wbf", [128, 8], BF16)
        sbd_st = sbuf("sbd_st", [1, D], F32)
        sbd_bf = sbuf("sbd_bf", [1, D], BF16)

        s_gw = Chan(st, nc, "s_gw")
        s_qx = Chan(st, nc, "s_qx")
        s_gpc = Chan(st, nc, "s_gpc")
        s_v0 = Chan(st, nc, "s_v0")
        s_t1 = Chan(st, nc, "s_t1")
        s_s1 = Chan(st, nc, "s_s1")
        s_v1 = Chan(st, nc, "s_v1")
        s_t1b = Chan(st, nc, "s_t1b")
        s_v1b = Chan(st, nc, "s_v1b")
        s_t1c = Chan(st, nc, "s_t1c")
        s_s1c = Chan(st, nc, "s_s1c")
        s_v2s = Chan(st, nc, "s_v2s")
        s_s2s = Chan(st, nc, "s_s2s")
        s_t2p = Chan(st, nc, "s_t2p")
        s_v2p = Chan(st, nc, "s_v2p")
        s_t2m = Chan(st, nc, "s_t2m")
        s_s2m = Chan(st, nc, "s_s2m")
        s_v2m = Chan(st, nc, "s_v2m")
        s_t3 = Chan(st, nc, "s_t3")
        s_v3 = Chan(st, nc, "s_v3")
        s_t4w = Chan(st, nc, "s_t4w")
        s_v4w = Chan(st, nc, "s_v4w")
        s_t4 = Chan(st, nc, "s_t4")
        s_s4 = Chan(st, nc, "s_s4")
        s_v4 = Chan(st, nc, "s_v4")
        s_v5 = Chan(st, nc, "s_v5")

        dmarec = {}

        with nc.Block() as blk:
            @blk.gpsimd
            def _(gp):
                s_gpc.inc(gp.iota(ident[:, :], pattern=[[-1, 128]], base=0,
                                  channel_multiplier=1,
                                  allow_small_or_imprecise_dtypes=True))
                dmarec["gwt"] = dma_ser(gp, s_gw, gwt_b.ap().rearrange("p (kc e) -> p kc e", kc=NKC),
                    gwt_h.rearrange("(kc p) e -> p kc e", p=128))
                dmarec["gbias"] = dma_ser(gp, s_gw, gbias_b[:, :], gbias_h)
                dmarec["wupk"] = dma_ser(gp, s_gw, wupk_b.ap().rearrange("p (lc d) -> p lc d", lc=2),
                    wupk_h.rearrange("(lc p) d -> p lc d", p=128))
                dmarec["wupv"] = dma_ser(gp, s_gw, wupv_b.ap().rearrange("p (lc d) -> p lc d", lc=2),
                    wupv_h.rearrange("(lc p) d -> p lc d", p=128))
                dmarec["bd"] = dma_ser(gp, s_gw, bd_aug[0:8, :], bd_h)

            @blk.sync
            def _(sy):
                dmarec["sbd"] = dma_ser(sy, s_qx, sbd_st[:, :], sbd_h)
                sy.wait_ge(s_v0.sem, 5)
                dmarec["sbdrow"] = dma_ser(sy, s_qx, bd_aug[8:9, :], sbd_bf[:, :])

            @blk.vector
            def _(ve):
                ve.memset(consts[:, 0:1], LN_EPS)
                ve.memset(consts[:, 1:2], DECAY_SCALE)
                ve.memset(consts[:, 2:3], ATT_SCALE_LNBIAS)
                ve.memset(consts[:, 3:4], -1.0)
                ve.memset(onesf[:, :], 1.0)
                ve.memset(mrv[:, 2:8], -7.25e30)
                s_v0.inc(ve.memset(wT_q[0:32, :], 1.0))
                ve.wait_ge(s_gpc.sem, 1)
                s_v0.inc(ve.tensor_single_scalar(ident[:, :], ident[:, :], 0.0,
                                                 op=ALU.is_equal))
                ve.wait_ge(s_qx.sem, dmarec["sbd"])
                s_v0.inc(ve.tensor_single_scalar(sbd_st[:, :], sbd_st[:, :], 1.0,
                                                 op=ALU.mult))
                ve.wait_ge(s_v0.sem, 3)
                s_v0.inc(ve.tensor_single_scalar(sbd_st[:, :], sbd_st[:, :], 0.1,
                                                 op=ALU.mult))
                ve.wait_ge(s_v0.sem, 4)
                s_v0.inc(ve.tensor_copy(sbd_bf[:, :], sbd_st[:, :]))

        with ExitStack() as stp1:
            latT = stp1.enter_context(nc.sbuf_tensor("latT", [128, 2 * S], BF16, side="left"))
            with ExitStack() as stp1q:
                xnT = stp1q.enter_context(nc.sbuf_tensor("xnT", [128, NKC * S], BF16, side="right"))
                wq_b = stp1q.enter_context(nc.sbuf_tensor("wq_b", [128, NKC * D], BF16, side="right"))
                with ExitStack() as stp1b:
                    wk_b = stp1b.enter_context(nc.sbuf_tensor("wk_b", [128, NKC * D], BF16, side="right"))
                    wv_b = stp1b.enter_context(nc.sbuf_tensor("wv_b", [128, NKC * D], BF16, side="right"))
                    wdn_b = stp1b.enter_context(nc.sbuf_tensor("wdn_b", [128, NKC * DL], BF16, side="right"))
                    kv_b = stp1b.enter_context(nc.sbuf_tensor("kv_b", [128, NKC * 512], BF16, side="right"))
                    with ExitStack() as stp1a:
                        sb1 = lambda n, sh, dt: stp1a.enter_context(nc.sbuf_tensor(n, sh, dt, side="right"))
                        xst = sb1("xst", [128, 2 * D], F32)
                        stats = sb1("stats", [128, 2 * 12], F32)
                        mv = sb1("mv", [128, 2 * 2], F32)
                        rstd = sb1("rstd", [128, 2], F32)
                        nmr = sb1("nmr", [128, 2], F32)
                        xnbf = sb1("xnbf", [128, 2 * D], BF16)
                        m8 = sb1("m8", [128, 8], F32)
                        killed = sb1("killed", [128, 8], F32)
                        gmask = sb1("gmask", [128, 8], F32)
                        gden = sb1("gden", [128, 1], F32)

                        VQ1, SQ1, TQ1 = 13, 11, 9
                        with ExitStack() as stps, nc.Block() as blk:
                            trp = [stps.enter_context(nc.psum_tensor(f"trp{i}", [128, 128], BF16)) for i in range(2)]
                            gps = [stps.enter_context(nc.psum_tensor(f"gps{i}", [128, NE], F32)) for i in range(2)]

                            @blk.gpsimd
                            def _(gp):
                                for nm, dst, src in (("wk", wk_b, wk_h), ("wv", wv_b, wv_h),
                                                     ("wq", wq_b, wq_h)):
                                    dmarec[nm] = dma_ser(gp, s_gw, dst.ap().rearrange("p (kc d) -> p kc d", kc=NKC),
                                        src.rearrange("(kc p) d -> p kc d", p=128))
                                dmarec["wdn"] = dma_ser(gp, s_gw, wdn_b.ap().rearrange("p (kc l) -> p kc l", kc=NKC),
                                    wdn_h.rearrange("(kc p) l -> p kc l", p=128))

                            @blk.sync
                            def _(sy):
                                for tt in range(NT):
                                    if tt >= 2:
                                        sy.wait_ge(s_v1.sem, VQ1 * (tt - 2) + 1)
                                        sy.wait_ge(s_s1.sem, SQ1 * (tt - 2) + 2)
                                    dmarec[("xt", tt)] = dma_ser(sy, s_qx, xst[:, (tt % 2) * D:(tt % 2 + 1) * D],
                                        x_h[tt * 128:(tt + 1) * 128, :])

                            @blk.tensor
                            def _(te):
                                te.wait_ge(s_v0.sem, 2)
                                te.wait_ge(s_gw.sem, dmarec["gbias"])
                                for tt in range(NT):
                                    te.wait_ge(s_s1.sem, SQ1 * tt + 2)
                                    for i in range(8):
                                        gi = 8 * tt + i
                                        if gi >= 2:
                                            pj, pi = divmod(gi - 2, 8)
                                            te.wait_ge(s_s1.sem, SQ1 * pj + 3 + pi)
                                        s_t1.inc(te.transpose(
                                            trp[gi % 2][:, :],
                                            xnbf[:, (tt % 2) * D + i * 128:(tt % 2) * D + (i + 1) * 128],
                                            ident[:, :]))
                                    te.wait_ge(s_s1.sem, SQ1 * tt + 10)
                                    if tt >= 2:
                                        te.wait_ge(s_s1.sem, SQ1 * (tt - 2) + 11)
                                    for kc in range(NKC):
                                        te.matmul(gps[tt % 2][:, :],
                                                  xnT[:, kc * S + tt * 128:kc * S + (tt + 1) * 128],
                                                  gwt_b[:, kc * NE:(kc + 1) * NE],
                                                  start=(kc == 0), stop=False)
                                    s_t1.inc(te.matmul(gps[tt % 2][:, :], onesf[0:1, 0:128],
                                                       gbias_b[0:1, :], start=False, stop=True))
                                assert s_t1.n == TQ1 * NT

                            @blk.scalar
                            def _(sc):
                                for tt in range(NT):
                                    sc.wait_ge(s_v1.sem, VQ1 * tt + 2)
                                    s_s1.inc(sc.activation(rstd[:, tt % 2:tt % 2 + 1],
                                                           mv[:, (tt % 2) * 2 + 1:(tt % 2) * 2 + 2],
                                                           AF.Sqrt, bias=consts[:, 0:1]))
                                    sc.wait_ge(s_v1.sem, VQ1 * tt + 5)
                                    sc.wait_ge(s_qx.sem, dmarec[("xt", tt)])
                                    if tt >= 2:
                                        sc.wait_ge(s_t1.sem, TQ1 * (tt - 2) + 8)
                                    s_s1.inc(sc.activation(
                                        xnbf[:, (tt % 2) * D:(tt % 2 + 1) * D],
                                        xst[:, (tt % 2) * D:(tt % 2 + 1) * D],
                                        AF.Identity,
                                        bias=nmr[:, tt % 2:tt % 2 + 1],
                                        scale=rstd[:, tt % 2:tt % 2 + 1]))
                                    for i in range(8):
                                        sc.wait_ge(s_t1.sem, TQ1 * tt + i + 1)
                                        s_s1.inc(sc.activation(
                                            xnT[:, i * S + tt * 128:i * S + (tt + 1) * 128],
                                            trp[(8 * tt + i) % 2][:, :], AF.Identity))
                                    sc.wait_ge(s_t1.sem, TQ1 * tt + 9)
                                    s_s1.inc(sc.activation(gsig[:, tt * NE:(tt + 1) * NE],
                                                           gps[tt % 2][:, :], AF.Sigmoid))
                                assert s_s1.n == SQ1 * NT

                            @blk.vector
                            def _(ve):
                                ve.wait_ge(s_v0.sem, 5)
                                for tt in range(NT):
                                    b = tt % 2
                                    ve.wait_ge(s_qx.sem, dmarec[("xt", tt)])
                                    if tt >= 2:
                                        ve.wait_ge(s_v1.sem, VQ1 * (tt - 2) + 2)
                                    ve.bn_stats(stats[:, b * 12:b * 12 + 6],
                                                xst[:, b * D:b * D + 512])
                                    s_v1.inc(ve.bn_stats(stats[:, b * 12 + 6:b * 12 + 12],
                                                         xst[:, b * D + 512:b * D + 1024]))
                                    ve.wait_ge(s_v1.sem, VQ1 * tt + 1)
                                    if tt >= 2:
                                        ve.wait_ge(s_s1.sem, SQ1 * (tt - 2) + 1)
                                    s_v1.inc(ve.bn_aggr(mv[:, b * 2:b * 2 + 2],
                                                        stats[:, b * 12:b * 12 + 12]))
                                    ve.wait_ge(s_s1.sem, SQ1 * tt + 1)
                                    s_v1.inc(ve.reciprocal(rstd[:, b:b + 1], rstd[:, b:b + 1]))
                                    ve.wait_ge(s_v1.sem, VQ1 * tt + 3)
                                    s_v1.inc(ve.tensor_mul(nmr[:, b:b + 1], mv[:, b * 2:b * 2 + 1],
                                                           rstd[:, b:b + 1]))
                                    ve.wait_ge(s_v1.sem, VQ1 * tt + 4)
                                    s_v1.inc(ve.tensor_single_scalar(nmr[:, b:b + 1], nmr[:, b:b + 1],
                                                                     -1.0, op=ALU.mult))
                                    gs = gsig[:, tt * NE:(tt + 1) * NE]
                                    ve.wait_ge(s_s1.sem, SQ1 * tt + 11)
                                    s_v1.inc(ve.max(m8[:, :], gs))
                                    ve.wait_ge(s_v1.sem, VQ1 * tt + 6)
                                    if tt >= 1:
                                        ve.wait_ge(s_v1.sem, VQ1 * (tt - 1) + 8)
                                    s_v1.inc(ve.tensor_copy(mrv[:, 0:2], m8[:, 0:2]))
                                    ve.wait_ge(s_v1.sem, VQ1 * tt + 7)
                                    s_v1.inc(ve.match_replace(killed[:, :], mrv[:, :], gs, -1e30))
                                    ve.wait_ge(s_v1.sem, VQ1 * tt + 8)
                                    s_v1.inc(ve.tensor_tensor(gmask[:, :], gs, killed[:, :],
                                                              op=ALU.is_gt))
                                    s_v1.inc(ve.tensor_add(gden[:, :], m8[:, 0:1], m8[:, 1:2]))
                                    ve.wait_ge(s_v1.sem, VQ1 * tt + 10)
                                    s_v1.inc(ve.tensor_single_scalar(gden[:, :], gden[:, :],
                                                                     GATE_EPS, op=ALU.add))
                                    ve.wait_ge(s_v1.sem, VQ1 * tt + 11)
                                    s_v1.inc(ve.reciprocal(gden[:, :], gden[:, :]))
                                    ve.wait_ge(s_v1.sem, VQ1 * tt + 12)
                                    s_v1.inc(ve.scalar_tensor_tensor(
                                        wgt[:, tt * NE:(tt + 1) * NE], gs, gden[:, :],
                                        gmask[:, :], op0=ALU.mult, op1=ALU.mult))
                                assert s_v1.n == VQ1 * NT

                    with ExitStack() as stps, nc.Block() as blk:
                        kvp = [stps.enter_context(nc.psum_tensor(f"kvp{i}", [128, 512], F32)) for i in range(2)]
                        latp = [stps.enter_context(nc.psum_tensor(f"latp{i}", [128, 512], F32)) for i in range(2)]

                        @blk.sync
                        def _(sy):
                            dmarec["xnt_spill"] = dma_ser(sy, s_qx, xnt_sp, xnT[:, :])

                        @blk.tensor
                        def _(te):
                            te.wait_ge(s_gw.sem, dmarec["wdn"])
                            te.wait_ge(s_s1.sem, SQ1 * NT)
                            for nn in range(NQC):
                                for mt in range(8):
                                    gi = 10 * nn + mt
                                    if gi >= 2:
                                        te.wait_ge(s_v1b.sem, gi - 1)
                                    first = True
                                    last = None
                                    for wsrc in (wk_b, wv_b):
                                        for kc in range(NKC):
                                            last = te.matmul(
                                                kvp[mt % 2][:, :],
                                                wsrc[:, kc * D + mt * 128:kc * D + (mt + 1) * 128],
                                                xnT[:, kc * S + nn * 512:kc * S + (nn + 1) * 512],
                                                start=first,
                                                stop=(wsrc is wv_b and kc == NKC - 1))
                                            first = False
                                    s_t1b.inc(last)
                                te.wait_ge(s_v1b.sem, 10 * nn + 8)
                                for lt in range(2):
                                    gi = 10 * nn + 8 + lt
                                    if gi >= 2:
                                        te.wait_ge(s_v1b.sem, gi - 1)
                                    last = None
                                    for kc in range(NKC):
                                        last = te.matmul(
                                            latp[lt][:, :],
                                            wdn_b[:, kc * DL + lt * 128:kc * DL + (lt + 1) * 128],
                                            kv_b[:, kc * 512:(kc + 1) * 512],
                                            start=(kc == 0), stop=(kc == NKC - 1))
                                    s_t1b.inc(last)
                            assert s_t1b.n == 10 * NQC

                        @blk.vector
                        def _(ve):
                            for nn in range(NQC):
                                for mt in range(8):
                                    ve.wait_ge(s_t1b.sem, 10 * nn + mt + 1)
                                    if mt == 0 and nn >= 1:
                                        ve.wait_ge(s_t1b.sem, 10 * (nn - 1) + 10)
                                    s_v1b.inc(ve.tensor_copy(kv_b[:, mt * 512:(mt + 1) * 512],
                                                             kvp[mt % 2][:, :]))
                                for lt in range(2):
                                    ve.wait_ge(s_t1b.sem, 10 * nn + 9 + lt)
                                    s_v1b.inc(ve.tensor_copy(
                                        latT[:, lt * S + nn * 512:lt * S + (nn + 1) * 512],
                                        latp[lt][:, :]))
                            assert s_v1b.n == 10 * NQC

                qT = stp1q.enter_context(nc.sbuf_tensor("qT", [128, NKC * S], BF16, side="left"))
                with ExitStack() as stps, nc.Block() as blk:
                    qp = [stps.enter_context(nc.psum_tensor(f"qp{i}", [128, 512], F32)) for i in range(2)]

                    @blk.tensor
                    def _(te):
                        te.wait_ge(s_gw.sem, dmarec["wq"])
                        for mt in range(8):
                            for nn in range(NQC):
                                gi = mt * NQC + nn
                                if gi >= 2:
                                    te.wait_ge(s_s1c.sem, gi - 1)
                                last = None
                                for kc in range(NKC):
                                    last = te.matmul(
                                        qp[gi % 2][:, :],
                                        wq_b[:, kc * D + mt * 128:kc * D + (mt + 1) * 128],
                                        xnT[:, kc * S + nn * 512:kc * S + (nn + 1) * 512],
                                        start=(kc == 0), stop=(kc == NKC - 1))
                                s_t1c.inc(last)
                        assert s_t1c.n == 8 * NQC

                    @blk.scalar
                    def _(sc):
                        for mt in range(8):
                            for nn in range(NQC):
                                gi = mt * NQC + nn
                                sc.wait_ge(s_t1c.sem, gi + 1)
                                s_s1c.inc(sc.activation(
                                    qT[:, mt * S + nn * 512:mt * S + (nn + 1) * 512],
                                    qp[gi % 2][:, :], AF.Identity))
                        assert s_s1c.n == 8 * NQC

                    @blk.sync
                    def _(sy):
                        sy.wait_ge(s_qx.sem, dmarec["xnt_spill"])
                        sy.wait_ge(s_s1c.sem, 8 * NQC)
                        dmarec["q_spill"] = dma_ser(sy, s_qx, q_sp, qT[:, :])

                if debug:
                    with nc.Block() as blk:
                        @blk.sync
                        def _(sy):
                            dma_ser(sy, s_qx, dbg_lat, latT[:, :])
                            dma_ser(sy, s_qx, dbg_q, qT[:, :])
                            dma_ser(sy, s_qx, dbg_gsig, gsig[:, :])
                            dma_ser(sy, s_qx, dbg_w, wgt[:, :])
                            sy.wait_ge(s_qx.sem, s_qx.n)
            with ExitStack() as stp2:
                sb2 = lambda n, sh, dt: stp2.enter_context(nc.sbuf_tensor(n, sh, dt, side="right"))
                decayM = sb2("decayM", [128, 2 * S], BF16)
                with ExitStack() as stp2s:
                    dmraw = stp2s.enter_context(nc.sbuf_tensor("dmraw", [128, 2 * S], F32, side="right"))
                    with nc.Block() as blk:
                        @blk.gpsimd
                        def _(gp):
                            s_gpc.inc(gp.iota(dmraw[:, :], pattern=[[-1, 2 * S]], base=S - 1,
                                              channel_multiplier=1,
                                              allow_small_or_imprecise_dtypes=True))

                        @blk.scalar
                        def _(sc):
                            sc.wait_ge(s_gpc.sem, s_gpc.n)
                            s_s2s.inc(sc.activation(dmraw[:, :], dmraw[:, :], AF.Abs))
                            sc.wait_ge(s_s2s.sem, 1)
                            s_s2s.inc(sc.activation(decayM[:, :], dmraw[:, :], AF.Exp,
                                                    bias=consts[:, 2:3], scale=consts[:, 1:2]))

                krecT = sb2("krecT", [128, 2 * S], BF16)
                vrec = sb2("vrec", [128, NT * 260], BF16)
                sTb = [sb2(f"sTb{i}", [128, EXPB * 512], BF16) for i in range(2)]
                expT = [sb2(f"expT{i}", [128, EXPB * 512], BF16) for i in range(2)]
                pvT = sb2("pvT", [128, S], BF16)
                pvst = [sb2(f"pvst{i}", [64, 512], BF16) for i in range(2)]
                bcsb = sb2("bcsb", [64, 512], F32)
                rr_sb = sb2("rr_sb", [128, 512], BF16)
                rln = sb2("rln", [128, 512], F32)
                qstage = [sb2(f"qstage{i}", [128, 512], BF16) for i in range(2)]
                comb = sb2("comb", [128, (S // 512) * D], F32)
                xnTh = sb2("xnTh", [128, NKC * (S // 4)], BF16)
                wu_sb = sb2("wu_sb", [128, NKC * FCB], BF16)
                wd_sb = sb2("wd_sb", [128, NFT * D], BF16)
                a1s = sb2("a1s", [128, NFT * (512 if (S // 4) % 512 == 0 else S // 4)], BF16)
                bu_sb = sb2("bu_sb", [128, 9 * (FF // 128)], F32)

                NJ = 4 * NQC
                TQ2 = KT + NKG + 1
                SQ2 = NKG + 2
                VQ2 = KT + 2
                NFB = 9 * NFC
                QTOK = S // 4
                NTQ = QTOK // 128
                TCQ = 512 if QTOK % 512 == 0 else QTOK
                NTCQ = QTOK // TCQ
                NTIQ = TCQ // 128
                TQ4 = NTCQ * (NFT + NTIQ)
                SQ4 = NTCQ * NFT
                VQ4 = NTQ
                trec = {}
                oddrec = {}

                def sc_pos(kt):
                    return kt + 1 + max(0, kt // EXPB - 1)

                def make_sched():
                    att = [("att", j, u) for j in range(NJ) for u in range(NKG + 1)]
                    ffn = []
                    for fb in range(NFB):
                        for tc in range(NTCQ):
                            ffn += [("h1", fb, tc, ft) for ft in range(NFT)]
                            ffn += [("y", fb, tc, ti) for ti in range(NTIQ)]
                    sched = []
                    na, nf = len(att), len(ffn)
                    ia = jf = 0
                    for ia in range(na):
                        sched.append(att[ia])
                        target = (ia + 1) * nf // na
                        while jf < target:
                            sched.append(ffn[jf])
                            jf += 1
                    sched += ffn[jf:]
                    return sched

                for g in range(4):
                    with ExitStack() as stps, nc.Block() as blk:
                        krp = [stps.enter_context(nc.psum_tensor(f"krp{g}{i}", [128, 512], F32)) for i in range(2)]
                        vrp = [stps.enter_context(nc.psum_tensor(f"vrp{g}{i}", [128, 256], F32)) for i in range(2)]
                        trwp = stps.enter_context(nc.psum_tensor(f"trw{g}", [8, 128], BF16))
                        T0p, V0p = s_t2p.n, s_v2p.n
                        NKR = 2 * NQC

                        @blk.tensor
                        def _(te):
                            te.wait_ge(s_gw.sem, dmarec["wupv"])
                            if g == 0:
                                te.wait_ge(s_v1b.sem, 10 * NQC)
                            for mt in range(2):
                                for nn in range(NQC):
                                    gi = mt * NQC + nn
                                    if gi >= 2:
                                        te.wait_ge(s_v2p.sem, V0p + 1 + gi - 1)
                                    last = None
                                    for lc in range(2):
                                        last = te.matmul(
                                            krp[gi % 2][:, :],
                                            wupk_b[:, lc * D + g * 256 + mt * 128:lc * D + g * 256 + (mt + 1) * 128],
                                            latT[:, lc * S + nn * 512:lc * S + (nn + 1) * 512],
                                            start=(lc == 0), stop=(lc == 1))
                                    s_t2p.inc(last)
                            for tt in range(NT):
                                if tt >= 2:
                                    te.wait_ge(s_v2p.sem, V0p + 1 + NKR + tt - 1)
                                last = None
                                for lc in range(2):
                                    last = te.matmul(
                                        vrp[tt % 2][:, :],
                                        latT[:, lc * S + tt * 128:lc * S + (tt + 1) * 128],
                                        wupv_b[:, lc * D + g * 256:lc * D + (g + 1) * 256],
                                        start=(lc == 0), stop=(lc == 1))
                                s_t2p.inc(last)
                            for tt in range(NTQ):
                                te.wait_ge(s_v2p.sem, V0p + 1 + NKR + NT + 2 * tt + 1)
                                if tt >= 1:
                                    te.wait_ge(s_v2p.sem, V0p + 1 + NKR + NT + 2 * (tt - 1) + 2)
                                s_t2p.inc(te.transpose(trwp[:, :], wbf[:, :], ident[:, :]))
                            assert s_t2p.n == T0p + NKR + NT + NTQ

                        @blk.vector
                        def _(ve):
                            if g > 0:
                                ve.wait_ge(s_t2m.sem, TQ2 * NJ * g)
                            s_v2p.inc(ve.memset(vrec[:, :], 1.0))
                            for mt in range(2):
                                for nn in range(NQC):
                                    gi = mt * NQC + nn
                                    ve.wait_ge(s_t2p.sem, T0p + gi + 1)
                                    s_v2p.inc(ve.tensor_copy(
                                        krecT[:, mt * S + nn * 512:mt * S + (nn + 1) * 512],
                                        krp[gi % 2][:, :]))
                            ve.wait_ge(s_v2p.sem, V0p + 1)
                            for tt in range(NT):
                                ve.wait_ge(s_t2p.sem, T0p + NKR + tt + 1)
                                last = None
                                for lam in range(4):
                                    last = ve.tensor_copy(
                                        vrec[:, tt * 260 + lam * 65:tt * 260 + lam * 65 + 64],
                                        vrp[tt % 2][:, lam * 64:(lam + 1) * 64])
                                s_v2p.inc(last)
                            for tt in range(NTQ):
                                gt = g * NTQ + tt
                                if tt >= 1:
                                    ve.wait_ge(s_t2p.sem, T0p + NKR + NT + tt)
                                if g > 0:
                                    ve.wait_ge(s_t4w.sem, NTQ * g)
                                s_v2p.inc(ve.tensor_copy(wbf[:, :],
                                                         wgt[:, gt * NE:(gt + 1) * NE]))
                                ve.wait_ge(s_t2p.sem, T0p + NKR + NT + tt + 1)
                                s_v2p.inc(ve.tensor_copy(wT_q[0:8, tt * 128:(tt + 1) * 128],
                                                         trwp[:, :]))
                            assert s_v2p.n == V0p + 1 + NKR + NT + 2 * NTQ

                    with ExitStack() as stps, nc.Block() as blk:
                        scp = [stps.enter_context(nc.psum_tensor(f"scp{g}{i}", [128, 512], F32)) for i in range(2)]
                        pvp = [stps.enter_context(nc.psum_tensor(f"pvp{g}{i}", [128, 512], F32)) for i in range(2)]
                        bcp = stps.enter_context(nc.psum_tensor(f"bcp{g}", [64, 512], F32))
                        h1p = stps.enter_context(nc.psum_tensor(f"h1p{g}", [128, 512], F32))
                        yp = stps.enter_context(nc.psum_tensor(f"yp{g}", [128, D], F32))
                        v2p_done = s_v2p.n
                        s1c_done = s_s1c.n
                        T2M0 = s_t2m.n
                        FB0 = NFB * g
                        T4W0 = s_t4w.n
                        V4W0 = s_v4w.n

                        def fb_target(j):
                            return ((j + 1) * NFB + NJ - 1) // NJ if j < NJ - 1 else NFB

                        @blk.gpsimd
                        def _(gp):
                            for e in range(9):
                                for fc in range(NFC):
                                    b = FB0 + e * NFC + fc
                                    wsrc = wut_h[e, :, :] if e < NE else swut_h
                                    dsrc = wdt_h[e, :, :] if e < NE else swdt_h
                                    if b >= 1:
                                        gp.wait_ge(s_t4.sem, TQ4 * (b - 1) + (NTCQ - 1) * (NFT + NTIQ) + NFT)
                                    dmarec[("wu", b)] = dma_ser(gp, s_gw,
                                        wu_sb.ap().rearrange("p (kc f) -> p kc f", kc=NKC),
                                        wsrc[:, fc * FCB:(fc + 1) * FCB].rearrange(
                                            "(kc p) f -> p kc f", p=128))
                                    if b >= 1:
                                        gp.wait_ge(s_t4.sem, TQ4 * b)
                                    dmarec[("wd", b)] = dma_ser(gp, s_gw,
                                        wd_sb.ap().rearrange("p (fi d) -> p fi d", fi=NFT),
                                        dsrc[fc * FCB:(fc + 1) * FCB, :].rearrange(
                                            "(fi p) d -> p fi d", p=128))

                        @blk.sync
                        def _(sy):
                            dmarec[("xnth", g)] = dma_ser(sy, s_qx,
                                xnTh.ap().rearrange("p (kc t) -> p kc t", kc=NKC),
                                xnt_sp.rearrange("p (kc t) -> p kc t", kc=NKC)[:, :, g * QTOK:(g + 1) * QTOK])
                            for e in range(9):
                                src = bu2_h[e, :, :] if e < NE else sbu2_h
                                dmarec[("bu", g, e)] = dma_ser(sy, s_qx,
                                    bu_sb[:, e * 32:e * 32 + 32], src)
                            oj = 0

                            def emit_post(sy, j):
                                nonlocal oj
                                J = g * NJ + j
                                lam, qc = divmod(j, NQC)
                                if lam % 2 == 1:
                                    sy.wait_ge(s_v2m.sem, VQ2 * (J + 1))
                                    oddrec[("odd", g * 2 * NQC + oj)] = dma_ser(sy, s_qx,
                                        pvT[64:128, qc * 512:(qc + 1) * 512],
                                        pvst[oj % 2][0:64, :])
                                    oj += 1
                                    if qc == NQC - 1:
                                        ph = lam // 2
                                        dmarec[("pvspill", g, ph)] = dma_ser(sy, s_qx,
                                            pv_sp[:, (2 * g + ph) * S:(2 * g + ph + 1) * S], pvT[:, :])

                            for j in range(NJ):
                                J = g * NJ + j
                                lam, qc = divmod(j, NQC)
                                pbase = 64 * (lam % 2)
                                qtile = (g * 4 + lam) // 2
                                if j >= 2:
                                    sy.wait_ge(s_t2m.sem, T2M0 + TQ2 * (j - 2) + sc_pos(KT - 1))
                                dmarec[("q", J)] = dma_ser(sy, s_qx,
                                    qstage[j % 2][pbase:pbase + 64, :],
                                    q_sp[pbase:pbase + 64, qtile * S + qc * 512:qtile * S + (qc + 1) * 512])
                                if j >= 1:
                                    emit_post(sy, j - 1)
                            emit_post(sy, NJ - 1)
                            sy.wait_ge(s_v4.sem, VQ4 * NFB * (g + 1))
                            dmarec[("combspill", g)] = dma_ser(sy, s_qx,
                                comb_sp[g * QTOK:(g + 1) * QTOK, :].rearrange("(tt p) d -> p tt d", p=128),
                                comb.ap().rearrange("p (tt d) -> p tt d", tt=NTQ))

                        def ffn_tensor(te, fb):
                            b = FB0 + fb
                            te.wait_ge(s_gw.sem, dmarec[("wu", b)])
                            if fb == 0:
                                te.wait_ge(s_qx.sem, dmarec[("xnth", g)])
                                te.wait_ge(s_v4w.sem, NTQ * (g + 1))
                            for tc in range(NTCQ):
                                for ft in range(NFT):
                                    hh = SQ4 * b + tc * NFT + ft
                                    if hh >= 1:
                                        te.wait_ge(s_s4.sem, hh)
                                    last = None
                                    for kc in range(NKC):
                                        last = te.matmul(
                                            h1p[:, 0:TCQ],
                                            wu_sb[:, kc * FCB + ft * 128:kc * FCB + (ft + 1) * 128],
                                            xnTh[:, kc * QTOK + tc * TCQ:kc * QTOK + (tc + 1) * TCQ],
                                            start=(kc == 0), stop=(kc == NKC - 1))
                                    s_t4.inc(last)
                                te.wait_ge(s_gw.sem, dmarec[("wd", b)])
                                for ti in range(NTIQ):
                                    yy = VQ4 * b + tc * NTIQ + ti
                                    if yy >= 1:
                                        te.wait_ge(s_v4.sem, yy)
                                    te.wait_ge(s_s4.sem, SQ4 * b + NFT * (tc + 1))
                                    last = None
                                    for kc in range(NFT):
                                        for no, nw in nsplit(D):
                                            last = te.matmul(
                                                yp[:, no:no + nw],
                                                a1s[:, kc * TCQ + ti * 128:kc * TCQ + (ti + 1) * 128],
                                                wd_sb[:, kc * D + no:kc * D + no + nw],
                                                start=(kc == 0), stop=(kc == NFT - 1))
                                    s_t4.inc(last)

                        def ffn_scalar(sc, fb):
                            b = FB0 + fb
                            e, fc = divmod(fb, NFC)
                            if fc == 0:
                                sc.wait_ge(s_qx.sem, dmarec[("bu", g, e)])
                            for tc in range(NTCQ):
                                for ft in range(NFT):
                                    tv = TQ4 * b + tc * (NFT + NTIQ) + ft + 1
                                    sc.wait_ge(s_t4.sem, tv)
                                    s_s4.inc(sc.activation(
                                        a1s[:, ft * TCQ:(ft + 1) * TCQ],
                                        h1p[:, 0:TCQ], AF.Gelu,
                                        bias=bu_sb[:, e * 32 + fc * NFT + ft:
                                                   e * 32 + fc * NFT + ft + 1]))

                        def ffn_vector(ve, fb):
                            b = FB0 + fb
                            e, fc = divmod(fb, NFC)
                            for tc in range(NTCQ):
                                for ti in range(NTIQ):
                                    tt = tc * NTIQ + ti
                                    tv = TQ4 * b + tc * (NFT + NTIQ) + NFT + ti + 1
                                    ve.wait_ge(s_t4.sem, tv)
                                    gt = g * NTQ + tt
                                    scalar = (wgt[:, gt * NE + e:gt * NE + e + 1]
                                              if e < NE else 0.1)
                                    if fb == 0:
                                        ve.wait_ge(s_v4w.sem, V4W0 + tt + 1)
                                    else:
                                        ve.wait_ge(s_v4.sem, VQ4 * (b - 1) + tt + 1)
                                    s_v4.inc(ve.scalar_tensor_tensor(
                                        comb[:, tt * D:(tt + 1) * D],
                                        yp[:, :], scalar,
                                        comb[:, tt * D:(tt + 1) * D],
                                        op0=ALU.mult, op1=ALU.add))

                        @blk.tensor
                        def _(te):
                            te.wait_ge(s_v2p.sem, v2p_done)
                            te.wait_ge(s_s1c.sem, s1c_done)
                            te.wait_ge(s_gw.sem, dmarec["bd"])
                            te.wait_ge(s_qx.sem, dmarec["sbdrow"])
                            for tt in range(NTQ):
                                if tt >= 1:
                                    te.wait_ge(s_v4w.sem, V4W0 + tt)
                                last = None
                                for nn in range(2):
                                    last = te.matmul(yp[:, nn * 512:(nn + 1) * 512],
                                                     wT_q[0:9, tt * 128:(tt + 1) * 128],
                                                     bd_aug[0:9, nn * 512:(nn + 1) * 512],
                                                     start=True, stop=True)
                                s_t4w.inc(last)

                            def emit_pv(J, kg):
                                j = J - g * NJ
                                lam, qc = divmod(j, NQC)
                                te.wait_ge(s_s2m.sem, SQ2 * J + kg + 1)
                                if kg == 0 and J >= 2:
                                    te.wait_ge(s_v2m.sem, VQ2 * (J - 1))
                                last = None
                                for i in range(EXPB):
                                    kt = kg * EXPB + i
                                    last = te.matmul(
                                        pvp[J % 2][0:65, :],
                                        vrec[:, kt * 260 + lam * 65:kt * 260 + lam * 65 + 65],
                                        expT[kg % 2][:, i * 512:(i + 1) * 512],
                                        start=(kt == 0), stop=(kt == KT - 1))
                                trec[(J, "pv", kg)] = s_t2m.inc(last)

                            for tag in make_sched():
                                if tag[0] == "att":
                                    _, j, u = tag
                                    J = g * NJ + j
                                    lam, qc = divmod(j, NQC)
                                    pbase = 64 * (lam % 2)
                                    ktile = lam // 2
                                    if u < NKG:
                                        kg = u
                                        for i in range(EXPB):
                                            kt = kg * EXPB + i
                                            m = KT * J + kt
                                            if m >= 2:
                                                jj, kk = divmod(m - 2, KT)
                                                te.wait_ge(s_v2m.sem, VQ2 * jj + kk + 1)
                                            if kt == 0:
                                                te.wait_ge(s_qx.sem, dmarec[("q", J)])
                                            v = s_t2m.inc(te.matmul(
                                                scp[m % 2][:, :],
                                                krecT[pbase:pbase + 64,
                                                      ktile * S + kt * 128:ktile * S + (kt + 1) * 128],
                                                qstage[j % 2][pbase:pbase + 64, :],
                                                start=True, stop=True))
                                            trec[(J, "sc", kt)] = v
                                            assert v == T2M0 + TQ2 * j + sc_pos(kt), (v, j, kt)
                                        if kg >= 1:
                                            emit_pv(J, kg - 1)
                                    else:
                                        emit_pv(J, NKG - 1)
                                        te.wait_ge(s_s2m.sem, SQ2 * (J + 1))
                                        if J >= 1:
                                            te.wait_ge(s_v2m.sem, VQ2 * (J - 1) + KT + 1)
                                        trec[(J, "bc")] = s_t2m.inc(te.matmul(
                                            bcp[:, :], onesf[64:65, 0:64], rr_sb[64:65, :],
                                            start=True, stop=True))
                                elif tag[0] == "h1":
                                    _, fb, tc, ft = tag
                                    b = FB0 + fb
                                    if tc == 0 and ft == 0:
                                        te.wait_ge(s_gw.sem, dmarec[("wu", b)])
                                        if fb == 0:
                                            te.wait_ge(s_qx.sem, dmarec[("xnth", g)])
                                            te.wait_ge(s_v4w.sem, NTQ * (g + 1))
                                    hh = SQ4 * b + tc * NFT + ft
                                    if hh >= 1:
                                        te.wait_ge(s_s4.sem, hh)
                                    last = None
                                    for kc in range(NKC):
                                        last = te.matmul(
                                            h1p[:, 0:TCQ],
                                            wu_sb[:, kc * FCB + ft * 128:kc * FCB + (ft + 1) * 128],
                                            xnTh[:, kc * QTOK + tc * TCQ:kc * QTOK + (tc + 1) * TCQ],
                                            start=(kc == 0), stop=(kc == NKC - 1))
                                    s_t4.inc(last)
                                else:
                                    _, fb, tc, ti = tag
                                    b = FB0 + fb
                                    if ti == 0:
                                        te.wait_ge(s_gw.sem, dmarec[("wd", b)])
                                    yy = VQ4 * b + tc * NTIQ + ti
                                    if yy >= 1:
                                        te.wait_ge(s_v4.sem, yy)
                                    te.wait_ge(s_s4.sem, SQ4 * b + NFT * (tc + 1))
                                    last = None
                                    for kc in range(NFT):
                                        for no, nw in nsplit(D):
                                            last = te.matmul(
                                                yp[:, no:no + nw],
                                                a1s[:, kc * TCQ + ti * 128:kc * TCQ + (ti + 1) * 128],
                                                wd_sb[:, kc * D + no:kc * D + no + nw],
                                                start=(kc == 0), stop=(kc == NFT - 1))
                                    s_t4.inc(last)
                            assert s_t2m.n == TQ2 * NJ * (g + 1)
                            assert s_t4.n == TQ4 * NFB * (g + 1)

                        @blk.scalar
                        def _(sc):
                            for tag in make_sched():
                                if tag[0] == "att":
                                    _, j, u = tag
                                    J = g * NJ + j
                                    if u == 0:
                                        continue
                                    if u <= NKG:
                                        kg = u - 1
                                        pj = NKG * J + kg
                                        sc.wait_ge(s_v2m.sem, VQ2 * J + EXPB * (kg + 1))
                                        if pj >= 2:
                                            jj, kk = divmod(pj - 2, NKG)
                                            sc.wait_ge(s_t2m.sem, trec[(jj, "pv", kk)])
                                        s_s2m.inc(sc.activation(expT[kg % 2][:, :],
                                                                sTb[kg % 2][:, :], AF.Exp))
                                    if u == NKG:
                                        sc.wait_ge(s_t2m.sem, trec[(J, "pv", NKG - 1)])
                                        if J >= 1:
                                            sc.wait_ge(s_t2m.sem, trec[(J - 1, "bc")])
                                        s_s2m.inc(sc.activation(rln[64:65, :], pvp[J % 2][64:65, :],
                                                                AF.Ln))
                                        sc.wait_ge(s_s2m.sem, SQ2 * J + SQ2 - 1)
                                        s_s2m.inc(sc.activation(rr_sb[64:65, :], rln[64:65, :],
                                                                AF.Exp, scale=consts[64:65, 3:4]))
                                elif tag[0] == "h1":
                                    _, fb, tc, ft = tag
                                    b = FB0 + fb
                                    e, fc = divmod(fb, NFC)
                                    if fc == 0 and tc == 0 and ft == 0:
                                        sc.wait_ge(s_qx.sem, dmarec[("bu", g, e)])
                                    tv = TQ4 * b + tc * (NFT + NTIQ) + ft + 1
                                    sc.wait_ge(s_t4.sem, tv)
                                    s_s4.inc(sc.activation(
                                        a1s[:, ft * TCQ:(ft + 1) * TCQ],
                                        h1p[:, 0:TCQ], AF.Gelu,
                                        bias=bu_sb[:, e * 32 + fc * NFT + ft:
                                                   e * 32 + fc * NFT + ft + 1]))
                            assert s_s2m.n == SQ2 * NJ * (g + 1)
                            assert s_s4.n == SQ4 * NFB * (g + 1)

                        @blk.vector
                        def _(ve):
                            if g == 0:
                                ve.wait_ge(s_s2s.sem, 2)
                            ve.wait_ge(s_v2p.sem, v2p_done)
                            if g > 0:
                                ve.wait_ge(s_qx.sem, dmarec[("combspill", g - 1)])
                            for tt in range(NTQ):
                                ve.wait_ge(s_t4w.sem, T4W0 + tt + 1)
                                s_v4w.inc(ve.tensor_copy(comb[:, tt * D:(tt + 1) * D],
                                                         yp[:, :]))
                            for tag in make_sched():
                                if tag[0] == "att":
                                    _, j, u = tag
                                    J = g * NJ + j
                                    lam, qc = divmod(j, NQC)
                                    odd = (lam % 2 == 1)
                                    if u < NKG:
                                        kg = u
                                        for i in range(EXPB):
                                            kt = kg * EXPB + i
                                            pj = NKG * J + kg
                                            c0 = (S - 1) - 128 * kt + 512 * qc
                                            slot = sTb[kg % 2][:, (kt % EXPB) * 512:(kt % EXPB + 1) * 512]
                                            if pj >= 2:
                                                jj, kk = divmod(pj - 2, NKG)
                                                ve.wait_ge(s_s2m.sem, SQ2 * jj + kk + 1)
                                            ve.wait_ge(s_t2m.sem, trec[(J, "sc", kt)])
                                            s_v2m.inc(ve.tensor_mul(
                                                slot,
                                                scp[(KT * J + kt) % 2][:, :],
                                                decayM[:, c0:c0 + 512]))
                                    else:
                                        ve.wait_ge(s_t2m.sem, trec[(J, "bc")])
                                        if J >= 1:
                                            ve.wait_ge(s_v2m.sem, VQ2 * (J - 1) + KT + 2)
                                        s_v2m.inc(ve.tensor_copy(bcsb[:, :], bcp[:, :]))
                                        ve.wait_ge(s_v2m.sem, VQ2 * J + KT + 1)
                                        if odd:
                                            oj = g * 2 * NQC + sum(1 for jj2 in range(j) if (jj2 // NQC) % 2 == 1)
                                            if ("odd", oj - 2) in oddrec:
                                                ve.wait_ge(s_qx.sem, oddrec[("odd", oj - 2)])
                                            dst = pvst[oj % 2][0:64, :]
                                        else:
                                            if qc == 0 and lam // 2 == 1:
                                                ve.wait_ge(s_qx.sem, dmarec[("pvspill", g, 0)])
                                            dst = pvT[0:64, qc * 512:(qc + 1) * 512]
                                        s_v2m.inc(ve.tensor_mul(dst, pvp[J % 2][0:64, :], bcsb[:, :]))
                                elif tag[0] == "y":
                                    _, fb, tc, ti = tag
                                    b = FB0 + fb
                                    e, fc = divmod(fb, NFC)
                                    tt = tc * NTIQ + ti
                                    tv = TQ4 * b + tc * (NFT + NTIQ) + NFT + ti + 1
                                    ve.wait_ge(s_t4.sem, tv)
                                    gt = g * NTQ + tt
                                    scalar = (wgt[:, gt * NE + e:gt * NE + e + 1]
                                              if e < NE else 0.1)
                                    if fb == 0:
                                        ve.wait_ge(s_v4w.sem, V4W0 + tt + 1)
                                    else:
                                        ve.wait_ge(s_v4.sem, VQ4 * (b - 1) + tt + 1)
                                    s_v4.inc(ve.scalar_tensor_tensor(
                                        comb[:, tt * D:(tt + 1) * D],
                                        yp[:, :], scalar,
                                        comb[:, tt * D:(tt + 1) * D],
                                        op0=ALU.mult, op1=ALU.add))
                            assert s_v2m.n == VQ2 * NJ * (g + 1)
                            assert s_v4.n == VQ4 * NFB * (g + 1)
                            assert s_v4w.n == NTQ * (g + 1)

            with ExitStack() as stp3:
                sb3 = lambda n, sh, dt: stp3.enter_context(nc.sbuf_tensor(n, sh, dt, side="right"))
                pvF = sb3("pvF", [128, NKC * S], BF16)
                wo_b = sb3("wo_b", [128, NKC * D], BF16)
                ast = sb3("ast", [128, 2 * D], F32)
                cst = sb3("cst", [128, 2 * D], F32)
                xst = sb3("xst3", [128, 2 * D], F32)
                with ExitStack() as stps, nc.Block() as blk:
                    wop = [stps.enter_context(nc.psum_tensor(f"wop{i}", [128, 512], F32)) for i in range(2)]

                    @blk.gpsimd
                    def _(gp):
                        dmarec["wo"] = dma_ser(gp, s_gw,
                            wo_b.ap().rearrange("p (kc d) -> p kc d", kc=NKC),
                            wo_h.rearrange("(kc p) d -> p kc d", p=128))

                    @blk.sync
                    def _(sy):
                        sy.wait_ge(s_qx.sem, dmarec[("pvspill", 3, 1)])
                        dmarec["pvload"] = dma_ser(sy, s_qx, pvF[:, :], pv_sp)
                        for tt in range(NT):
                            if tt >= 2:
                                sy.wait_ge(s_v3.sem, 4 * (tt - 2) + 4)
                            dmarec[("c3", tt)] = dma_ser(sy, s_qx,
                                cst[:, (tt % 2) * D:(tt % 2 + 1) * D],
                                comb_sp[tt * 128:(tt + 1) * 128, :])
                            dmarec[("x3", tt)] = dma_ser(sy, s_qx,
                                xst[:, (tt % 2) * D:(tt % 2 + 1) * D],
                                x_h[tt * 128:(tt + 1) * 128, :])
                            sy.wait_ge(s_v3.sem, 4 * tt + 4)
                            dmarec[("out", tt)] = dma_ser(sy, s_qx,
                                out_h[tt * 128:(tt + 1) * 128, :],
                                ast[:, (tt % 2) * D:(tt % 2 + 1) * D])
                        sy.wait_ge(s_qx.sem, dmarec[("out", NT - 1)])

                    @blk.tensor
                    def _(te):
                        te.wait_ge(s_gw.sem, dmarec["wo"])
                        te.wait_ge(s_qx.sem, dmarec["pvload"])
                        for tt in range(NT):
                            for nn in range(2):
                                gi = 2 * tt + nn
                                if gi >= 2:
                                    te.wait_ge(s_v3.sem, 4 * (tt - 1) + nn + 1)
                                last = None
                                for kc in range(NKC):
                                    last = te.matmul(
                                        wop[gi % 2][:, :],
                                        pvF[:, kc * S + tt * 128:kc * S + (tt + 1) * 128],
                                        wo_b[:, kc * D + nn * 512:kc * D + (nn + 1) * 512],
                                        start=(kc == 0), stop=(kc == NKC - 1))
                                s_t3.inc(last)
                        assert s_t3.n == 2 * NT

                    @blk.vector
                    def _(ve):
                        for tt in range(NT):
                            for nn in range(2):
                                gi = 2 * tt + nn
                                ve.wait_ge(s_t3.sem, gi + 1)
                                if tt >= 2 and nn == 0:
                                    ve.wait_ge(s_qx.sem, dmarec[("out", tt - 2)])
                                s_v3.inc(ve.tensor_copy(
                                    ast[:, (tt % 2) * D + nn * 512:(tt % 2) * D + (nn + 1) * 512],
                                    wop[gi % 2][:, :]))
                            ve.wait_ge(s_qx.sem, dmarec[("x3", tt)])
                            ve.wait_ge(s_v3.sem, 4 * tt + 2)
                            s_v3.inc(ve.tensor_add(ast[:, (tt % 2) * D:(tt % 2 + 1) * D],
                                                   ast[:, (tt % 2) * D:(tt % 2 + 1) * D],
                                                   xst[:, (tt % 2) * D:(tt % 2 + 1) * D]))
                            ve.wait_ge(s_v3.sem, 4 * tt + 3)
                            s_v3.inc(ve.tensor_add(ast[:, (tt % 2) * D:(tt % 2 + 1) * D],
                                                   ast[:, (tt % 2) * D:(tt % 2 + 1) * D],
                                                   cst[:, (tt % 2) * D:(tt % 2 + 1) * D]))
                        assert s_v3.n == 4 * NT
    return nc


def _prep_inputs(L0_attn_x, L0_attn_Wq, L0_attn_Wk, L0_attn_Wv, L0_attn_Wo,
                 L0_attn_W_down_kv, L0_attn_W_up_k, L0_attn_W_up_v,
                 L0_ffn_gate_w, L0_ffn_gate_bias, L0_ffn_Wu, L0_ffn_bu,
                 L0_ffn_Wd, L0_ffn_bd, L0_ffn_sWu, L0_ffn_sbu, L0_ffn_sWd,
                 L0_ffn_sbd, S):
    c = np.ascontiguousarray
    f = lambda a: np.asarray(a, dtype=np.float32)
    shared = {
        "wq": c(f(L0_attn_Wq)),
        "wk": c(f(L0_attn_Wk)),
        "wv": c(f(L0_attn_Wv)),
        "wo": c(f(L0_attn_Wo)),
        "wdn": c(f(L0_attn_W_down_kv)),
        "wupk": c(f(L0_attn_W_up_k)),
        "wupv": c(f(L0_attn_W_up_v)),
        "gwt": c(f(L0_ffn_gate_w).T),
        "gbias": c(f(L0_ffn_gate_bias).reshape(1, NE)),
        "wut": c(f(L0_ffn_Wu).transpose(0, 2, 1)),
        "bu2": c(f(L0_ffn_bu).reshape(NE, FF // 128, 128).transpose(0, 2, 1)),
        "wdt": c(f(L0_ffn_Wd).transpose(0, 2, 1)),
        "bd": c(f(L0_ffn_bd)),
        "swut": c(f(L0_ffn_sWu).T),
        "sbu2": c(f(L0_ffn_sbu).reshape(FF // 128, 128).T),
        "swdt": c(f(L0_ffn_sWd).T),
        "sbd": c(f(L0_ffn_sbd).reshape(1, D)),
    }
    x = f(L0_attn_x)
    in_maps = []
    for b in range(x.shape[0]):
        m = dict(shared)
        m["x"] = c(x[b, :S, :])
        in_maps.append(m)
    return in_maps


def kernel(**inputs):
    S = 4096
    in_maps = _prep_inputs(S=S, **inputs)
    nc = build(S=S)
    res = run_bass_kernel_spmd(nc, in_maps, core_ids=list(range(len(in_maps))))
    return np.stack([r["out"] for r in res.results], axis=0)


# version 10
# speedup vs baseline: 1.3147x; 1.3147x over previous
import numpy as np
from contextlib import ExitStack

import concourse.bass as bass
import concourse.mybir as mybir
from concourse.bass_utils import run_bass_kernel_spmd

F32 = mybir.dt.float32
BF16 = mybir.dt.bfloat16
AF = mybir.ActivationFunctionType
ALU = mybir.AluOpType

D = 1024
NKC = D // 128
NH = 16
HD = 64
DL = 256
NE = 8
FF = 4096
FCB = 1024
NFC = FF // FCB
NFT = FCB // 128
QB = 512
EXPB = 4

LN_EPS = 1e-5
GATE_EPS = 1e-6
DECAY_SCALE = -(0.01 * 0.01)
ATT_SCALE_LNBIAS = float(np.log(1.0 / (np.sqrt(np.float64(1024.0)) + 1e-8)))


class Chan:

    def __init__(self, stack, nc, name):
        self.sem = stack.enter_context(nc.semaphore(name))
        self.n = 0

    def inc(self, inst, k=1):
        inst.then_inc(self.sem, k)
        self.n += k
        return self.n


def nsplit(total, maxn=512):
    return [(o, min(maxn, total - o)) for o in range(0, total, maxn)]


def dma_ser(eng, chan, out, in_):
    v = chan.inc(eng.dma_start(out, in_), 16)
    eng.wait_ge(chan.sem, v)
    return v


def build(S=4096, debug=False, force_tcq=None):
    NT = S // 128
    NQC = S // QB
    KT = NT
    NKG = KT // EXPB
    TH = S // 2
    NTTH = TH // 128
    TC = 1024 if TH % 1024 == 0 else TH
    NTC = TH // TC
    NTI = TC // 128

    nc = bass.Bass("TRN2", target_bir_lowering=False, debug=False)

    x_h = nc.dram_tensor("x", [S, D], F32, kind="ExternalInput").ap()
    wq_h = nc.dram_tensor("wq", [D, D], F32, kind="ExternalInput").ap()
    wk_h = nc.dram_tensor("wk", [D, D], F32, kind="ExternalInput").ap()
    wv_h = nc.dram_tensor("wv", [D, D], F32, kind="ExternalInput").ap()
    wo_h = nc.dram_tensor("wo", [D, D], F32, kind="ExternalInput").ap()
    wdn_h = nc.dram_tensor("wdn", [D, DL], F32, kind="ExternalInput").ap()
    wupk_h = nc.dram_tensor("wupk", [DL, D], F32, kind="ExternalInput").ap()
    wupv_h = nc.dram_tensor("wupv", [DL, D], F32, kind="ExternalInput").ap()
    gwt_h = nc.dram_tensor("gwt", [D, NE], F32, kind="ExternalInput").ap()
    gbias_h = nc.dram_tensor("gbias", [1, NE], F32, kind="ExternalInput").ap()
    wut_h = nc.dram_tensor("wut", [NE, D, FF], F32, kind="ExternalInput").ap()
    bu2_h = nc.dram_tensor("bu2", [NE, 128, FF // 128], F32, kind="ExternalInput").ap()
    wdt_h = nc.dram_tensor("wdt", [NE, FF, D], F32, kind="ExternalInput").ap()
    bd_h = nc.dram_tensor("bd", [NE, D], F32, kind="ExternalInput").ap()
    swut_h = nc.dram_tensor("swut", [D, FF], F32, kind="ExternalInput").ap()
    sbu2_h = nc.dram_tensor("sbu2", [128, FF // 128], F32, kind="ExternalInput").ap()
    swdt_h = nc.dram_tensor("swdt", [FF, D], F32, kind="ExternalInput").ap()
    sbd_h = nc.dram_tensor("sbd", [1, D], F32, kind="ExternalInput").ap()
    out_h = nc.dram_tensor("out", [S, D], F32, kind="ExternalOutput").ap()

    scratch_kind = "ExternalOutput" if debug else "Internal"
    xnt_sp = nc.dram_tensor("xnt_sp", [128, NKC * S], BF16, kind=scratch_kind).ap()
    pv_sp = nc.dram_tensor("pv_sp", [128, NKC * S], BF16, kind=scratch_kind).ap()
    q_sp = nc.dram_tensor("q_sp", [128, NKC * S], BF16, kind=scratch_kind).ap()
    rr_dram = nc.dram_tensor("rr_dram", [1, 512], BF16, kind="Internal").ap()
    comb_sp = nc.dram_tensor("comb_sp", [S, D], F32, kind=scratch_kind).ap()
    if debug:
        dbg_lat = nc.dram_tensor("dbg_lat", [128, 2 * S], BF16, kind="ExternalOutput").ap()
        dbg_q = nc.dram_tensor("dbg_q", [128, NKC * S], BF16, kind="ExternalOutput").ap()
        dbg_gsig = nc.dram_tensor("dbg_gsig", [128, NE * NT], F32, kind="ExternalOutput").ap()
        dbg_w = nc.dram_tensor("dbg_w", [128, NE * NT], F32, kind="ExternalOutput").ap()

    st = ExitStack()
    with st:
        def sbuf(name, shape, dtype, **kw):
            return st.enter_context(nc.sbuf_tensor(name, shape, dtype, **kw))

        ident = sbuf("ident", [128, 128], BF16)
        onesf = sbuf("onesf", [128, 128], BF16)
        consts = sbuf("consts", [128, 4], F32)
        mrv = sbuf("mrv", [128, 8], F32)
        gsig = sbuf("gsig", [128, NE * NT], F32)
        wgt = sbuf("wgt", [128, NE * NT], F32)
        wupk_b = sbuf("wupk_b", [128, 2 * D], BF16)
        wupv_b = sbuf("wupv_b", [128, 2 * D], BF16)
        gwt_b = sbuf("gwt_b", [128, NKC * NE], BF16)
        gbias_b = sbuf("gbias_b", [1, NE], BF16)
        bd_aug = sbuf("bd_aug", [32, D], BF16)
        wT_q = sbuf("wT_q", [32, (S // 512) * 128], BF16)
        wbf = sbuf("wbf", [128, 8], BF16)
        sbd_st = sbuf("sbd_st", [1, D], F32)
        sbd_bf = sbuf("sbd_bf", [1, D], BF16)

        s_gw = Chan(st, nc, "s_gw")
        s_qx = Chan(st, nc, "s_qx")
        s_gpc = Chan(st, nc, "s_gpc")
        s_v0 = Chan(st, nc, "s_v0")
        s_t1 = Chan(st, nc, "s_t1")
        s_s1 = Chan(st, nc, "s_s1")
        s_v1 = Chan(st, nc, "s_v1")
        s_t1b = Chan(st, nc, "s_t1b")
        s_v1b = Chan(st, nc, "s_v1b")
        s_t1c = Chan(st, nc, "s_t1c")
        s_s1c = Chan(st, nc, "s_s1c")
        s_v2s = Chan(st, nc, "s_v2s")
        s_s2s = Chan(st, nc, "s_s2s")
        s_t2p = Chan(st, nc, "s_t2p")
        s_v2p = Chan(st, nc, "s_v2p")
        s_t2m = Chan(st, nc, "s_t2m")
        s_s2m = Chan(st, nc, "s_s2m")
        s_v2m = Chan(st, nc, "s_v2m")
        s_t3 = Chan(st, nc, "s_t3")
        s_v3 = Chan(st, nc, "s_v3")
        s_t4w = Chan(st, nc, "s_t4w")
        s_v4w = Chan(st, nc, "s_v4w")
        s_t4 = Chan(st, nc, "s_t4")
        s_s4 = Chan(st, nc, "s_s4")
        s_v4 = Chan(st, nc, "s_v4")
        s_v5 = Chan(st, nc, "s_v5")

        dmarec = {}

        with nc.Block() as blk:
            @blk.gpsimd
            def _(gp):
                s_gpc.inc(gp.iota(ident[:, :], pattern=[[-1, 128]], base=0,
                                  channel_multiplier=1,
                                  allow_small_or_imprecise_dtypes=True))
                dmarec["gwt"] = dma_ser(gp, s_gw, gwt_b.ap().rearrange("p (kc e) -> p kc e", kc=NKC),
                    gwt_h.rearrange("(kc p) e -> p kc e", p=128))
                dmarec["gbias"] = dma_ser(gp, s_gw, gbias_b[:, :], gbias_h)
                dmarec["wupk"] = dma_ser(gp, s_gw, wupk_b.ap().rearrange("p (lc d) -> p lc d", lc=2),
                    wupk_h.rearrange("(lc p) d -> p lc d", p=128))
                dmarec["wupv"] = dma_ser(gp, s_gw, wupv_b.ap().rearrange("p (lc d) -> p lc d", lc=2),
                    wupv_h.rearrange("(lc p) d -> p lc d", p=128))
                dmarec["bd"] = dma_ser(gp, s_gw, bd_aug[0:8, :], bd_h)

            @blk.sync
            def _(sy):
                dmarec["sbd"] = dma_ser(sy, s_qx, sbd_st[:, :], sbd_h)
                sy.wait_ge(s_v0.sem, 5)
                dmarec["sbdrow"] = dma_ser(sy, s_qx, bd_aug[8:9, :], sbd_bf[:, :])

            @blk.vector
            def _(ve):
                ve.memset(consts[:, 0:1], LN_EPS)
                ve.memset(consts[:, 1:2], DECAY_SCALE)
                ve.memset(consts[:, 2:3], ATT_SCALE_LNBIAS)
                ve.memset(consts[:, 3:4], -1.0)
                ve.memset(onesf[:, :], 1.0)
                ve.memset(mrv[:, 2:8], -7.25e30)
                s_v0.inc(ve.memset(wT_q[0:32, :], 1.0))
                ve.wait_ge(s_gpc.sem, 1)
                s_v0.inc(ve.tensor_single_scalar(ident[:, :], ident[:, :], 0.0,
                                                 op=ALU.is_equal))
                ve.wait_ge(s_qx.sem, dmarec["sbd"])
                s_v0.inc(ve.tensor_single_scalar(sbd_st[:, :], sbd_st[:, :], 1.0,
                                                 op=ALU.mult))
                ve.wait_ge(s_v0.sem, 3)
                s_v0.inc(ve.tensor_single_scalar(sbd_st[:, :], sbd_st[:, :], 0.1,
                                                 op=ALU.mult))
                ve.wait_ge(s_v0.sem, 4)
                s_v0.inc(ve.tensor_copy(sbd_bf[:, :], sbd_st[:, :]))

        with ExitStack() as stp1:
            latT = stp1.enter_context(nc.sbuf_tensor("latT", [128, 2 * S], BF16, side="left"))
            with ExitStack() as stp1q:
                xnT = stp1q.enter_context(nc.sbuf_tensor("xnT", [128, NKC * S], BF16, side="right"))
                wq_b = stp1q.enter_context(nc.sbuf_tensor("wq_b", [128, NKC * D], BF16, side="right"))
                with ExitStack() as stp1b:
                    wk_b = stp1b.enter_context(nc.sbuf_tensor("wk_b", [128, NKC * D], BF16, side="right"))
                    wv_b = stp1b.enter_context(nc.sbuf_tensor("wv_b", [128, NKC * D], BF16, side="right"))
                    wdn_b = stp1b.enter_context(nc.sbuf_tensor("wdn_b", [128, NKC * DL], BF16, side="right"))
                    kv_b = stp1b.enter_context(nc.sbuf_tensor("kv_b", [128, NKC * 512], BF16, side="right"))
                    with ExitStack() as stp1a:
                        sb1 = lambda n, sh, dt: stp1a.enter_context(nc.sbuf_tensor(n, sh, dt, side="right"))
                        xst = sb1("xst", [128, 2 * D], F32)
                        stats = sb1("stats", [128, 2 * 12], F32)
                        mv = sb1("mv", [128, 2 * 2], F32)
                        rstd = sb1("rstd", [128, 2], F32)
                        nmr = sb1("nmr", [128, 2], F32)
                        xnbf = sb1("xnbf", [128, 2 * D], BF16)
                        m8 = sb1("m8", [128, 8], F32)
                        killed = sb1("killed", [128, 8], F32)
                        gmask = sb1("gmask", [128, 8], F32)
                        gden = sb1("gden", [128, 1], F32)

                        VQ1, SQ1, TQ1 = 13, 11, 9
                        with ExitStack() as stps, nc.Block() as blk:
                            trp = [stps.enter_context(nc.psum_tensor(f"trp{i}", [128, 128], BF16)) for i in range(2)]
                            gps = [stps.enter_context(nc.psum_tensor(f"gps{i}", [128, NE], F32)) for i in range(2)]

                            @blk.gpsimd
                            def _(gp):
                                for nm, dst, src in (("wk", wk_b, wk_h), ("wv", wv_b, wv_h),
                                                     ("wq", wq_b, wq_h)):
                                    dmarec[nm] = dma_ser(gp, s_gw, dst.ap().rearrange("p (kc d) -> p kc d", kc=NKC),
                                        src.rearrange("(kc p) d -> p kc d", p=128))
                                dmarec["wdn"] = dma_ser(gp, s_gw, wdn_b.ap().rearrange("p (kc l) -> p kc l", kc=NKC),
                                    wdn_h.rearrange("(kc p) l -> p kc l", p=128))

                            @blk.sync
                            def _(sy):
                                for tt in range(NT):
                                    if tt >= 2:
                                        sy.wait_ge(s_v1.sem, VQ1 * (tt - 2) + 1)
                                        sy.wait_ge(s_s1.sem, SQ1 * (tt - 2) + 2)
                                    dmarec[("xt", tt)] = dma_ser(sy, s_qx, xst[:, (tt % 2) * D:(tt % 2 + 1) * D],
                                        x_h[tt * 128:(tt + 1) * 128, :])

                            @blk.tensor
                            def _(te):
                                te.wait_ge(s_v0.sem, 2)
                                te.wait_ge(s_gw.sem, dmarec["gbias"])
                                for tt in range(NT):
                                    te.wait_ge(s_s1.sem, SQ1 * tt + 2)
                                    for i in range(8):
                                        gi = 8 * tt + i
                                        if gi >= 2:
                                            pj, pi = divmod(gi - 2, 8)
                                            te.wait_ge(s_s1.sem, SQ1 * pj + 3 + pi)
                                        s_t1.inc(te.transpose(
                                            trp[gi % 2][:, :],
                                            xnbf[:, (tt % 2) * D + i * 128:(tt % 2) * D + (i + 1) * 128],
                                            ident[:, :]))
                                    te.wait_ge(s_s1.sem, SQ1 * tt + 10)
                                    if tt >= 2:
                                        te.wait_ge(s_s1.sem, SQ1 * (tt - 2) + 11)
                                    for kc in range(NKC):
                                        te.matmul(gps[tt % 2][:, :],
                                                  xnT[:, kc * S + tt * 128:kc * S + (tt + 1) * 128],
                                                  gwt_b[:, kc * NE:(kc + 1) * NE],
                                                  start=(kc == 0), stop=False)
                                    s_t1.inc(te.matmul(gps[tt % 2][:, :], onesf[0:1, 0:128],
                                                       gbias_b[0:1, :], start=False, stop=True))
                                assert s_t1.n == TQ1 * NT

                            @blk.scalar
                            def _(sc):
                                for tt in range(NT):
                                    sc.wait_ge(s_v1.sem, VQ1 * tt + 2)
                                    s_s1.inc(sc.activation(rstd[:, tt % 2:tt % 2 + 1],
                                                           mv[:, (tt % 2) * 2 + 1:(tt % 2) * 2 + 2],
                                                           AF.Sqrt, bias=consts[:, 0:1]))
                                    sc.wait_ge(s_v1.sem, VQ1 * tt + 5)
                                    sc.wait_ge(s_qx.sem, dmarec[("xt", tt)])
                                    if tt >= 2:
                                        sc.wait_ge(s_t1.sem, TQ1 * (tt - 2) + 8)
                                    s_s1.inc(sc.activation(
                                        xnbf[:, (tt % 2) * D:(tt % 2 + 1) * D],
                                        xst[:, (tt % 2) * D:(tt % 2 + 1) * D],
                                        AF.Identity,
                                        bias=nmr[:, tt % 2:tt % 2 + 1],
                                        scale=rstd[:, tt % 2:tt % 2 + 1]))
                                    for i in range(8):
                                        sc.wait_ge(s_t1.sem, TQ1 * tt + i + 1)
                                        s_s1.inc(sc.activation(
                                            xnT[:, i * S + tt * 128:i * S + (tt + 1) * 128],
                                            trp[(8 * tt + i) % 2][:, :], AF.Identity))
                                    sc.wait_ge(s_t1.sem, TQ1 * tt + 9)
                                    s_s1.inc(sc.activation(gsig[:, tt * NE:(tt + 1) * NE],
                                                           gps[tt % 2][:, :], AF.Sigmoid))
                                assert s_s1.n == SQ1 * NT

                            @blk.vector
                            def _(ve):
                                ve.wait_ge(s_v0.sem, 5)
                                for tt in range(NT):
                                    b = tt % 2
                                    ve.wait_ge(s_qx.sem, dmarec[("xt", tt)])
                                    if tt >= 2:
                                        ve.wait_ge(s_v1.sem, VQ1 * (tt - 2) + 2)
                                    ve.bn_stats(stats[:, b * 12:b * 12 + 6],
                                                xst[:, b * D:b * D + 512])
                                    s_v1.inc(ve.bn_stats(stats[:, b * 12 + 6:b * 12 + 12],
                                                         xst[:, b * D + 512:b * D + 1024]))
                                    ve.wait_ge(s_v1.sem, VQ1 * tt + 1)
                                    if tt >= 2:
                                        ve.wait_ge(s_s1.sem, SQ1 * (tt - 2) + 1)
                                    s_v1.inc(ve.bn_aggr(mv[:, b * 2:b * 2 + 2],
                                                        stats[:, b * 12:b * 12 + 12]))
                                    ve.wait_ge(s_s1.sem, SQ1 * tt + 1)
                                    s_v1.inc(ve.reciprocal(rstd[:, b:b + 1], rstd[:, b:b + 1]))
                                    ve.wait_ge(s_v1.sem, VQ1 * tt + 3)
                                    s_v1.inc(ve.tensor_mul(nmr[:, b:b + 1], mv[:, b * 2:b * 2 + 1],
                                                           rstd[:, b:b + 1]))
                                    ve.wait_ge(s_v1.sem, VQ1 * tt + 4)
                                    s_v1.inc(ve.tensor_single_scalar(nmr[:, b:b + 1], nmr[:, b:b + 1],
                                                                     -1.0, op=ALU.mult))
                                    gs = gsig[:, tt * NE:(tt + 1) * NE]
                                    ve.wait_ge(s_s1.sem, SQ1 * tt + 11)
                                    s_v1.inc(ve.max(m8[:, :], gs))
                                    ve.wait_ge(s_v1.sem, VQ1 * tt + 6)
                                    if tt >= 1:
                                        ve.wait_ge(s_v1.sem, VQ1 * (tt - 1) + 8)
                                    s_v1.inc(ve.tensor_copy(mrv[:, 0:2], m8[:, 0:2]))
                                    ve.wait_ge(s_v1.sem, VQ1 * tt + 7)
                                    s_v1.inc(ve.match_replace(killed[:, :], mrv[:, :], gs, -1e30))
                                    ve.wait_ge(s_v1.sem, VQ1 * tt + 8)
                                    s_v1.inc(ve.tensor_tensor(gmask[:, :], gs, killed[:, :],
                                                              op=ALU.is_gt))
                                    s_v1.inc(ve.tensor_add(gden[:, :], m8[:, 0:1], m8[:, 1:2]))
                                    ve.wait_ge(s_v1.sem, VQ1 * tt + 10)
                                    s_v1.inc(ve.tensor_single_scalar(gden[:, :], gden[:, :],
                                                                     GATE_EPS, op=ALU.add))
                                    ve.wait_ge(s_v1.sem, VQ1 * tt + 11)
                                    s_v1.inc(ve.reciprocal(gden[:, :], gden[:, :]))
                                    ve.wait_ge(s_v1.sem, VQ1 * tt + 12)
                                    s_v1.inc(ve.scalar_tensor_tensor(
                                        wgt[:, tt * NE:(tt + 1) * NE], gs, gden[:, :],
                                        gmask[:, :], op0=ALU.mult, op1=ALU.mult))
                                assert s_v1.n == VQ1 * NT

                    with ExitStack() as stps, nc.Block() as blk:
                        kvp = [stps.enter_context(nc.psum_tensor(f"kvp{i}", [128, 512], F32)) for i in range(2)]
                        latp = [stps.enter_context(nc.psum_tensor(f"latp{i}", [128, 512], F32)) for i in range(2)]

                        @blk.sync
                        def _(sy):
                            dmarec["xnt_spill"] = dma_ser(sy, s_qx, xnt_sp, xnT[:, :])

                        @blk.tensor
                        def _(te):
                            te.wait_ge(s_gw.sem, dmarec["wdn"])
                            te.wait_ge(s_s1.sem, SQ1 * NT)
                            for nn in range(NQC):
                                for mt in range(8):
                                    gi = 10 * nn + mt
                                    if gi >= 2:
                                        te.wait_ge(s_v1b.sem, gi - 1)
                                    first = True
                                    last = None
                                    for wsrc in (wk_b, wv_b):
                                        for kc in range(NKC):
                                            last = te.matmul(
                                                kvp[mt % 2][:, :],
                                                wsrc[:, kc * D + mt * 128:kc * D + (mt + 1) * 128],
                                                xnT[:, kc * S + nn * 512:kc * S + (nn + 1) * 512],
                                                start=first,
                                                stop=(wsrc is wv_b and kc == NKC - 1))
                                            first = False
                                    s_t1b.inc(last)
                                te.wait_ge(s_v1b.sem, 10 * nn + 8)
                                for lt in range(2):
                                    gi = 10 * nn + 8 + lt
                                    if gi >= 2:
                                        te.wait_ge(s_v1b.sem, gi - 1)
                                    last = None
                                    for kc in range(NKC):
                                        last = te.matmul(
                                            latp[lt][:, :],
                                            wdn_b[:, kc * DL + lt * 128:kc * DL + (lt + 1) * 128],
                                            kv_b[:, kc * 512:(kc + 1) * 512],
                                            start=(kc == 0), stop=(kc == NKC - 1))
                                    s_t1b.inc(last)
                            assert s_t1b.n == 10 * NQC

                        @blk.vector
                        def _(ve):
                            for nn in range(NQC):
                                for mt in range(8):
                                    ve.wait_ge(s_t1b.sem, 10 * nn + mt + 1)
                                    if mt == 0 and nn >= 1:
                                        ve.wait_ge(s_t1b.sem, 10 * (nn - 1) + 10)
                                    s_v1b.inc(ve.tensor_copy(kv_b[:, mt * 512:(mt + 1) * 512],
                                                             kvp[mt % 2][:, :]))
                                for lt in range(2):
                                    ve.wait_ge(s_t1b.sem, 10 * nn + 9 + lt)
                                    s_v1b.inc(ve.tensor_copy(
                                        latT[:, lt * S + nn * 512:lt * S + (nn + 1) * 512],
                                        latp[lt][:, :]))
                            assert s_v1b.n == 10 * NQC

                qT = stp1q.enter_context(nc.sbuf_tensor("qT", [128, NKC * S], BF16, side="left"))
                with ExitStack() as stps, nc.Block() as blk:
                    qp = [stps.enter_context(nc.psum_tensor(f"qp{i}", [128, 512], F32)) for i in range(2)]

                    @blk.tensor
                    def _(te):
                        te.wait_ge(s_gw.sem, dmarec["wq"])
                        for mt in range(8):
                            for nn in range(NQC):
                                gi = mt * NQC + nn
                                if gi >= 2:
                                    te.wait_ge(s_s1c.sem, gi - 1)
                                last = None
                                for kc in range(NKC):
                                    last = te.matmul(
                                        qp[gi % 2][:, :],
                                        wq_b[:, kc * D + mt * 128:kc * D + (mt + 1) * 128],
                                        xnT[:, kc * S + nn * 512:kc * S + (nn + 1) * 512],
                                        start=(kc == 0), stop=(kc == NKC - 1))
                                s_t1c.inc(last)
                        assert s_t1c.n == 8 * NQC

                    @blk.scalar
                    def _(sc):
                        for mt in range(8):
                            for nn in range(NQC):
                                gi = mt * NQC + nn
                                sc.wait_ge(s_t1c.sem, gi + 1)
                                s_s1c.inc(sc.activation(
                                    qT[:, mt * S + nn * 512:mt * S + (nn + 1) * 512],
                                    qp[gi % 2][:, :], AF.Identity))
                        assert s_s1c.n == 8 * NQC

                    @blk.sync
                    def _(sy):
                        sy.wait_ge(s_qx.sem, dmarec["xnt_spill"])
                        sy.wait_ge(s_s1c.sem, 8 * NQC)
                        dmarec["q_spill"] = dma_ser(sy, s_qx, q_sp, qT[:, :])

                if debug:
                    with nc.Block() as blk:
                        @blk.sync
                        def _(sy):
                            dma_ser(sy, s_qx, dbg_lat, latT[:, :])
                            dma_ser(sy, s_qx, dbg_q, qT[:, :])
                            dma_ser(sy, s_qx, dbg_gsig, gsig[:, :])
                            dma_ser(sy, s_qx, dbg_w, wgt[:, :])
                            sy.wait_ge(s_qx.sem, s_qx.n)
            with ExitStack() as stp2:
                sb2 = lambda n, sh, dt: stp2.enter_context(nc.sbuf_tensor(n, sh, dt, side="right"))
                decayM = sb2("decayM", [128, 2 * S], BF16)
                with ExitStack() as stp2s:
                    dmraw = stp2s.enter_context(nc.sbuf_tensor("dmraw", [128, 2 * S], F32, side="right"))
                    with nc.Block() as blk:
                        @blk.gpsimd
                        def _(gp):
                            s_gpc.inc(gp.iota(dmraw[:, :], pattern=[[-1, 2 * S]], base=S - 1,
                                              channel_multiplier=1,
                                              allow_small_or_imprecise_dtypes=True))

                        @blk.scalar
                        def _(sc):
                            sc.wait_ge(s_gpc.sem, s_gpc.n)
                            s_s2s.inc(sc.activation(dmraw[:, :], dmraw[:, :], AF.Abs))
                            sc.wait_ge(s_s2s.sem, 1)
                            s_s2s.inc(sc.activation(decayM[:, :], dmraw[:, :], AF.Exp,
                                                    bias=consts[:, 2:3], scale=consts[:, 1:2]))

                krecT = sb2("krecT", [128, 2 * S], BF16)
                vrec = sb2("vrec", [128, NT * 260], BF16)
                sTb = [sb2(f"sTb{i}", [128, EXPB * 512], BF16) for i in range(2)]
                expT = [sb2(f"expT{i}", [128, EXPB * 512], BF16) for i in range(2)]
                pvT = sb2("pvT", [128, S], BF16)
                pvst = [sb2(f"pvst{i}", [64, 512], BF16) for i in range(2)]
                bcsb = sb2("bcsb", [64, 512], BF16)
                rr_sb = sb2("rr_sb", [128, 512], BF16)
                rln = sb2("rln", [128, 512], F32)
                qstage = [sb2(f"qstage{i}", [128, 512], BF16) for i in range(2)]
                comb = sb2("comb", [128, (S // 512) * D], F32)
                xnTh = sb2("xnTh", [128, NKC * (S // 4)], BF16)
                wu_sb = sb2("wu_sb", [128, NKC * FCB], BF16)
                wd_sb = sb2("wd_sb", [128, NFT * D], BF16)
                a1s = sb2("a1s", [128, NFT * (512 if (S // 4) % 512 == 0 else S // 4)], BF16)
                bu_sb = sb2("bu_sb", [128, 9 * (FF // 128)], F32)

                NJ = 4 * NQC
                TQ2 = KT + NKG
                SQ2 = NKG + 2
                VQ2 = KT + 1
                NFB = 9 * NFC
                QTOK = S // 4
                NTQ = QTOK // 128
                TCQ = force_tcq or (512 if QTOK % 512 == 0 else QTOK)
                NTCQ = QTOK // TCQ
                NTIQ = TCQ // 128
                TQ4 = NTCQ * (NFT + 2 * NTIQ)
                SQ4 = NTCQ * NFT
                VQ4 = 2 * NTQ
                trec = {}
                oddrec = {}

                def sc_pos(kt):
                    return kt + 1 + max(0, kt // EXPB - 1)

                def make_sched():
                    att = [("att", j, u) for j in range(NJ) for u in range(NKG + 1)]
                    ffn = []
                    for fb in range(NFB):
                        for tc in range(NTCQ):
                            ffn += [("h1", fb, tc, ft) for ft in range(NFT)]
                            ffn += [("y", fb, tc, ti) for ti in range(NTIQ)]
                    sched = []
                    na, nf = len(att), len(ffn)
                    ia = jf = 0
                    for ia in range(na):
                        sched.append(att[ia])
                        target = (ia + 1) * nf // na
                        while jf < target:
                            sched.append(ffn[jf])
                            jf += 1
                    sched += ffn[jf:]
                    return sched

                for g in range(4):
                    with ExitStack() as stps, nc.Block() as blk:
                        krp = [stps.enter_context(nc.psum_tensor(f"krp{g}{i}", [128, 512], F32)) for i in range(2)]
                        vrp = [stps.enter_context(nc.psum_tensor(f"vrp{g}{i}", [128, 256], F32)) for i in range(2)]
                        trwp = stps.enter_context(nc.psum_tensor(f"trw{g}", [8, 128], BF16))
                        T0p, V0p = s_t2p.n, s_v2p.n
                        NKR = 2 * NQC

                        @blk.tensor
                        def _(te):
                            te.wait_ge(s_gw.sem, dmarec["wupv"])
                            if g == 0:
                                te.wait_ge(s_v1b.sem, 10 * NQC)
                            for mt in range(2):
                                for nn in range(NQC):
                                    gi = mt * NQC + nn
                                    if gi >= 2:
                                        te.wait_ge(s_v2p.sem, V0p + 1 + gi - 1)
                                    last = None
                                    for lc in range(2):
                                        last = te.matmul(
                                            krp[gi % 2][:, :],
                                            wupk_b[:, lc * D + g * 256 + mt * 128:lc * D + g * 256 + (mt + 1) * 128],
                                            latT[:, lc * S + nn * 512:lc * S + (nn + 1) * 512],
                                            start=(lc == 0), stop=(lc == 1))
                                    s_t2p.inc(last)
                            for tt in range(NT):
                                if tt >= 2:
                                    te.wait_ge(s_v2p.sem, V0p + 1 + NKR + tt - 1)
                                last = None
                                for lc in range(2):
                                    last = te.matmul(
                                        vrp[tt % 2][:, :],
                                        latT[:, lc * S + tt * 128:lc * S + (tt + 1) * 128],
                                        wupv_b[:, lc * D + g * 256:lc * D + (g + 1) * 256],
                                        start=(lc == 0), stop=(lc == 1))
                                s_t2p.inc(last)
                            for tt in range(NTQ):
                                te.wait_ge(s_v2p.sem, V0p + 1 + NKR + NT + 2 * tt + 1)
                                if tt >= 1:
                                    te.wait_ge(s_v2p.sem, V0p + 1 + NKR + NT + 2 * (tt - 1) + 2)
                                s_t2p.inc(te.transpose(trwp[:, :], wbf[:, :], ident[:, :]))
                            assert s_t2p.n == T0p + NKR + NT + NTQ

                        @blk.vector
                        def _(ve):
                            if g > 0:
                                ve.wait_ge(s_t2m.sem, TQ2 * NJ * g)
                            s_v2p.inc(ve.memset(vrec[:, :], 1.0))
                            for mt in range(2):
                                for nn in range(NQC):
                                    gi = mt * NQC + nn
                                    ve.wait_ge(s_t2p.sem, T0p + gi + 1)
                                    s_v2p.inc(ve.tensor_copy(
                                        krecT[:, mt * S + nn * 512:mt * S + (nn + 1) * 512],
                                        krp[gi % 2][:, :]))
                            ve.wait_ge(s_v2p.sem, V0p + 1)
                            for tt in range(NT):
                                ve.wait_ge(s_t2p.sem, T0p + NKR + tt + 1)
                                last = None
                                for lam in range(4):
                                    last = ve.tensor_copy(
                                        vrec[:, tt * 260 + lam * 65:tt * 260 + lam * 65 + 64],
                                        vrp[tt % 2][:, lam * 64:(lam + 1) * 64])
                                s_v2p.inc(last)
                            for tt in range(NTQ):
                                gt = g * NTQ + tt
                                if tt >= 1:
                                    ve.wait_ge(s_t2p.sem, T0p + NKR + NT + tt)
                                if g > 0:
                                    ve.wait_ge(s_t4w.sem, NTQ * g)
                                s_v2p.inc(ve.tensor_copy(wbf[:, :],
                                                         wgt[:, gt * NE:(gt + 1) * NE]))
                                ve.wait_ge(s_t2p.sem, T0p + NKR + NT + tt + 1)
                                s_v2p.inc(ve.tensor_copy(wT_q[0:8, tt * 128:(tt + 1) * 128],
                                                         trwp[:, :]))
                            assert s_v2p.n == V0p + 1 + NKR + NT + 2 * NTQ

                    with ExitStack() as stps, nc.Block() as blk:
                        scp = [stps.enter_context(nc.psum_tensor(f"scp{g}{i}", [128, 512], F32)) for i in range(2)]
                        pvp = [stps.enter_context(nc.psum_tensor(f"pvp{g}{i}", [128, 512], F32)) for i in range(2)]
                        h1p = [stps.enter_context(nc.psum_tensor(f"h1p{g}{i}", [128, TCQ], F32)) for i in range(2)]
                        yp = [stps.enter_context(nc.psum_tensor(f"yp{g}{i}", [128, 512], F32)) for i in range(2)]
                        v2p_done = s_v2p.n
                        s1c_done = s_s1c.n
                        T2M0 = s_t2m.n
                        FB0 = NFB * g
                        T4W0 = s_t4w.n
                        V4W0 = s_v4w.n

                        def fb_target(j):
                            return ((j + 1) * NFB + NJ - 1) // NJ if j < NJ - 1 else NFB

                        @blk.gpsimd
                        def _(gp):
                            for e in range(9):
                                for fc in range(NFC):
                                    b = FB0 + e * NFC + fc
                                    wsrc = wut_h[e, :, :] if e < NE else swut_h
                                    dsrc = wdt_h[e, :, :] if e < NE else swdt_h
                                    if b >= 1:
                                        gp.wait_ge(s_t4.sem, TQ4 * (b - 1) + (NTCQ - 1) * (NFT + 2 * NTIQ) + NFT)
                                    dmarec[("wu", b)] = dma_ser(gp, s_gw,
                                        wu_sb.ap().rearrange("p (kc f) -> p kc f", kc=NKC),
                                        wsrc[:, fc * FCB:(fc + 1) * FCB].rearrange(
                                            "(kc p) f -> p kc f", p=128))
                                    if b >= 1:
                                        gp.wait_ge(s_t4.sem, TQ4 * b)
                                    dmarec[("wd", b)] = dma_ser(gp, s_gw,
                                        wd_sb.ap().rearrange("p (fi d) -> p fi d", fi=NFT),
                                        dsrc[fc * FCB:(fc + 1) * FCB, :].rearrange(
                                            "(fi p) d -> p fi d", p=128))

                        @blk.sync
                        def _(sy):
                            dmarec[("xnth", g)] = dma_ser(sy, s_qx,
                                xnTh.ap().rearrange("p (kc t) -> p kc t", kc=NKC),
                                xnt_sp.rearrange("p (kc t) -> p kc t", kc=NKC)[:, :, g * QTOK:(g + 1) * QTOK])
                            for e in range(9):
                                src = bu2_h[e, :, :] if e < NE else sbu2_h
                                dmarec[("bu", g, e)] = dma_ser(sy, s_qx,
                                    bu_sb[:, e * 32:e * 32 + 32], src)
                            oj = 0

                            def emit_post(sy, j):
                                nonlocal oj
                                J = g * NJ + j
                                lam, qc = divmod(j, NQC)
                                if lam % 2 == 1:
                                    sy.wait_ge(s_v2m.sem, VQ2 * (J + 1))
                                    oddrec[("odd", g * 2 * NQC + oj)] = dma_ser(sy, s_qx,
                                        pvT[64:128, qc * 512:(qc + 1) * 512],
                                        pvst[oj % 2][0:64, :])
                                    oj += 1
                                    if qc == NQC - 1:
                                        ph = lam // 2
                                        dmarec[("pvspill", g, ph)] = dma_ser(sy, s_qx,
                                            pv_sp[:, (2 * g + ph) * S:(2 * g + ph + 1) * S], pvT[:, :])

                            for j in range(NJ):
                                J = g * NJ + j
                                lam, qc = divmod(j, NQC)
                                pbase = 64 * (lam % 2)
                                qtile = (g * 4 + lam) // 2
                                if j >= 2:
                                    sy.wait_ge(s_t2m.sem, T2M0 + TQ2 * (j - 2) + sc_pos(KT - 1))
                                dmarec[("q", J)] = dma_ser(sy, s_qx,
                                    qstage[j % 2][pbase:pbase + 64, :],
                                    q_sp[pbase:pbase + 64, qtile * S + qc * 512:qtile * S + (qc + 1) * 512])
                                if j >= 1:
                                    sy.wait_ge(s_s2m.sem, SQ2 * (J - 1) + SQ2)
                                    sy.wait_ge(s_v2m.sem, VQ2 * (J - 1))
                                    dma_ser(sy, s_qx, rr_dram, rr_sb[64:65, :])
                                    dmarec[("bc", J - 1)] = dma_ser(sy, s_qx,
                                        bcsb[0:64, :],
                                        bass.AP(rr_dram.tensor, 0, [[0, 64], [1, 512]]))
                                    emit_post(sy, j - 1)
                            sy.wait_ge(s_s2m.sem, SQ2 * (g * NJ + NJ))
                            sy.wait_ge(s_v2m.sem, VQ2 * (g * NJ + NJ - 1))
                            dma_ser(sy, s_qx, rr_dram, rr_sb[64:65, :])
                            dmarec[("bc", g * NJ + NJ - 1)] = dma_ser(sy, s_qx,
                                bcsb[0:64, :],
                                bass.AP(rr_dram.tensor, 0, [[0, 64], [1, 512]]))
                            emit_post(sy, NJ - 1)
                            sy.wait_ge(s_v4.sem, VQ4 * NFB * (g + 1))
                            dmarec[("combspill", g)] = dma_ser(sy, s_qx,
                                comb_sp[g * QTOK:(g + 1) * QTOK, :].rearrange("(tt p) d -> p tt d", p=128),
                                comb.ap().rearrange("p (tt d) -> p tt d", tt=NTQ))

                        def ffn_tensor(te, fb):
                            b = FB0 + fb
                            te.wait_ge(s_gw.sem, dmarec[("wu", b)])
                            if fb == 0:
                                te.wait_ge(s_qx.sem, dmarec[("xnth", g)])
                                te.wait_ge(s_v4w.sem, NTQ * (g + 1))
                            for tc in range(NTCQ):
                                for ft in range(NFT):
                                    hh = SQ4 * b + tc * NFT + ft
                                    if hh >= 1:
                                        te.wait_ge(s_s4.sem, hh)
                                    last = None
                                    for kc in range(NKC):
                                        last = te.matmul(
                                            h1p[:, 0:TCQ],
                                            wu_sb[:, kc * FCB + ft * 128:kc * FCB + (ft + 1) * 128],
                                            xnTh[:, kc * QTOK + tc * TCQ:kc * QTOK + (tc + 1) * TCQ],
                                            start=(kc == 0), stop=(kc == NKC - 1))
                                    s_t4.inc(last)
                                te.wait_ge(s_gw.sem, dmarec[("wd", b)])
                                for ti in range(NTIQ):
                                    yy = VQ4 * b + tc * NTIQ + ti
                                    if yy >= 1:
                                        te.wait_ge(s_v4.sem, yy)
                                    te.wait_ge(s_s4.sem, SQ4 * b + NFT * (tc + 1))
                                    last = None
                                    for kc in range(NFT):
                                        for no, nw in nsplit(D):
                                            last = te.matmul(
                                                yp[:, no:no + nw],
                                                a1s[:, kc * TCQ + ti * 128:kc * TCQ + (ti + 1) * 128],
                                                wd_sb[:, kc * D + no:kc * D + no + nw],
                                                start=(kc == 0), stop=(kc == NFT - 1))
                                    s_t4.inc(last)

                        def ffn_scalar(sc, fb):
                            b = FB0 + fb
                            e, fc = divmod(fb, NFC)
                            if fc == 0:
                                sc.wait_ge(s_qx.sem, dmarec[("bu", g, e)])
                            for tc in range(NTCQ):
                                for ft in range(NFT):
                                    tv = TQ4 * b + tc * (NFT + 2 * NTIQ) + ft + 1
                                    sc.wait_ge(s_t4.sem, tv)
                                    hh = SQ4 * b + tc * NFT + ft
                                    s_s4.inc(sc.activation(
                                        a1s[:, ft * TCQ:(ft + 1) * TCQ],
                                        h1p[hh % 2][:, 0:TCQ], AF.Gelu,
                                        bias=bu_sb[:, e * 32 + fc * NFT + ft:
                                                   e * 32 + fc * NFT + ft + 1]))

                        def ffn_vector(ve, fb):
                            b = FB0 + fb
                            e, fc = divmod(fb, NFC)
                            for tc in range(NTCQ):
                                for ti in range(NTIQ):
                                    tt = tc * NTIQ + ti
                                    tv = TQ4 * b + tc * (NFT + NTIQ) + NFT + ti + 1
                                    ve.wait_ge(s_t4.sem, tv)
                                    gt = g * NTQ + tt
                                    scalar = (wgt[:, gt * NE + e:gt * NE + e + 1]
                                              if e < NE else 0.1)
                                    if fb == 0:
                                        ve.wait_ge(s_v4w.sem, V4W0 + tt + 1)
                                    else:
                                        ve.wait_ge(s_v4.sem, VQ4 * (b - 1) + tt + 1)
                                    s_v4.inc(ve.scalar_tensor_tensor(
                                        comb[:, tt * D:(tt + 1) * D],
                                        yp[:, :], scalar,
                                        comb[:, tt * D:(tt + 1) * D],
                                        op0=ALU.mult, op1=ALU.add))

                        @blk.tensor
                        def _(te):
                            te.wait_ge(s_v2p.sem, v2p_done)
                            te.wait_ge(s_s1c.sem, s1c_done)
                            te.wait_ge(s_gw.sem, dmarec["bd"])
                            te.wait_ge(s_qx.sem, dmarec["sbdrow"])
                            for tt in range(NTQ):
                                if tt >= 1:
                                    te.wait_ge(s_v4w.sem, V4W0 + 2 * (tt - 1) + 2)
                                last = None
                                for nn in range(2):
                                    last = te.matmul(yp[nn][:, :],
                                                     wT_q[0:9, tt * 128:(tt + 1) * 128],
                                                     bd_aug[0:9, nn * 512:(nn + 1) * 512],
                                                     start=True, stop=True)
                                s_t4w.inc(last)

                            def emit_pv(J, kg):
                                j = J - g * NJ
                                lam, qc = divmod(j, NQC)
                                te.wait_ge(s_s2m.sem, SQ2 * J + kg + 1)
                                if kg == 0 and J >= 2:
                                    te.wait_ge(s_v2m.sem, VQ2 * (J - 1))
                                last = None
                                for i in range(EXPB):
                                    kt = kg * EXPB + i
                                    last = te.matmul(
                                        pvp[J % 2][0:65, :],
                                        vrec[:, kt * 260 + lam * 65:kt * 260 + lam * 65 + 65],
                                        expT[kg % 2][:, i * 512:(i + 1) * 512],
                                        start=(kt == 0), stop=(kt == KT - 1))
                                trec[(J, "pv", kg)] = s_t2m.inc(last)

                            for tag in make_sched():
                                if tag[0] == "att":
                                    _, j, u = tag
                                    J = g * NJ + j
                                    lam, qc = divmod(j, NQC)
                                    pbase = 64 * (lam % 2)
                                    ktile = lam // 2
                                    if u < NKG:
                                        kg = u
                                        for i in range(EXPB):
                                            kt = kg * EXPB + i
                                            m = KT * J + kt
                                            if m >= 2:
                                                jj, kk = divmod(m - 2, KT)
                                                te.wait_ge(s_v2m.sem, VQ2 * jj + kk + 1)
                                            if kt == 0:
                                                te.wait_ge(s_qx.sem, dmarec[("q", J)])
                                            v = s_t2m.inc(te.matmul(
                                                scp[m % 2][:, :],
                                                krecT[pbase:pbase + 64,
                                                      ktile * S + kt * 128:ktile * S + (kt + 1) * 128],
                                                qstage[j % 2][pbase:pbase + 64, :],
                                                start=True, stop=True))
                                            trec[(J, "sc", kt)] = v
                                            assert v == T2M0 + TQ2 * j + sc_pos(kt), (v, j, kt)
                                        if kg >= 1:
                                            emit_pv(J, kg - 1)
                                    else:
                                        emit_pv(J, NKG - 1)
                                elif tag[0] == "h1":
                                    _, fb, tc, ft = tag
                                    b = FB0 + fb
                                    if tc == 0 and ft == 0:
                                        te.wait_ge(s_gw.sem, dmarec[("wu", b)])
                                        if fb == 0:
                                            te.wait_ge(s_qx.sem, dmarec[("xnth", g)])
                                            te.wait_ge(s_v4w.sem, 2 * NTQ * (g + 1))
                                    hh = SQ4 * b + tc * NFT + ft
                                    if hh >= 2:
                                        te.wait_ge(s_s4.sem, hh - 1)
                                    last = None
                                    for kc in range(NKC):
                                        last = te.matmul(
                                            h1p[hh % 2][:, 0:TCQ],
                                            wu_sb[:, kc * FCB + ft * 128:kc * FCB + (ft + 1) * 128],
                                            xnTh[:, kc * QTOK + tc * TCQ:kc * QTOK + (tc + 1) * TCQ],
                                            start=(kc == 0), stop=(kc == NKC - 1))
                                    s_t4.inc(last)
                                else:
                                    _, fb, tc, ti = tag
                                    b = FB0 + fb
                                    if ti == 0:
                                        te.wait_ge(s_gw.sem, dmarec[("wd", b)])
                                    te.wait_ge(s_s4.sem, SQ4 * b + NFT * (tc + 1))
                                    for nn in range(2):
                                        yh = VQ4 * b + 2 * (tc * NTIQ + ti) + nn
                                        if yh >= 2:
                                            te.wait_ge(s_v4.sem, yh - 1)
                                        last = None
                                        for kc in range(NFT):
                                            last = te.matmul(
                                                yp[yh % 2][:, :],
                                                a1s[:, kc * TCQ + ti * 128:kc * TCQ + (ti + 1) * 128],
                                                wd_sb[:, kc * D + nn * 512:kc * D + (nn + 1) * 512],
                                                start=(kc == 0), stop=(kc == NFT - 1))
                                        s_t4.inc(last)
                            assert s_t2m.n == TQ2 * NJ * (g + 1)
                            assert s_t4.n == TQ4 * NFB * (g + 1)

                        @blk.scalar
                        def _(sc):
                            for tag in make_sched():
                                if tag[0] == "att":
                                    _, j, u = tag
                                    J = g * NJ + j
                                    if u == 0:
                                        continue
                                    if u <= NKG:
                                        kg = u - 1
                                        pj = NKG * J + kg
                                        sc.wait_ge(s_v2m.sem, VQ2 * J + EXPB * (kg + 1))
                                        if pj >= 2:
                                            jj, kk = divmod(pj - 2, NKG)
                                            sc.wait_ge(s_t2m.sem, trec[(jj, "pv", kk)])
                                        s_s2m.inc(sc.activation(expT[kg % 2][:, :],
                                                                sTb[kg % 2][:, :], AF.Exp))
                                    if u == NKG:
                                        sc.wait_ge(s_t2m.sem, trec[(J, "pv", NKG - 1)])
                                        if J >= 1:
                                            sc.wait_ge(s_qx.sem, dmarec[("bc", J - 1)])
                                        s_s2m.inc(sc.activation(rln[64:65, :], pvp[J % 2][64:65, :],
                                                                AF.Ln))
                                        sc.wait_ge(s_s2m.sem, SQ2 * J + SQ2 - 1)
                                        s_s2m.inc(sc.activation(rr_sb[64:65, :], rln[64:65, :],
                                                                AF.Exp, scale=consts[64:65, 3:4]))
                                elif tag[0] == "h1":
                                    _, fb, tc, ft = tag
                                    b = FB0 + fb
                                    e, fc = divmod(fb, NFC)
                                    if fc == 0 and tc == 0 and ft == 0:
                                        sc.wait_ge(s_qx.sem, dmarec[("bu", g, e)])
                                    tv = TQ4 * b + tc * (NFT + 2 * NTIQ) + ft + 1
                                    sc.wait_ge(s_t4.sem, tv)
                                    hh = SQ4 * b + tc * NFT + ft
                                    s_s4.inc(sc.activation(
                                        a1s[:, ft * TCQ:(ft + 1) * TCQ],
                                        h1p[hh % 2][:, 0:TCQ], AF.Gelu,
                                        bias=bu_sb[:, e * 32 + fc * NFT + ft:
                                                   e * 32 + fc * NFT + ft + 1]))
                            assert s_s2m.n == SQ2 * NJ * (g + 1)
                            assert s_s4.n == SQ4 * NFB * (g + 1)

                        @blk.vector
                        def _(ve):
                            if g == 0:
                                ve.wait_ge(s_s2s.sem, 2)
                            ve.wait_ge(s_v2p.sem, v2p_done)
                            if g > 0:
                                ve.wait_ge(s_qx.sem, dmarec[("combspill", g - 1)])
                            for tt in range(NTQ):
                                ve.wait_ge(s_t4w.sem, T4W0 + tt + 1)
                                s_v4w.inc(ve.tensor_copy(comb[:, tt * D:tt * D + 512],
                                                         yp[0][:, :]))
                                s_v4w.inc(ve.tensor_copy(comb[:, tt * D + 512:(tt + 1) * D],
                                                         yp[1][:, :]))
                            for tag in make_sched():
                                if tag[0] == "att":
                                    _, j, u = tag
                                    J = g * NJ + j
                                    lam, qc = divmod(j, NQC)
                                    odd = (lam % 2 == 1)
                                    if u < NKG:
                                        kg = u
                                        for i in range(EXPB):
                                            kt = kg * EXPB + i
                                            pj = NKG * J + kg
                                            c0 = (S - 1) - 128 * kt + 512 * qc
                                            slot = sTb[kg % 2][:, (kt % EXPB) * 512:(kt % EXPB + 1) * 512]
                                            if pj >= 2:
                                                jj, kk = divmod(pj - 2, NKG)
                                                ve.wait_ge(s_s2m.sem, SQ2 * jj + kk + 1)
                                            ve.wait_ge(s_t2m.sem, trec[(J, "sc", kt)])
                                            s_v2m.inc(ve.tensor_mul(
                                                slot,
                                                scp[(KT * J + kt) % 2][:, :],
                                                decayM[:, c0:c0 + 512]))
                                    else:
                                        ve.wait_ge(s_t2m.sem, trec[(J, "pv", NKG - 1)])
                                        ve.wait_ge(s_qx.sem, dmarec[("bc", J)])
                                        if odd:
                                            oj = g * 2 * NQC + sum(1 for jj2 in range(j) if (jj2 // NQC) % 2 == 1)
                                            if ("odd", oj - 2) in oddrec:
                                                ve.wait_ge(s_qx.sem, oddrec[("odd", oj - 2)])
                                            dst = pvst[oj % 2][0:64, :]
                                        else:
                                            if qc == 0 and lam // 2 == 1:
                                                ve.wait_ge(s_qx.sem, dmarec[("pvspill", g, 0)])
                                            dst = pvT[0:64, qc * 512:(qc + 1) * 512]
                                        s_v2m.inc(ve.tensor_mul(dst, pvp[J % 2][0:64, :], bcsb[:, :]))
                                elif tag[0] == "y":
                                    _, fb, tc, ti = tag
                                    b = FB0 + fb
                                    e, fc = divmod(fb, NFC)
                                    tt = tc * NTIQ + ti
                                    gt = g * NTQ + tt
                                    scalar = (wgt[:, gt * NE + e:gt * NE + e + 1]
                                              if e < NE else 0.1)
                                    for nn in range(2):
                                        yh = VQ4 * b + 2 * tt + nn
                                        tv = TQ4 * b + tc * (NFT + 2 * NTIQ) + NFT + 2 * ti + nn + 1
                                        ve.wait_ge(s_t4.sem, tv)
                                        if fb == 0:
                                            ve.wait_ge(s_v4w.sem, V4W0 + 2 * tt + nn + 1)
                                        else:
                                            ve.wait_ge(s_v4.sem, VQ4 * (b - 1) + 2 * tt + nn + 1)
                                        cs = comb[:, tt * D + nn * 512:tt * D + (nn + 1) * 512]
                                        s_v4.inc(ve.scalar_tensor_tensor(
                                            cs, yp[yh % 2][:, :], scalar, cs,
                                            op0=ALU.mult, op1=ALU.add))
                            assert s_v2m.n == VQ2 * NJ * (g + 1)
                            assert s_v4.n == VQ4 * NFB * (g + 1)
                            assert s_v4w.n == 2 * NTQ * (g + 1)

            with ExitStack() as stp3:
                sb3 = lambda n, sh, dt: stp3.enter_context(nc.sbuf_tensor(n, sh, dt, side="right"))
                pvF = sb3("pvF", [128, NKC * S], BF16)
                wo_b = sb3("wo_b", [128, NKC * D], BF16)
                ast = sb3("ast", [128, 2 * D], F32)
                cst = sb3("cst", [128, 2 * D], F32)
                xst = sb3("xst3", [128, 2 * D], F32)
                with ExitStack() as stps, nc.Block() as blk:
                    wop = [stps.enter_context(nc.psum_tensor(f"wop{i}", [128, 512], F32)) for i in range(2)]

                    @blk.gpsimd
                    def _(gp):
                        dmarec["wo"] = dma_ser(gp, s_gw,
                            wo_b.ap().rearrange("p (kc d) -> p kc d", kc=NKC),
                            wo_h.rearrange("(kc p) d -> p kc d", p=128))

                    @blk.sync
                    def _(sy):
                        sy.wait_ge(s_qx.sem, dmarec[("pvspill", 3, 1)])
                        dmarec["pvload"] = dma_ser(sy, s_qx, pvF[:, :], pv_sp)
                        for tt in range(NT):
                            if tt >= 2:
                                sy.wait_ge(s_v3.sem, 4 * (tt - 2) + 4)
                            dmarec[("c3", tt)] = dma_ser(sy, s_qx,
                                cst[:, (tt % 2) * D:(tt % 2 + 1) * D],
                                comb_sp[tt * 128:(tt + 1) * 128, :])
                            dmarec[("x3", tt)] = dma_ser(sy, s_qx,
                                xst[:, (tt % 2) * D:(tt % 2 + 1) * D],
                                x_h[tt * 128:(tt + 1) * 128, :])
                            sy.wait_ge(s_v3.sem, 4 * tt + 4)
                            dmarec[("out", tt)] = dma_ser(sy, s_qx,
                                out_h[tt * 128:(tt + 1) * 128, :],
                                ast[:, (tt % 2) * D:(tt % 2 + 1) * D])
                        sy.wait_ge(s_qx.sem, dmarec[("out", NT - 1)])

                    @blk.tensor
                    def _(te):
                        te.wait_ge(s_gw.sem, dmarec["wo"])
                        te.wait_ge(s_qx.sem, dmarec["pvload"])
                        for tt in range(NT):
                            for nn in range(2):
                                gi = 2 * tt + nn
                                if gi >= 2:
                                    te.wait_ge(s_v3.sem, 4 * (tt - 1) + nn + 1)
                                last = None
                                for kc in range(NKC):
                                    last = te.matmul(
                                        wop[gi % 2][:, :],
                                        pvF[:, kc * S + tt * 128:kc * S + (tt + 1) * 128],
                                        wo_b[:, kc * D + nn * 512:kc * D + (nn + 1) * 512],
                                        start=(kc == 0), stop=(kc == NKC - 1))
                                s_t3.inc(last)
                        assert s_t3.n == 2 * NT

                    @blk.vector
                    def _(ve):
                        for tt in range(NT):
                            for nn in range(2):
                                gi = 2 * tt + nn
                                ve.wait_ge(s_t3.sem, gi + 1)
                                if tt >= 2 and nn == 0:
                                    ve.wait_ge(s_qx.sem, dmarec[("out", tt - 2)])
                                s_v3.inc(ve.tensor_copy(
                                    ast[:, (tt % 2) * D + nn * 512:(tt % 2) * D + (nn + 1) * 512],
                                    wop[gi % 2][:, :]))
                            ve.wait_ge(s_qx.sem, dmarec[("x3", tt)])
                            ve.wait_ge(s_v3.sem, 4 * tt + 2)
                            s_v3.inc(ve.tensor_add(ast[:, (tt % 2) * D:(tt % 2 + 1) * D],
                                                   ast[:, (tt % 2) * D:(tt % 2 + 1) * D],
                                                   xst[:, (tt % 2) * D:(tt % 2 + 1) * D]))
                            ve.wait_ge(s_v3.sem, 4 * tt + 3)
                            s_v3.inc(ve.tensor_add(ast[:, (tt % 2) * D:(tt % 2 + 1) * D],
                                                   ast[:, (tt % 2) * D:(tt % 2 + 1) * D],
                                                   cst[:, (tt % 2) * D:(tt % 2 + 1) * D]))
                        assert s_v3.n == 4 * NT
    return nc


def _prep_inputs(L0_attn_x, L0_attn_Wq, L0_attn_Wk, L0_attn_Wv, L0_attn_Wo,
                 L0_attn_W_down_kv, L0_attn_W_up_k, L0_attn_W_up_v,
                 L0_ffn_gate_w, L0_ffn_gate_bias, L0_ffn_Wu, L0_ffn_bu,
                 L0_ffn_Wd, L0_ffn_bd, L0_ffn_sWu, L0_ffn_sbu, L0_ffn_sWd,
                 L0_ffn_sbd, S):
    c = np.ascontiguousarray
    f = lambda a: np.asarray(a, dtype=np.float32)
    shared = {
        "wq": c(f(L0_attn_Wq)),
        "wk": c(f(L0_attn_Wk)),
        "wv": c(f(L0_attn_Wv)),
        "wo": c(f(L0_attn_Wo)),
        "wdn": c(f(L0_attn_W_down_kv)),
        "wupk": c(f(L0_attn_W_up_k)),
        "wupv": c(f(L0_attn_W_up_v)),
        "gwt": c(f(L0_ffn_gate_w).T),
        "gbias": c(f(L0_ffn_gate_bias).reshape(1, NE)),
        "wut": c(f(L0_ffn_Wu).transpose(0, 2, 1)),
        "bu2": c(f(L0_ffn_bu).reshape(NE, FF // 128, 128).transpose(0, 2, 1)),
        "wdt": c(f(L0_ffn_Wd).transpose(0, 2, 1)),
        "bd": c(f(L0_ffn_bd)),
        "swut": c(f(L0_ffn_sWu).T),
        "sbu2": c(f(L0_ffn_sbu).reshape(FF // 128, 128).T),
        "swdt": c(f(L0_ffn_sWd).T),
        "sbd": c(f(L0_ffn_sbd).reshape(1, D)),
    }
    x = f(L0_attn_x)
    in_maps = []
    for b in range(x.shape[0]):
        m = dict(shared)
        m["x"] = c(x[b, :S, :])
        in_maps.append(m)
    return in_maps


def kernel(**inputs):
    S = 4096
    in_maps = _prep_inputs(S=S, **inputs)
    nc = build(S=S)
    res = run_bass_kernel_spmd(nc, in_maps, core_ids=list(range(len(in_maps))))
    return np.stack([r["out"] for r in res.results], axis=0)


# version 11
# speedup vs baseline: 1.3399x; 1.0192x over previous
import numpy as np
from contextlib import ExitStack

import concourse.bass as bass
import concourse.mybir as mybir
from concourse.bass_utils import run_bass_kernel_spmd

F32 = mybir.dt.float32
BF16 = mybir.dt.bfloat16
AF = mybir.ActivationFunctionType
ALU = mybir.AluOpType

D = 1024
NKC = D // 128
NH = 16
HD = 64
DL = 256
NE = 8
FF = 4096
FCB = 1024
NFC = FF // FCB
NFT = FCB // 128
QB = 512
EXPB = 8

LN_EPS = 1e-5
GATE_EPS = 1e-6
DECAY_SCALE = -(0.01 * 0.01)
ATT_SCALE_LNBIAS = float(np.log(1.0 / (np.sqrt(np.float64(1024.0)) + 1e-8)))


class Chan:

    def __init__(self, stack, nc, name):
        self.sem = stack.enter_context(nc.semaphore(name))
        self.n = 0

    def inc(self, inst, k=1):
        inst.then_inc(self.sem, k)
        self.n += k
        return self.n


def nsplit(total, maxn=512):
    return [(o, min(maxn, total - o)) for o in range(0, total, maxn)]


def dma_ser(eng, chan, out, in_):
    v = chan.inc(eng.dma_start(out, in_), 16)
    eng.wait_ge(chan.sem, v)
    return v


def build(S=4096, debug=False, force_tcq=None):
    NT = S // 128
    NQC = S // QB
    KT = NT
    EB = min(EXPB, KT)
    NKG = KT // EB
    TH = S // 2
    NTTH = TH // 128
    TC = 1024 if TH % 1024 == 0 else TH
    NTC = TH // TC
    NTI = TC // 128

    nc = bass.Bass("TRN2", target_bir_lowering=False, debug=False)

    x_h = nc.dram_tensor("x", [S, D], F32, kind="ExternalInput").ap()
    wq_h = nc.dram_tensor("wq", [D, D], F32, kind="ExternalInput").ap()
    wk_h = nc.dram_tensor("wk", [D, D], F32, kind="ExternalInput").ap()
    wv_h = nc.dram_tensor("wv", [D, D], F32, kind="ExternalInput").ap()
    wo_h = nc.dram_tensor("wo", [D, D], F32, kind="ExternalInput").ap()
    wdn_h = nc.dram_tensor("wdn", [D, DL], F32, kind="ExternalInput").ap()
    wupk_h = nc.dram_tensor("wupk", [DL, D], F32, kind="ExternalInput").ap()
    wupv_h = nc.dram_tensor("wupv", [DL, D], F32, kind="ExternalInput").ap()
    gwt_h = nc.dram_tensor("gwt", [D, NE], F32, kind="ExternalInput").ap()
    gbias_h = nc.dram_tensor("gbias", [1, NE], F32, kind="ExternalInput").ap()
    wut_h = nc.dram_tensor("wut", [NE, D, FF], F32, kind="ExternalInput").ap()
    bu2_h = nc.dram_tensor("bu2", [NE, 128, FF // 128], F32, kind="ExternalInput").ap()
    wdt_h = nc.dram_tensor("wdt", [NE, FF, D], F32, kind="ExternalInput").ap()
    bd_h = nc.dram_tensor("bd", [NE, D], F32, kind="ExternalInput").ap()
    swut_h = nc.dram_tensor("swut", [D, FF], F32, kind="ExternalInput").ap()
    sbu2_h = nc.dram_tensor("sbu2", [128, FF // 128], F32, kind="ExternalInput").ap()
    swdt_h = nc.dram_tensor("swdt", [FF, D], F32, kind="ExternalInput").ap()
    sbd_h = nc.dram_tensor("sbd", [1, D], F32, kind="ExternalInput").ap()
    out_h = nc.dram_tensor("out", [S, D], F32, kind="ExternalOutput").ap()

    scratch_kind = "ExternalOutput" if debug else "Internal"
    xnt_sp = nc.dram_tensor("xnt_sp", [128, NKC * S], BF16, kind=scratch_kind).ap()
    pv_sp = nc.dram_tensor("pv_sp", [128, NKC * S], BF16, kind=scratch_kind).ap()
    q_sp = nc.dram_tensor("q_sp", [128, NKC * S], BF16, kind=scratch_kind).ap()
    rr_dram = nc.dram_tensor("rr_dram", [1, 512], BF16, kind="Internal").ap()
    comb_sp = nc.dram_tensor("comb_sp", [S, D], F32, kind=scratch_kind).ap()
    if debug:
        dbg_lat = nc.dram_tensor("dbg_lat", [128, 2 * S], BF16, kind="ExternalOutput").ap()
        dbg_q = nc.dram_tensor("dbg_q", [128, NKC * S], BF16, kind="ExternalOutput").ap()
        dbg_gsig = nc.dram_tensor("dbg_gsig", [128, NE * NT], F32, kind="ExternalOutput").ap()
        dbg_w = nc.dram_tensor("dbg_w", [128, NE * NT], F32, kind="ExternalOutput").ap()

    st = ExitStack()
    with st:
        def sbuf(name, shape, dtype, **kw):
            return st.enter_context(nc.sbuf_tensor(name, shape, dtype, **kw))

        ident = sbuf("ident", [128, 128], BF16)
        onesf = sbuf("onesf", [128, 128], BF16)
        consts = sbuf("consts", [128, 4], F32)
        mrv = sbuf("mrv", [128, 8], F32)
        gsig = sbuf("gsig", [128, NE * NT], F32)
        wgt = sbuf("wgt", [128, NE * NT], F32)
        wupk_b = sbuf("wupk_b", [128, 2 * D], BF16)
        wupv_b = sbuf("wupv_b", [128, 2 * D], BF16)
        gwt_b = sbuf("gwt_b", [128, NKC * NE], BF16)
        gbias_b = sbuf("gbias_b", [1, NE], BF16)
        bd_aug = sbuf("bd_aug", [32, D], BF16)
        wT_q = sbuf("wT_q", [32, (S // 512) * 128], BF16)
        wbf = sbuf("wbf", [128, 8], BF16)
        sbd_st = sbuf("sbd_st", [1, D], F32)
        sbd_bf = sbuf("sbd_bf", [1, D], BF16)

        s_gw = Chan(st, nc, "s_gw")
        s_qx = Chan(st, nc, "s_qx")
        s_gpc = Chan(st, nc, "s_gpc")
        s_v0 = Chan(st, nc, "s_v0")
        s_t1 = Chan(st, nc, "s_t1")
        s_s1 = Chan(st, nc, "s_s1")
        s_v1 = Chan(st, nc, "s_v1")
        s_t1b = Chan(st, nc, "s_t1b")
        s_v1b = Chan(st, nc, "s_v1b")
        s_t1c = Chan(st, nc, "s_t1c")
        s_s1c = Chan(st, nc, "s_s1c")
        s_v2s = Chan(st, nc, "s_v2s")
        s_s2s = Chan(st, nc, "s_s2s")
        s_t2p = Chan(st, nc, "s_t2p")
        s_v2p = Chan(st, nc, "s_v2p")
        s_t2m = Chan(st, nc, "s_t2m")
        s_s2m = Chan(st, nc, "s_s2m")
        s_v2m = Chan(st, nc, "s_v2m")
        s_t3 = Chan(st, nc, "s_t3")
        s_v3 = Chan(st, nc, "s_v3")
        s_t4w = Chan(st, nc, "s_t4w")
        s_v4w = Chan(st, nc, "s_v4w")
        s_t4 = Chan(st, nc, "s_t4")
        s_s4 = Chan(st, nc, "s_s4")
        s_v4 = Chan(st, nc, "s_v4")
        s_v5 = Chan(st, nc, "s_v5")

        dmarec = {}

        with nc.Block() as blk:
            @blk.gpsimd
            def _(gp):
                s_gpc.inc(gp.iota(ident[:, :], pattern=[[-1, 128]], base=0,
                                  channel_multiplier=1,
                                  allow_small_or_imprecise_dtypes=True))
                dmarec["gwt"] = dma_ser(gp, s_gw, gwt_b.ap().rearrange("p (kc e) -> p kc e", kc=NKC),
                    gwt_h.rearrange("(kc p) e -> p kc e", p=128))
                dmarec["gbias"] = dma_ser(gp, s_gw, gbias_b[:, :], gbias_h)
                dmarec["wupk"] = dma_ser(gp, s_gw, wupk_b.ap().rearrange("p (lc d) -> p lc d", lc=2),
                    wupk_h.rearrange("(lc p) d -> p lc d", p=128))
                dmarec["wupv"] = dma_ser(gp, s_gw, wupv_b.ap().rearrange("p (lc d) -> p lc d", lc=2),
                    wupv_h.rearrange("(lc p) d -> p lc d", p=128))
                dmarec["bd"] = dma_ser(gp, s_gw, bd_aug[0:8, :], bd_h)

            @blk.sync
            def _(sy):
                dmarec["sbd"] = dma_ser(sy, s_qx, sbd_st[:, :], sbd_h)
                sy.wait_ge(s_v0.sem, 5)
                dmarec["sbdrow"] = dma_ser(sy, s_qx, bd_aug[8:9, :], sbd_bf[:, :])

            @blk.vector
            def _(ve):
                ve.memset(consts[:, 0:1], LN_EPS)
                ve.memset(consts[:, 1:2], DECAY_SCALE)
                ve.memset(consts[:, 2:3], ATT_SCALE_LNBIAS)
                ve.memset(consts[:, 3:4], -1.0)
                ve.memset(onesf[:, :], 1.0)
                ve.memset(mrv[:, 2:8], -7.25e30)
                s_v0.inc(ve.memset(wT_q[0:32, :], 1.0))
                ve.wait_ge(s_gpc.sem, 1)
                s_v0.inc(ve.tensor_single_scalar(ident[:, :], ident[:, :], 0.0,
                                                 op=ALU.is_equal))
                ve.wait_ge(s_qx.sem, dmarec["sbd"])
                s_v0.inc(ve.tensor_single_scalar(sbd_st[:, :], sbd_st[:, :], 1.0,
                                                 op=ALU.mult))
                ve.wait_ge(s_v0.sem, 3)
                s_v0.inc(ve.tensor_single_scalar(sbd_st[:, :], sbd_st[:, :], 0.1,
                                                 op=ALU.mult))
                ve.wait_ge(s_v0.sem, 4)
                s_v0.inc(ve.tensor_copy(sbd_bf[:, :], sbd_st[:, :]))

        with ExitStack() as stp1:
            latT = stp1.enter_context(nc.sbuf_tensor("latT", [128, 2 * S], BF16, side="left"))
            with ExitStack() as stp1q:
                xnT = stp1q.enter_context(nc.sbuf_tensor("xnT", [128, NKC * S], BF16, side="right"))
                wq_b = stp1q.enter_context(nc.sbuf_tensor("wq_b", [128, NKC * D], BF16, side="right"))
                with ExitStack() as stp1b:
                    wk_b = stp1b.enter_context(nc.sbuf_tensor("wk_b", [128, NKC * D], BF16, side="right"))
                    wv_b = stp1b.enter_context(nc.sbuf_tensor("wv_b", [128, NKC * D], BF16, side="right"))
                    wdn_b = stp1b.enter_context(nc.sbuf_tensor("wdn_b", [128, NKC * DL], BF16, side="right"))
                    kv_b = stp1b.enter_context(nc.sbuf_tensor("kv_b", [128, NKC * 512], BF16, side="right"))
                    with ExitStack() as stp1a:
                        sb1 = lambda n, sh, dt: stp1a.enter_context(nc.sbuf_tensor(n, sh, dt, side="right"))
                        xst = sb1("xst", [128, 2 * D], F32)
                        stats = sb1("stats", [128, 2 * 12], F32)
                        mv = sb1("mv", [128, 2 * 2], F32)
                        rstd = sb1("rstd", [128, 2], F32)
                        nmr = sb1("nmr", [128, 2], F32)
                        xnbf = sb1("xnbf", [128, 2 * D], BF16)
                        m8 = sb1("m8", [128, 8], F32)
                        killed = sb1("killed", [128, 8], F32)
                        gmask = sb1("gmask", [128, 8], F32)
                        gden = sb1("gden", [128, 1], F32)

                        VQ1, SQ1, TQ1 = 13, 11, 9
                        with ExitStack() as stps, nc.Block() as blk:
                            trp = [stps.enter_context(nc.psum_tensor(f"trp{i}", [128, 128], BF16)) for i in range(2)]
                            gps = [stps.enter_context(nc.psum_tensor(f"gps{i}", [128, NE], F32)) for i in range(2)]

                            @blk.gpsimd
                            def _(gp):
                                for nm, dst, src in (("wk", wk_b, wk_h), ("wv", wv_b, wv_h),
                                                     ("wq", wq_b, wq_h)):
                                    dmarec[nm] = dma_ser(gp, s_gw, dst.ap().rearrange("p (kc d) -> p kc d", kc=NKC),
                                        src.rearrange("(kc p) d -> p kc d", p=128))
                                dmarec["wdn"] = dma_ser(gp, s_gw, wdn_b.ap().rearrange("p (kc l) -> p kc l", kc=NKC),
                                    wdn_h.rearrange("(kc p) l -> p kc l", p=128))

                            @blk.sync
                            def _(sy):
                                for tt in range(NT):
                                    if tt >= 2:
                                        sy.wait_ge(s_v1.sem, VQ1 * (tt - 2) + 1)
                                        sy.wait_ge(s_s1.sem, SQ1 * (tt - 2) + 2)
                                    dmarec[("xt", tt)] = dma_ser(sy, s_qx, xst[:, (tt % 2) * D:(tt % 2 + 1) * D],
                                        x_h[tt * 128:(tt + 1) * 128, :])

                            @blk.tensor
                            def _(te):
                                te.wait_ge(s_v0.sem, 2)
                                te.wait_ge(s_gw.sem, dmarec["gbias"])
                                for tt in range(NT):
                                    te.wait_ge(s_s1.sem, SQ1 * tt + 2)
                                    for i in range(8):
                                        gi = 8 * tt + i
                                        if gi >= 2:
                                            pj, pi = divmod(gi - 2, 8)
                                            te.wait_ge(s_s1.sem, SQ1 * pj + 3 + pi)
                                        s_t1.inc(te.transpose(
                                            trp[gi % 2][:, :],
                                            xnbf[:, (tt % 2) * D + i * 128:(tt % 2) * D + (i + 1) * 128],
                                            ident[:, :]))
                                    te.wait_ge(s_s1.sem, SQ1 * tt + 10)
                                    if tt >= 2:
                                        te.wait_ge(s_s1.sem, SQ1 * (tt - 2) + 11)
                                    for kc in range(NKC):
                                        te.matmul(gps[tt % 2][:, :],
                                                  xnT[:, kc * S + tt * 128:kc * S + (tt + 1) * 128],
                                                  gwt_b[:, kc * NE:(kc + 1) * NE],
                                                  start=(kc == 0), stop=False)
                                    s_t1.inc(te.matmul(gps[tt % 2][:, :], onesf[0:1, 0:128],
                                                       gbias_b[0:1, :], start=False, stop=True))
                                assert s_t1.n == TQ1 * NT

                            @blk.scalar
                            def _(sc):
                                for tt in range(NT):
                                    sc.wait_ge(s_v1.sem, VQ1 * tt + 2)
                                    s_s1.inc(sc.activation(rstd[:, tt % 2:tt % 2 + 1],
                                                           mv[:, (tt % 2) * 2 + 1:(tt % 2) * 2 + 2],
                                                           AF.Sqrt, bias=consts[:, 0:1]))
                                    sc.wait_ge(s_v1.sem, VQ1 * tt + 5)
                                    sc.wait_ge(s_qx.sem, dmarec[("xt", tt)])
                                    if tt >= 2:
                                        sc.wait_ge(s_t1.sem, TQ1 * (tt - 2) + 8)
                                    s_s1.inc(sc.activation(
                                        xnbf[:, (tt % 2) * D:(tt % 2 + 1) * D],
                                        xst[:, (tt % 2) * D:(tt % 2 + 1) * D],
                                        AF.Identity,
                                        bias=nmr[:, tt % 2:tt % 2 + 1],
                                        scale=rstd[:, tt % 2:tt % 2 + 1]))
                                    for i in range(8):
                                        sc.wait_ge(s_t1.sem, TQ1 * tt + i + 1)
                                        s_s1.inc(sc.activation(
                                            xnT[:, i * S + tt * 128:i * S + (tt + 1) * 128],
                                            trp[(8 * tt + i) % 2][:, :], AF.Identity))
                                    sc.wait_ge(s_t1.sem, TQ1 * tt + 9)
                                    s_s1.inc(sc.activation(gsig[:, tt * NE:(tt + 1) * NE],
                                                           gps[tt % 2][:, :], AF.Sigmoid))
                                assert s_s1.n == SQ1 * NT

                            @blk.vector
                            def _(ve):
                                ve.wait_ge(s_v0.sem, 5)
                                for tt in range(NT):
                                    b = tt % 2
                                    ve.wait_ge(s_qx.sem, dmarec[("xt", tt)])
                                    if tt >= 2:
                                        ve.wait_ge(s_v1.sem, VQ1 * (tt - 2) + 2)
                                    ve.bn_stats(stats[:, b * 12:b * 12 + 6],
                                                xst[:, b * D:b * D + 512])
                                    s_v1.inc(ve.bn_stats(stats[:, b * 12 + 6:b * 12 + 12],
                                                         xst[:, b * D + 512:b * D + 1024]))
                                    ve.wait_ge(s_v1.sem, VQ1 * tt + 1)
                                    if tt >= 2:
                                        ve.wait_ge(s_s1.sem, SQ1 * (tt - 2) + 1)
                                    s_v1.inc(ve.bn_aggr(mv[:, b * 2:b * 2 + 2],
                                                        stats[:, b * 12:b * 12 + 12]))
                                    ve.wait_ge(s_s1.sem, SQ1 * tt + 1)
                                    s_v1.inc(ve.reciprocal(rstd[:, b:b + 1], rstd[:, b:b + 1]))
                                    ve.wait_ge(s_v1.sem, VQ1 * tt + 3)
                                    s_v1.inc(ve.tensor_mul(nmr[:, b:b + 1], mv[:, b * 2:b * 2 + 1],
                                                           rstd[:, b:b + 1]))
                                    ve.wait_ge(s_v1.sem, VQ1 * tt + 4)
                                    s_v1.inc(ve.tensor_single_scalar(nmr[:, b:b + 1], nmr[:, b:b + 1],
                                                                     -1.0, op=ALU.mult))
                                    gs = gsig[:, tt * NE:(tt + 1) * NE]
                                    ve.wait_ge(s_s1.sem, SQ1 * tt + 11)
                                    s_v1.inc(ve.max(m8[:, :], gs))
                                    ve.wait_ge(s_v1.sem, VQ1 * tt + 6)
                                    if tt >= 1:
                                        ve.wait_ge(s_v1.sem, VQ1 * (tt - 1) + 8)
                                    s_v1.inc(ve.tensor_copy(mrv[:, 0:2], m8[:, 0:2]))
                                    ve.wait_ge(s_v1.sem, VQ1 * tt + 7)
                                    s_v1.inc(ve.match_replace(killed[:, :], mrv[:, :], gs, -1e30))
                                    ve.wait_ge(s_v1.sem, VQ1 * tt + 8)
                                    s_v1.inc(ve.tensor_tensor(gmask[:, :], gs, killed[:, :],
                                                              op=ALU.is_gt))
                                    s_v1.inc(ve.tensor_add(gden[:, :], m8[:, 0:1], m8[:, 1:2]))
                                    ve.wait_ge(s_v1.sem, VQ1 * tt + 10)
                                    s_v1.inc(ve.tensor_single_scalar(gden[:, :], gden[:, :],
                                                                     GATE_EPS, op=ALU.add))
                                    ve.wait_ge(s_v1.sem, VQ1 * tt + 11)
                                    s_v1.inc(ve.reciprocal(gden[:, :], gden[:, :]))
                                    ve.wait_ge(s_v1.sem, VQ1 * tt + 12)
                                    s_v1.inc(ve.scalar_tensor_tensor(
                                        wgt[:, tt * NE:(tt + 1) * NE], gs, gden[:, :],
                                        gmask[:, :], op0=ALU.mult, op1=ALU.mult))
                                assert s_v1.n == VQ1 * NT

                    with ExitStack() as stps, nc.Block() as blk:
                        kvp = [stps.enter_context(nc.psum_tensor(f"kvp{i}", [128, 512], F32)) for i in range(2)]
                        latp = [stps.enter_context(nc.psum_tensor(f"latp{i}", [128, 512], F32)) for i in range(2)]

                        @blk.sync
                        def _(sy):
                            dmarec["xnt_spill"] = dma_ser(sy, s_qx, xnt_sp, xnT[:, :])

                        @blk.tensor
                        def _(te):
                            te.wait_ge(s_gw.sem, dmarec["wdn"])
                            te.wait_ge(s_s1.sem, SQ1 * NT)
                            for nn in range(NQC):
                                for mt in range(8):
                                    gi = 10 * nn + mt
                                    if gi >= 2:
                                        te.wait_ge(s_v1b.sem, gi - 1)
                                    first = True
                                    last = None
                                    for wsrc in (wk_b, wv_b):
                                        for kc in range(NKC):
                                            last = te.matmul(
                                                kvp[mt % 2][:, :],
                                                wsrc[:, kc * D + mt * 128:kc * D + (mt + 1) * 128],
                                                xnT[:, kc * S + nn * 512:kc * S + (nn + 1) * 512],
                                                start=first,
                                                stop=(wsrc is wv_b and kc == NKC - 1))
                                            first = False
                                    s_t1b.inc(last)
                                te.wait_ge(s_v1b.sem, 10 * nn + 8)
                                for lt in range(2):
                                    gi = 10 * nn + 8 + lt
                                    if gi >= 2:
                                        te.wait_ge(s_v1b.sem, gi - 1)
                                    last = None
                                    for kc in range(NKC):
                                        last = te.matmul(
                                            latp[lt][:, :],
                                            wdn_b[:, kc * DL + lt * 128:kc * DL + (lt + 1) * 128],
                                            kv_b[:, kc * 512:(kc + 1) * 512],
                                            start=(kc == 0), stop=(kc == NKC - 1))
                                    s_t1b.inc(last)
                            assert s_t1b.n == 10 * NQC

                        @blk.vector
                        def _(ve):
                            for nn in range(NQC):
                                for mt in range(8):
                                    ve.wait_ge(s_t1b.sem, 10 * nn + mt + 1)
                                    if mt == 0 and nn >= 1:
                                        ve.wait_ge(s_t1b.sem, 10 * (nn - 1) + 10)
                                    s_v1b.inc(ve.tensor_copy(kv_b[:, mt * 512:(mt + 1) * 512],
                                                             kvp[mt % 2][:, :]))
                                for lt in range(2):
                                    ve.wait_ge(s_t1b.sem, 10 * nn + 9 + lt)
                                    s_v1b.inc(ve.tensor_copy(
                                        latT[:, lt * S + nn * 512:lt * S + (nn + 1) * 512],
                                        latp[lt][:, :]))
                            assert s_v1b.n == 10 * NQC

                qT = stp1q.enter_context(nc.sbuf_tensor("qT", [128, NKC * S], BF16, side="left"))
                with ExitStack() as stps, nc.Block() as blk:
                    qp = [stps.enter_context(nc.psum_tensor(f"qp{i}", [128, 512], F32)) for i in range(2)]

                    @blk.tensor
                    def _(te):
                        te.wait_ge(s_gw.sem, dmarec["wq"])
                        for mt in range(8):
                            for nn in range(NQC):
                                gi = mt * NQC + nn
                                if gi >= 2:
                                    te.wait_ge(s_s1c.sem, gi - 1)
                                last = None
                                for kc in range(NKC):
                                    last = te.matmul(
                                        qp[gi % 2][:, :],
                                        wq_b[:, kc * D + mt * 128:kc * D + (mt + 1) * 128],
                                        xnT[:, kc * S + nn * 512:kc * S + (nn + 1) * 512],
                                        start=(kc == 0), stop=(kc == NKC - 1))
                                s_t1c.inc(last)
                        assert s_t1c.n == 8 * NQC

                    @blk.scalar
                    def _(sc):
                        for mt in range(8):
                            for nn in range(NQC):
                                gi = mt * NQC + nn
                                sc.wait_ge(s_t1c.sem, gi + 1)
                                s_s1c.inc(sc.activation(
                                    qT[:, mt * S + nn * 512:mt * S + (nn + 1) * 512],
                                    qp[gi % 2][:, :], AF.Identity))
                        assert s_s1c.n == 8 * NQC

                    @blk.sync
                    def _(sy):
                        sy.wait_ge(s_qx.sem, dmarec["xnt_spill"])
                        sy.wait_ge(s_s1c.sem, 8 * NQC)
                        dmarec["q_spill"] = dma_ser(sy, s_qx, q_sp, qT[:, :])

                if debug:
                    with nc.Block() as blk:
                        @blk.sync
                        def _(sy):
                            dma_ser(sy, s_qx, dbg_lat, latT[:, :])
                            dma_ser(sy, s_qx, dbg_q, qT[:, :])
                            dma_ser(sy, s_qx, dbg_gsig, gsig[:, :])
                            dma_ser(sy, s_qx, dbg_w, wgt[:, :])
                            sy.wait_ge(s_qx.sem, s_qx.n)
            with ExitStack() as stp2:
                sb2 = lambda n, sh, dt: stp2.enter_context(nc.sbuf_tensor(n, sh, dt, side="right"))
                decayM = sb2("decayM", [128, 2 * S], BF16)
                with ExitStack() as stp2s:
                    dmraw = stp2s.enter_context(nc.sbuf_tensor("dmraw", [128, 2 * S], F32, side="right"))
                    with nc.Block() as blk:
                        @blk.gpsimd
                        def _(gp):
                            s_gpc.inc(gp.iota(dmraw[:, :], pattern=[[-1, 2 * S]], base=S - 1,
                                              channel_multiplier=1,
                                              allow_small_or_imprecise_dtypes=True))

                        @blk.scalar
                        def _(sc):
                            sc.wait_ge(s_gpc.sem, s_gpc.n)
                            s_s2s.inc(sc.activation(dmraw[:, :], dmraw[:, :], AF.Abs))
                            sc.wait_ge(s_s2s.sem, 1)
                            s_s2s.inc(sc.activation(decayM[:, :], dmraw[:, :], AF.Exp,
                                                    bias=consts[:, 2:3], scale=consts[:, 1:2]))

                krecT = sb2("krecT", [128, 2 * S], BF16)
                vrec = sb2("vrec", [128, NT * 260], BF16)
                sTb = [sb2(f"sTb{i}", [128, EB * 512], BF16) for i in range(2)]
                expT = sTb
                pvT = sb2("pvT", [128, S], BF16)
                pvst = [sb2(f"pvst{i}", [64, 512], BF16) for i in range(2)]
                bcsb = sb2("bcsb", [64, 512], BF16)
                rr_sb = sb2("rr_sb", [128, 512], BF16)
                rln = sb2("rln", [128, 512], F32)
                qstage = [sb2(f"qstage{i}", [128, 512], BF16) for i in range(2)]
                comb = sb2("comb", [128, (S // 512) * D], F32)
                xnTh = sb2("xnTh", [128, NKC * (S // 4)], BF16)
                wu_sb = sb2("wu_sb", [128, NKC * FCB], BF16)
                wd_sb = sb2("wd_sb", [128, NFT * D], BF16)
                a1s = sb2("a1s", [128, NFT * (512 if (S // 4) % 512 == 0 else S // 4)], BF16)
                bu_sb = sb2("bu_sb", [128, 9 * (FF // 128)], F32)

                NJ = 4 * NQC
                TQ2 = KT + NKG
                SQ2 = NKG + 2
                VQ2 = KT + 1
                NFB = 9 * NFC
                QTOK = S // 4
                NTQ = QTOK // 128
                TCQ = force_tcq or (512 if QTOK % 512 == 0 else QTOK)
                NTCQ = QTOK // TCQ
                NTIQ = TCQ // 128
                TQ4 = NTCQ * (NFT + 2 * NTIQ)
                SQ4 = NTCQ * NFT
                VQ4 = 2 * NTQ
                trec = {}
                oddrec = {}

                def sc_pos(kt):
                    return kt + 1 + max(0, kt // EB - 1)

                def make_sched():
                    att = [("att", j, u) for j in range(NJ) for u in range(NKG + 1)]
                    ffn = []
                    for fb in range(NFB):
                        for tc in range(NTCQ):
                            ffn += [("h1", fb, tc, ft) for ft in range(NFT)]
                            ffn += [("y", fb, tc, ti) for ti in range(NTIQ)]
                    sched = []
                    na, nf = len(att), len(ffn)
                    ia = jf = 0
                    for ia in range(na):
                        sched.append(att[ia])
                        target = (ia + 1) * nf // na
                        while jf < target:
                            sched.append(ffn[jf])
                            jf += 1
                    sched += ffn[jf:]
                    return sched

                for g in range(4):
                    with ExitStack() as stps, nc.Block() as blk:
                        krp = [stps.enter_context(nc.psum_tensor(f"krp{g}{i}", [128, 512], F32)) for i in range(2)]
                        vrp = [stps.enter_context(nc.psum_tensor(f"vrp{g}{i}", [128, 256], F32)) for i in range(2)]
                        trwp = stps.enter_context(nc.psum_tensor(f"trw{g}", [8, 128], BF16))
                        T0p, V0p = s_t2p.n, s_v2p.n
                        NKR = 2 * NQC

                        @blk.tensor
                        def _(te):
                            te.wait_ge(s_gw.sem, dmarec["wupv"])
                            if g == 0:
                                te.wait_ge(s_v1b.sem, 10 * NQC)
                            for mt in range(2):
                                for nn in range(NQC):
                                    gi = mt * NQC + nn
                                    if gi >= 2:
                                        te.wait_ge(s_v2p.sem, V0p + 1 + gi - 1)
                                    last = None
                                    for lc in range(2):
                                        last = te.matmul(
                                            krp[gi % 2][:, :],
                                            wupk_b[:, lc * D + g * 256 + mt * 128:lc * D + g * 256 + (mt + 1) * 128],
                                            latT[:, lc * S + nn * 512:lc * S + (nn + 1) * 512],
                                            start=(lc == 0), stop=(lc == 1))
                                    s_t2p.inc(last)
                            for tt in range(NT):
                                if tt >= 2:
                                    te.wait_ge(s_v2p.sem, V0p + 1 + NKR + tt - 1)
                                last = None
                                for lc in range(2):
                                    last = te.matmul(
                                        vrp[tt % 2][:, :],
                                        latT[:, lc * S + tt * 128:lc * S + (tt + 1) * 128],
                                        wupv_b[:, lc * D + g * 256:lc * D + (g + 1) * 256],
                                        start=(lc == 0), stop=(lc == 1))
                                s_t2p.inc(last)
                            for tt in range(NTQ):
                                te.wait_ge(s_v2p.sem, V0p + 1 + NKR + NT + 2 * tt + 1)
                                if tt >= 1:
                                    te.wait_ge(s_v2p.sem, V0p + 1 + NKR + NT + 2 * (tt - 1) + 2)
                                s_t2p.inc(te.transpose(trwp[:, :], wbf[:, :], ident[:, :]))
                            assert s_t2p.n == T0p + NKR + NT + NTQ

                        @blk.vector
                        def _(ve):
                            if g > 0:
                                ve.wait_ge(s_t2m.sem, TQ2 * NJ * g)
                            s_v2p.inc(ve.memset(vrec[:, :], 1.0))
                            for mt in range(2):
                                for nn in range(NQC):
                                    gi = mt * NQC + nn
                                    ve.wait_ge(s_t2p.sem, T0p + gi + 1)
                                    s_v2p.inc(ve.tensor_copy(
                                        krecT[:, mt * S + nn * 512:mt * S + (nn + 1) * 512],
                                        krp[gi % 2][:, :]))
                            ve.wait_ge(s_v2p.sem, V0p + 1)
                            for tt in range(NT):
                                ve.wait_ge(s_t2p.sem, T0p + NKR + tt + 1)
                                last = None
                                for lam in range(4):
                                    last = ve.tensor_copy(
                                        vrec[:, tt * 260 + lam * 65:tt * 260 + lam * 65 + 64],
                                        vrp[tt % 2][:, lam * 64:(lam + 1) * 64])
                                s_v2p.inc(last)
                            for tt in range(NTQ):
                                gt = g * NTQ + tt
                                if tt >= 1:
                                    ve.wait_ge(s_t2p.sem, T0p + NKR + NT + tt)
                                if g > 0:
                                    ve.wait_ge(s_t4w.sem, NTQ * g)
                                s_v2p.inc(ve.tensor_copy(wbf[:, :],
                                                         wgt[:, gt * NE:(gt + 1) * NE]))
                                ve.wait_ge(s_t2p.sem, T0p + NKR + NT + tt + 1)
                                s_v2p.inc(ve.tensor_copy(wT_q[0:8, tt * 128:(tt + 1) * 128],
                                                         trwp[:, :]))
                            assert s_v2p.n == V0p + 1 + NKR + NT + 2 * NTQ

                    with ExitStack() as stps, nc.Block() as blk:
                        scp = [stps.enter_context(nc.psum_tensor(f"scp{g}{i}", [128, 512], F32)) for i in range(2)]
                        pvp = [stps.enter_context(nc.psum_tensor(f"pvp{g}{i}", [128, 512], F32)) for i in range(2)]
                        h1p = [stps.enter_context(nc.psum_tensor(f"h1p{g}{i}", [128, TCQ], F32)) for i in range(2)]
                        yp = [stps.enter_context(nc.psum_tensor(f"yp{g}{i}", [128, 512], F32)) for i in range(2)]
                        v2p_done = s_v2p.n
                        s1c_done = s_s1c.n
                        T2M0 = s_t2m.n
                        FB0 = NFB * g
                        T4W0 = s_t4w.n
                        V4W0 = s_v4w.n

                        def fb_target(j):
                            return ((j + 1) * NFB + NJ - 1) // NJ if j < NJ - 1 else NFB

                        @blk.gpsimd
                        def _(gp):
                            for e in range(9):
                                for fc in range(NFC):
                                    b = FB0 + e * NFC + fc
                                    wsrc = wut_h[e, :, :] if e < NE else swut_h
                                    dsrc = wdt_h[e, :, :] if e < NE else swdt_h
                                    if b >= 1:
                                        gp.wait_ge(s_t4.sem, TQ4 * (b - 1) + (NTCQ - 1) * (NFT + 2 * NTIQ) + NFT)
                                    dmarec[("wu", b)] = dma_ser(gp, s_gw,
                                        wu_sb.ap().rearrange("p (kc f) -> p kc f", kc=NKC),
                                        wsrc[:, fc * FCB:(fc + 1) * FCB].rearrange(
                                            "(kc p) f -> p kc f", p=128))
                                    if b >= 1:
                                        gp.wait_ge(s_t4.sem, TQ4 * b)
                                    dmarec[("wd", b)] = dma_ser(gp, s_gw,
                                        wd_sb.ap().rearrange("p (fi d) -> p fi d", fi=NFT),
                                        dsrc[fc * FCB:(fc + 1) * FCB, :].rearrange(
                                            "(fi p) d -> p fi d", p=128))

                        @blk.sync
                        def _(sy):
                            dmarec[("xnth", g)] = dma_ser(sy, s_qx,
                                xnTh.ap().rearrange("p (kc t) -> p kc t", kc=NKC),
                                xnt_sp.rearrange("p (kc t) -> p kc t", kc=NKC)[:, :, g * QTOK:(g + 1) * QTOK])
                            for e in range(9):
                                src = bu2_h[e, :, :] if e < NE else sbu2_h
                                dmarec[("bu", g, e)] = dma_ser(sy, s_qx,
                                    bu_sb[:, e * 32:e * 32 + 32], src)
                            oj = 0

                            def emit_post(sy, j):
                                nonlocal oj
                                J = g * NJ + j
                                lam, qc = divmod(j, NQC)
                                if lam % 2 == 1:
                                    sy.wait_ge(s_v2m.sem, VQ2 * (J + 1))
                                    oddrec[("odd", g * 2 * NQC + oj)] = dma_ser(sy, s_qx,
                                        pvT[64:128, qc * 512:(qc + 1) * 512],
                                        pvst[oj % 2][0:64, :])
                                    oj += 1
                                    if qc == NQC - 1:
                                        ph = lam // 2
                                        dmarec[("pvspill", g, ph)] = dma_ser(sy, s_qx,
                                            pv_sp[:, (2 * g + ph) * S:(2 * g + ph + 1) * S], pvT[:, :])

                            for j in range(NJ):
                                J = g * NJ + j
                                lam, qc = divmod(j, NQC)
                                pbase = 64 * (lam % 2)
                                qtile = (g * 4 + lam) // 2
                                if j >= 2:
                                    sy.wait_ge(s_t2m.sem, T2M0 + TQ2 * (j - 2) + sc_pos(KT - 1))
                                dmarec[("q", J)] = dma_ser(sy, s_qx,
                                    qstage[j % 2][pbase:pbase + 64, :],
                                    q_sp[pbase:pbase + 64, qtile * S + qc * 512:qtile * S + (qc + 1) * 512])
                                if j >= 1:
                                    sy.wait_ge(s_s2m.sem, SQ2 * (J - 1) + SQ2)
                                    sy.wait_ge(s_v2m.sem, VQ2 * (J - 1))
                                    dma_ser(sy, s_qx, rr_dram, rr_sb[64:65, :])
                                    dmarec[("bc", J - 1)] = dma_ser(sy, s_qx,
                                        bcsb[0:64, :],
                                        bass.AP(rr_dram.tensor, 0, [[0, 64], [1, 512]]))
                                    emit_post(sy, j - 1)
                            sy.wait_ge(s_s2m.sem, SQ2 * (g * NJ + NJ))
                            sy.wait_ge(s_v2m.sem, VQ2 * (g * NJ + NJ - 1))
                            dma_ser(sy, s_qx, rr_dram, rr_sb[64:65, :])
                            dmarec[("bc", g * NJ + NJ - 1)] = dma_ser(sy, s_qx,
                                bcsb[0:64, :],
                                bass.AP(rr_dram.tensor, 0, [[0, 64], [1, 512]]))
                            emit_post(sy, NJ - 1)
                            sy.wait_ge(s_v4.sem, VQ4 * NFB * (g + 1))
                            dmarec[("combspill", g)] = dma_ser(sy, s_qx,
                                comb_sp[g * QTOK:(g + 1) * QTOK, :].rearrange("(tt p) d -> p tt d", p=128),
                                comb.ap().rearrange("p (tt d) -> p tt d", tt=NTQ))

                        def ffn_tensor(te, fb):
                            b = FB0 + fb
                            te.wait_ge(s_gw.sem, dmarec[("wu", b)])
                            if fb == 0:
                                te.wait_ge(s_qx.sem, dmarec[("xnth", g)])
                                te.wait_ge(s_v4w.sem, NTQ * (g + 1))
                            for tc in range(NTCQ):
                                for ft in range(NFT):
                                    hh = SQ4 * b + tc * NFT + ft
                                    if hh >= 1:
                                        te.wait_ge(s_s4.sem, hh)
                                    last = None
                                    for kc in range(NKC):
                                        last = te.matmul(
                                            h1p[:, 0:TCQ],
                                            wu_sb[:, kc * FCB + ft * 128:kc * FCB + (ft + 1) * 128],
                                            xnTh[:, kc * QTOK + tc * TCQ:kc * QTOK + (tc + 1) * TCQ],
                                            start=(kc == 0), stop=(kc == NKC - 1))
                                    s_t4.inc(last)
                                te.wait_ge(s_gw.sem, dmarec[("wd", b)])
                                for ti in range(NTIQ):
                                    yy = VQ4 * b + tc * NTIQ + ti
                                    if yy >= 1:
                                        te.wait_ge(s_v4.sem, yy)
                                    te.wait_ge(s_s4.sem, SQ4 * b + NFT * (tc + 1))
                                    last = None
                                    for kc in range(NFT):
                                        for no, nw in nsplit(D):
                                            last = te.matmul(
                                                yp[:, no:no + nw],
                                                a1s[:, kc * TCQ + ti * 128:kc * TCQ + (ti + 1) * 128],
                                                wd_sb[:, kc * D + no:kc * D + no + nw],
                                                start=(kc == 0), stop=(kc == NFT - 1))
                                    s_t4.inc(last)

                        def ffn_scalar(sc, fb):
                            b = FB0 + fb
                            e, fc = divmod(fb, NFC)
                            if fc == 0:
                                sc.wait_ge(s_qx.sem, dmarec[("bu", g, e)])
                            for tc in range(NTCQ):
                                for ft in range(NFT):
                                    tv = TQ4 * b + tc * (NFT + 2 * NTIQ) + ft + 1
                                    sc.wait_ge(s_t4.sem, tv)
                                    hh = SQ4 * b + tc * NFT + ft
                                    s_s4.inc(sc.activation(
                                        a1s[:, ft * TCQ:(ft + 1) * TCQ],
                                        h1p[hh % 2][:, 0:TCQ], AF.Gelu,
                                        bias=bu_sb[:, e * 32 + fc * NFT + ft:
                                                   e * 32 + fc * NFT + ft + 1]))

                        def ffn_vector(ve, fb):
                            b = FB0 + fb
                            e, fc = divmod(fb, NFC)
                            for tc in range(NTCQ):
                                for ti in range(NTIQ):
                                    tt = tc * NTIQ + ti
                                    tv = TQ4 * b + tc * (NFT + NTIQ) + NFT + ti + 1
                                    ve.wait_ge(s_t4.sem, tv)
                                    gt = g * NTQ + tt
                                    scalar = (wgt[:, gt * NE + e:gt * NE + e + 1]
                                              if e < NE else 0.1)
                                    if fb == 0:
                                        ve.wait_ge(s_v4w.sem, V4W0 + tt + 1)
                                    else:
                                        ve.wait_ge(s_v4.sem, VQ4 * (b - 1) + tt + 1)
                                    s_v4.inc(ve.scalar_tensor_tensor(
                                        comb[:, tt * D:(tt + 1) * D],
                                        yp[:, :], scalar,
                                        comb[:, tt * D:(tt + 1) * D],
                                        op0=ALU.mult, op1=ALU.add))

                        @blk.tensor
                        def _(te):
                            te.wait_ge(s_v2p.sem, v2p_done)
                            te.wait_ge(s_s1c.sem, s1c_done)
                            te.wait_ge(s_gw.sem, dmarec["bd"])
                            te.wait_ge(s_qx.sem, dmarec["sbdrow"])
                            for tt in range(NTQ):
                                if tt >= 1:
                                    te.wait_ge(s_v4w.sem, V4W0 + 2 * (tt - 1) + 2)
                                last = None
                                for nn in range(2):
                                    last = te.matmul(yp[nn][:, :],
                                                     wT_q[0:9, tt * 128:(tt + 1) * 128],
                                                     bd_aug[0:9, nn * 512:(nn + 1) * 512],
                                                     start=True, stop=True)
                                s_t4w.inc(last)

                            def emit_pv(J, kg):
                                j = J - g * NJ
                                lam, qc = divmod(j, NQC)
                                te.wait_ge(s_s2m.sem, SQ2 * J + kg + 1)
                                if kg == 0 and J >= 2:
                                    te.wait_ge(s_v2m.sem, VQ2 * (J - 1))
                                last = None
                                for i in range(EB):
                                    kt = kg * EB + i
                                    last = te.matmul(
                                        pvp[J % 2][0:65, :],
                                        vrec[:, kt * 260 + lam * 65:kt * 260 + lam * 65 + 65],
                                        expT[kg % 2][:, i * 512:(i + 1) * 512],
                                        start=(kt == 0), stop=(kt == KT - 1))
                                trec[(J, "pv", kg)] = s_t2m.inc(last)

                            for tag in make_sched():
                                if tag[0] == "att":
                                    _, j, u = tag
                                    J = g * NJ + j
                                    lam, qc = divmod(j, NQC)
                                    pbase = 64 * (lam % 2)
                                    ktile = lam // 2
                                    if u < NKG:
                                        kg = u
                                        for i in range(EB):
                                            kt = kg * EB + i
                                            m = KT * J + kt
                                            if m >= 2:
                                                jj, kk = divmod(m - 2, KT)
                                                te.wait_ge(s_v2m.sem, VQ2 * jj + kk + 1)
                                            if kt == 0:
                                                te.wait_ge(s_qx.sem, dmarec[("q", J)])
                                            v = s_t2m.inc(te.matmul(
                                                scp[m % 2][:, :],
                                                krecT[pbase:pbase + 64,
                                                      ktile * S + kt * 128:ktile * S + (kt + 1) * 128],
                                                qstage[j % 2][pbase:pbase + 64, :],
                                                start=True, stop=True))
                                            trec[(J, "sc", kt)] = v
                                            assert v == T2M0 + TQ2 * j + sc_pos(kt), (v, j, kt)
                                        if kg >= 1:
                                            emit_pv(J, kg - 1)
                                    else:
                                        emit_pv(J, NKG - 1)
                                elif tag[0] == "h1":
                                    _, fb, tc, ft = tag
                                    b = FB0 + fb
                                    if tc == 0 and ft == 0:
                                        te.wait_ge(s_gw.sem, dmarec[("wu", b)])
                                        if fb == 0:
                                            te.wait_ge(s_qx.sem, dmarec[("xnth", g)])
                                            te.wait_ge(s_v4w.sem, 2 * NTQ * (g + 1))
                                    hh = SQ4 * b + tc * NFT + ft
                                    if hh >= 2:
                                        te.wait_ge(s_s4.sem, hh - 1)
                                    last = None
                                    for kc in range(NKC):
                                        last = te.matmul(
                                            h1p[hh % 2][:, 0:TCQ],
                                            wu_sb[:, kc * FCB + ft * 128:kc * FCB + (ft + 1) * 128],
                                            xnTh[:, kc * QTOK + tc * TCQ:kc * QTOK + (tc + 1) * TCQ],
                                            start=(kc == 0), stop=(kc == NKC - 1))
                                    s_t4.inc(last)
                                else:
                                    _, fb, tc, ti = tag
                                    b = FB0 + fb
                                    if ti == 0:
                                        te.wait_ge(s_gw.sem, dmarec[("wd", b)])
                                    te.wait_ge(s_s4.sem, SQ4 * b + NFT * (tc + 1))
                                    for nn in range(2):
                                        yh = VQ4 * b + 2 * (tc * NTIQ + ti) + nn
                                        if yh >= 2:
                                            te.wait_ge(s_v4.sem, yh - 1)
                                        last = None
                                        for kc in range(NFT):
                                            last = te.matmul(
                                                yp[yh % 2][:, :],
                                                a1s[:, kc * TCQ + ti * 128:kc * TCQ + (ti + 1) * 128],
                                                wd_sb[:, kc * D + nn * 512:kc * D + (nn + 1) * 512],
                                                start=(kc == 0), stop=(kc == NFT - 1))
                                        s_t4.inc(last)
                            assert s_t2m.n == TQ2 * NJ * (g + 1)
                            assert s_t4.n == TQ4 * NFB * (g + 1)

                        @blk.scalar
                        def _(sc):
                            for tag in make_sched():
                                if tag[0] == "att":
                                    _, j, u = tag
                                    J = g * NJ + j
                                    if u == 0:
                                        continue
                                    if u <= NKG:
                                        kg = u - 1
                                        pj = NKG * J + kg
                                        sc.wait_ge(s_v2m.sem, VQ2 * J + EB * (kg + 1))
                                        s_s2m.inc(sc.activation(expT[kg % 2][:, :],
                                                                sTb[kg % 2][:, :], AF.Exp))
                                    if u == NKG:
                                        sc.wait_ge(s_t2m.sem, trec[(J, "pv", NKG - 1)])
                                        if J >= 1:
                                            sc.wait_ge(s_qx.sem, dmarec[("bc", J - 1)])
                                        s_s2m.inc(sc.activation(rln[64:65, :], pvp[J % 2][64:65, :],
                                                                AF.Ln))
                                        sc.wait_ge(s_s2m.sem, SQ2 * J + SQ2 - 1)
                                        s_s2m.inc(sc.activation(rr_sb[64:65, :], rln[64:65, :],
                                                                AF.Exp, scale=consts[64:65, 3:4]))
                                elif tag[0] == "h1":
                                    _, fb, tc, ft = tag
                                    b = FB0 + fb
                                    e, fc = divmod(fb, NFC)
                                    if fc == 0 and tc == 0 and ft == 0:
                                        sc.wait_ge(s_qx.sem, dmarec[("bu", g, e)])
                                    tv = TQ4 * b + tc * (NFT + 2 * NTIQ) + ft + 1
                                    sc.wait_ge(s_t4.sem, tv)
                                    hh = SQ4 * b + tc * NFT + ft
                                    s_s4.inc(sc.activation(
                                        a1s[:, ft * TCQ:(ft + 1) * TCQ],
                                        h1p[hh % 2][:, 0:TCQ], AF.Gelu,
                                        bias=bu_sb[:, e * 32 + fc * NFT + ft:
                                                   e * 32 + fc * NFT + ft + 1]))
                            assert s_s2m.n == SQ2 * NJ * (g + 1)
                            assert s_s4.n == SQ4 * NFB * (g + 1)

                        @blk.vector
                        def _(ve):
                            if g == 0:
                                ve.wait_ge(s_s2s.sem, 2)
                            ve.wait_ge(s_v2p.sem, v2p_done)
                            if g > 0:
                                ve.wait_ge(s_qx.sem, dmarec[("combspill", g - 1)])
                            for tt in range(NTQ):
                                ve.wait_ge(s_t4w.sem, T4W0 + tt + 1)
                                s_v4w.inc(ve.tensor_copy(comb[:, tt * D:tt * D + 512],
                                                         yp[0][:, :]))
                                s_v4w.inc(ve.tensor_copy(comb[:, tt * D + 512:(tt + 1) * D],
                                                         yp[1][:, :]))
                            for tag in make_sched():
                                if tag[0] == "att":
                                    _, j, u = tag
                                    J = g * NJ + j
                                    lam, qc = divmod(j, NQC)
                                    odd = (lam % 2 == 1)
                                    if u < NKG:
                                        kg = u
                                        for i in range(EB):
                                            kt = kg * EB + i
                                            pj = NKG * J + kg
                                            c0 = (S - 1) - 128 * kt + 512 * qc
                                            slot = sTb[kg % 2][:, (kt % EB) * 512:(kt % EB + 1) * 512]
                                            if pj >= 2:
                                                jj, kk = divmod(pj - 2, NKG)
                                                ve.wait_ge(s_t2m.sem, trec[(jj, "pv", kk)])
                                            ve.wait_ge(s_t2m.sem, trec[(J, "sc", kt)])
                                            s_v2m.inc(ve.tensor_mul(
                                                slot,
                                                scp[(KT * J + kt) % 2][:, :],
                                                decayM[:, c0:c0 + 512]))
                                    else:
                                        ve.wait_ge(s_t2m.sem, trec[(J, "pv", NKG - 1)])
                                        ve.wait_ge(s_qx.sem, dmarec[("bc", J)])
                                        if odd:
                                            oj = g * 2 * NQC + sum(1 for jj2 in range(j) if (jj2 // NQC) % 2 == 1)
                                            if ("odd", oj - 2) in oddrec:
                                                ve.wait_ge(s_qx.sem, oddrec[("odd", oj - 2)])
                                            dst = pvst[oj % 2][0:64, :]
                                        else:
                                            if qc == 0 and lam // 2 == 1:
                                                ve.wait_ge(s_qx.sem, dmarec[("pvspill", g, 0)])
                                            dst = pvT[0:64, qc * 512:(qc + 1) * 512]
                                        s_v2m.inc(ve.tensor_mul(dst, pvp[J % 2][0:64, :], bcsb[:, :]))
                                elif tag[0] == "y":
                                    _, fb, tc, ti = tag
                                    b = FB0 + fb
                                    e, fc = divmod(fb, NFC)
                                    tt = tc * NTIQ + ti
                                    gt = g * NTQ + tt
                                    scalar = (wgt[:, gt * NE + e:gt * NE + e + 1]
                                              if e < NE else 0.1)
                                    for nn in range(2):
                                        yh = VQ4 * b + 2 * tt + nn
                                        tv = TQ4 * b + tc * (NFT + 2 * NTIQ) + NFT + 2 * ti + nn + 1
                                        ve.wait_ge(s_t4.sem, tv)
                                        if fb == 0:
                                            ve.wait_ge(s_v4w.sem, V4W0 + 2 * tt + nn + 1)
                                        else:
                                            ve.wait_ge(s_v4.sem, VQ4 * (b - 1) + 2 * tt + nn + 1)
                                        cs = comb[:, tt * D + nn * 512:tt * D + (nn + 1) * 512]
                                        s_v4.inc(ve.scalar_tensor_tensor(
                                            cs, yp[yh % 2][:, :], scalar, cs,
                                            op0=ALU.mult, op1=ALU.add))
                            assert s_v2m.n == VQ2 * NJ * (g + 1)
                            assert s_v4.n == VQ4 * NFB * (g + 1)
                            assert s_v4w.n == 2 * NTQ * (g + 1)

            with ExitStack() as stp3:
                sb3 = lambda n, sh, dt: stp3.enter_context(nc.sbuf_tensor(n, sh, dt, side="right"))
                pvF = sb3("pvF", [128, NKC * S], BF16)
                wo_b = sb3("wo_b", [128, NKC * D], BF16)
                ast = sb3("ast", [128, 2 * D], F32)
                cst = sb3("cst", [128, 2 * D], F32)
                xst = sb3("xst3", [128, 2 * D], F32)
                with ExitStack() as stps, nc.Block() as blk:
                    wop = [stps.enter_context(nc.psum_tensor(f"wop{i}", [128, 512], F32)) for i in range(2)]

                    @blk.gpsimd
                    def _(gp):
                        dmarec["wo"] = dma_ser(gp, s_gw,
                            wo_b.ap().rearrange("p (kc d) -> p kc d", kc=NKC),
                            wo_h.rearrange("(kc p) d -> p kc d", p=128))

                    @blk.sync
                    def _(sy):
                        sy.wait_ge(s_qx.sem, dmarec[("pvspill", 3, 1)])
                        dmarec["pvload"] = dma_ser(sy, s_qx, pvF[:, :], pv_sp)
                        for tt in range(NT):
                            if tt >= 2:
                                sy.wait_ge(s_v3.sem, 4 * (tt - 2) + 4)
                            dmarec[("c3", tt)] = dma_ser(sy, s_qx,
                                cst[:, (tt % 2) * D:(tt % 2 + 1) * D],
                                comb_sp[tt * 128:(tt + 1) * 128, :])
                            dmarec[("x3", tt)] = dma_ser(sy, s_qx,
                                xst[:, (tt % 2) * D:(tt % 2 + 1) * D],
                                x_h[tt * 128:(tt + 1) * 128, :])
                            sy.wait_ge(s_v3.sem, 4 * tt + 4)
                            dmarec[("out", tt)] = dma_ser(sy, s_qx,
                                out_h[tt * 128:(tt + 1) * 128, :],
                                ast[:, (tt % 2) * D:(tt % 2 + 1) * D])
                        sy.wait_ge(s_qx.sem, dmarec[("out", NT - 1)])

                    @blk.tensor
                    def _(te):
                        te.wait_ge(s_gw.sem, dmarec["wo"])
                        te.wait_ge(s_qx.sem, dmarec["pvload"])
                        for tt in range(NT):
                            for nn in range(2):
                                gi = 2 * tt + nn
                                if gi >= 2:
                                    te.wait_ge(s_v3.sem, 4 * (tt - 1) + nn + 1)
                                last = None
                                for kc in range(NKC):
                                    last = te.matmul(
                                        wop[gi % 2][:, :],
                                        pvF[:, kc * S + tt * 128:kc * S + (tt + 1) * 128],
                                        wo_b[:, kc * D + nn * 512:kc * D + (nn + 1) * 512],
                                        start=(kc == 0), stop=(kc == NKC - 1))
                                s_t3.inc(last)
                        assert s_t3.n == 2 * NT

                    @blk.vector
                    def _(ve):
                        for tt in range(NT):
                            for nn in range(2):
                                gi = 2 * tt + nn
                                ve.wait_ge(s_t3.sem, gi + 1)
                                if tt >= 2 and nn == 0:
                                    ve.wait_ge(s_qx.sem, dmarec[("out", tt - 2)])
                                s_v3.inc(ve.tensor_copy(
                                    ast[:, (tt % 2) * D + nn * 512:(tt % 2) * D + (nn + 1) * 512],
                                    wop[gi % 2][:, :]))
                            ve.wait_ge(s_qx.sem, dmarec[("x3", tt)])
                            ve.wait_ge(s_v3.sem, 4 * tt + 2)
                            s_v3.inc(ve.tensor_add(ast[:, (tt % 2) * D:(tt % 2 + 1) * D],
                                                   ast[:, (tt % 2) * D:(tt % 2 + 1) * D],
                                                   xst[:, (tt % 2) * D:(tt % 2 + 1) * D]))
                            ve.wait_ge(s_v3.sem, 4 * tt + 3)
                            s_v3.inc(ve.tensor_add(ast[:, (tt % 2) * D:(tt % 2 + 1) * D],
                                                   ast[:, (tt % 2) * D:(tt % 2 + 1) * D],
                                                   cst[:, (tt % 2) * D:(tt % 2 + 1) * D]))
                        assert s_v3.n == 4 * NT
    return nc


def _prep_inputs(L0_attn_x, L0_attn_Wq, L0_attn_Wk, L0_attn_Wv, L0_attn_Wo,
                 L0_attn_W_down_kv, L0_attn_W_up_k, L0_attn_W_up_v,
                 L0_ffn_gate_w, L0_ffn_gate_bias, L0_ffn_Wu, L0_ffn_bu,
                 L0_ffn_Wd, L0_ffn_bd, L0_ffn_sWu, L0_ffn_sbu, L0_ffn_sWd,
                 L0_ffn_sbd, S):
    c = np.ascontiguousarray
    f = lambda a: np.asarray(a, dtype=np.float32)
    shared = {
        "wq": c(f(L0_attn_Wq)),
        "wk": c(f(L0_attn_Wk)),
        "wv": c(f(L0_attn_Wv)),
        "wo": c(f(L0_attn_Wo)),
        "wdn": c(f(L0_attn_W_down_kv)),
        "wupk": c(f(L0_attn_W_up_k)),
        "wupv": c(f(L0_attn_W_up_v)),
        "gwt": c(f(L0_ffn_gate_w).T),
        "gbias": c(f(L0_ffn_gate_bias).reshape(1, NE)),
        "wut": c(f(L0_ffn_Wu).transpose(0, 2, 1)),
        "bu2": c(f(L0_ffn_bu).reshape(NE, FF // 128, 128).transpose(0, 2, 1)),
        "wdt": c(f(L0_ffn_Wd).transpose(0, 2, 1)),
        "bd": c(f(L0_ffn_bd)),
        "swut": c(f(L0_ffn_sWu).T),
        "sbu2": c(f(L0_ffn_sbu).reshape(FF // 128, 128).T),
        "swdt": c(f(L0_ffn_sWd).T),
        "sbd": c(f(L0_ffn_sbd).reshape(1, D)),
    }
    x = f(L0_attn_x)
    in_maps = []
    for b in range(x.shape[0]):
        m = dict(shared)
        m["x"] = c(x[b, :S, :])
        in_maps.append(m)
    return in_maps


def kernel(**inputs):
    S = 4096
    in_maps = _prep_inputs(S=S, **inputs)
    nc = build(S=S)
    res = run_bass_kernel_spmd(nc, in_maps, core_ids=list(range(len(in_maps))))
    return np.stack([r["out"] for r in res.results], axis=0)


# version 12
# speedup vs baseline: 1.5830x; 1.1814x over previous
import numpy as np
from contextlib import ExitStack

import concourse.bass as bass
import concourse.mybir as mybir
from concourse.bass_utils import run_bass_kernel_spmd

F32 = mybir.dt.float32
BF16 = mybir.dt.bfloat16
AF = mybir.ActivationFunctionType
ALU = mybir.AluOpType

D = 1024
NKC = D // 128
NH = 16
HD = 64
DL = 256
NE = 8
FF = 4096
FCB = 1024
NFC = FF // FCB
NFT = FCB // 128
QB = 512
EXPB = 4

LN_EPS = 1e-5
GATE_EPS = 1e-6
DECAY_SCALE = -(0.01 * 0.01)
ATT_SCALE_LNBIAS = float(np.log(1.0 / (np.sqrt(np.float64(1024.0)) + 1e-8)))


class Chan:

    def __init__(self, stack, nc, name):
        self.sem = stack.enter_context(nc.semaphore(name))
        self.n = 0

    def inc(self, inst, k=1):
        inst.then_inc(self.sem, k)
        self.n += k
        return self.n


def nsplit(total, maxn=512):
    return [(o, min(maxn, total - o)) for o in range(0, total, maxn)]


def dma_ser(eng, chan, out, in_):
    v = chan.inc(eng.dma_start(out, in_), 16)
    eng.wait_ge(chan.sem, v)
    return v


def build(S=4096, debug=False):
    NT = S // 128
    NQC = S // QB
    KT = NT
    NKG = KT // EXPB
    TH = S // 2
    NTTH = TH // 128
    TC = 1024 if TH % 1024 == 0 else TH
    NTC = TH // TC
    NTI = TC // 128

    nc = bass.Bass("TRN2", target_bir_lowering=False, debug=False)

    x_h = nc.dram_tensor("x", [S, D], F32, kind="ExternalInput").ap()
    wq_h = nc.dram_tensor("wq", [D, D], F32, kind="ExternalInput").ap()
    wk_h = nc.dram_tensor("wk", [D, D], F32, kind="ExternalInput").ap()
    wv_h = nc.dram_tensor("wv", [D, D], F32, kind="ExternalInput").ap()
    wo_h = nc.dram_tensor("wo", [D, D], F32, kind="ExternalInput").ap()
    wdn_h = nc.dram_tensor("wdn", [D, DL], F32, kind="ExternalInput").ap()
    wupk_h = nc.dram_tensor("wupk", [DL, D], F32, kind="ExternalInput").ap()
    wupv_h = nc.dram_tensor("wupv", [DL, D], F32, kind="ExternalInput").ap()
    gwt_h = nc.dram_tensor("gwt", [D, NE], F32, kind="ExternalInput").ap()
    gbias_h = nc.dram_tensor("gbias", [1, NE], F32, kind="ExternalInput").ap()
    wut_h = nc.dram_tensor("wut", [NE, D, FF], F32, kind="ExternalInput").ap()
    bu2_h = nc.dram_tensor("bu2", [NE, 128, FF // 128], F32, kind="ExternalInput").ap()
    wdt_h = nc.dram_tensor("wdt", [NE, FF, D], F32, kind="ExternalInput").ap()
    bd_h = nc.dram_tensor("bd", [NE, D], F32, kind="ExternalInput").ap()
    swut_h = nc.dram_tensor("swut", [D, FF], F32, kind="ExternalInput").ap()
    sbu2_h = nc.dram_tensor("sbu2", [128, FF // 128], F32, kind="ExternalInput").ap()
    swdt_h = nc.dram_tensor("swdt", [FF, D], F32, kind="ExternalInput").ap()
    sbd_h = nc.dram_tensor("sbd", [1, D], F32, kind="ExternalInput").ap()
    out_h = nc.dram_tensor("out", [S, D], F32, kind="ExternalOutput").ap()

    scratch_kind = "ExternalOutput" if debug else "Internal"
    xnt_sp = nc.dram_tensor("xnt_sp", [128, NKC * S], BF16, kind=scratch_kind).ap()
    pv_sp = nc.dram_tensor("pv_sp", [128, NKC * S], BF16, kind=scratch_kind).ap()
    attn_sp = nc.dram_tensor("attn_sp", [S, D], F32, kind=scratch_kind).ap()
    if debug:
        dbg_lat = nc.dram_tensor("dbg_lat", [128, 2 * S], BF16, kind="ExternalOutput").ap()
        dbg_q = nc.dram_tensor("dbg_q", [128, NKC * S], BF16, kind="ExternalOutput").ap()
        dbg_gsig = nc.dram_tensor("dbg_gsig", [128, NE * NT], F32, kind="ExternalOutput").ap()
        dbg_w = nc.dram_tensor("dbg_w", [128, NE * NT], F32, kind="ExternalOutput").ap()

    st = ExitStack()
    with st:
        def sbuf(name, shape, dtype, **kw):
            return st.enter_context(nc.sbuf_tensor(name, shape, dtype, **kw))

        ident = sbuf("ident", [128, 128], BF16)
        onesf = sbuf("onesf", [128, 128], BF16)
        consts = sbuf("consts", [128, 4], F32)
        mrv = sbuf("mrv", [128, 8], F32)
        gsig = sbuf("gsig", [128, NE * NT], F32)
        wgt = sbuf("wgt", [128, NE * NT], F32)
        wupk_b = sbuf("wupk_b", [128, 2 * D], BF16)
        wupv_b = sbuf("wupv_b", [128, 2 * D], BF16)
        gwt_b = sbuf("gwt_b", [128, NKC * NE], BF16)
        gbias_b = sbuf("gbias_b", [1, NE], BF16)
        bd_aug = sbuf("bd_aug", [32, D], BF16)
        wt_sb = sbuf("wt_sb", [32, 128], BF16)
        sbd_st = sbuf("sbd_st", [1, D], F32)
        sbd_bf = sbuf("sbd_bf", [1, D], BF16)

        s_gw = Chan(st, nc, "s_gw")
        s_qx = Chan(st, nc, "s_qx")
        s_gpc = Chan(st, nc, "s_gpc")
        s_v0 = Chan(st, nc, "s_v0")
        s_t1 = Chan(st, nc, "s_t1")
        s_s1 = Chan(st, nc, "s_s1")
        s_v1 = Chan(st, nc, "s_v1")
        s_t1b = Chan(st, nc, "s_t1b")
        s_v1b = Chan(st, nc, "s_v1b")
        s_t1c = Chan(st, nc, "s_t1c")
        s_s1c = Chan(st, nc, "s_s1c")
        s_v2s = Chan(st, nc, "s_v2s")
        s_s2s = Chan(st, nc, "s_s2s")
        s_t2p = Chan(st, nc, "s_t2p")
        s_v2p = Chan(st, nc, "s_v2p")
        s_t2m = Chan(st, nc, "s_t2m")
        s_s2m = Chan(st, nc, "s_s2m")
        s_v2m = Chan(st, nc, "s_v2m")
        s_t2mB = Chan(st, nc, "s_t2mB")
        s_s2mB = Chan(st, nc, "s_s2mB")
        s_v2mB = Chan(st, nc, "s_v2mB")
        s_t3 = Chan(st, nc, "s_t3")
        s_v3 = Chan(st, nc, "s_v3")
        s_t4w = Chan(st, nc, "s_t4w")
        s_v4w = Chan(st, nc, "s_v4w")
        s_t4 = Chan(st, nc, "s_t4")
        s_s4 = Chan(st, nc, "s_s4")
        s_v4 = Chan(st, nc, "s_v4")
        s_v5 = Chan(st, nc, "s_v5")

        dmarec = {}

        with nc.Block() as blk:
            @blk.gpsimd
            def _(gp):
                s_gpc.inc(gp.iota(ident[:, :], pattern=[[-1, 128]], base=0,
                                  channel_multiplier=1,
                                  allow_small_or_imprecise_dtypes=True))
                dmarec["gwt"] = dma_ser(gp, s_gw, gwt_b.ap().rearrange("p (kc e) -> p kc e", kc=NKC),
                    gwt_h.rearrange("(kc p) e -> p kc e", p=128))
                dmarec["gbias"] = dma_ser(gp, s_gw, gbias_b[:, :], gbias_h)
                dmarec["wupk"] = dma_ser(gp, s_gw, wupk_b.ap().rearrange("p (lc d) -> p lc d", lc=2),
                    wupk_h.rearrange("(lc p) d -> p lc d", p=128))
                dmarec["wupv"] = dma_ser(gp, s_gw, wupv_b.ap().rearrange("p (lc d) -> p lc d", lc=2),
                    wupv_h.rearrange("(lc p) d -> p lc d", p=128))
                dmarec["bd"] = dma_ser(gp, s_gw, bd_aug[0:8, :], bd_h)

            @blk.sync
            def _(sy):
                dmarec["sbd"] = dma_ser(sy, s_qx, sbd_st[:, :], sbd_h)
                sy.wait_ge(s_v0.sem, 5)
                dmarec["sbdrow"] = dma_ser(sy, s_qx, bd_aug[8:9, :], sbd_bf[:, :])

            @blk.vector
            def _(ve):
                ve.memset(consts[:, 0:1], LN_EPS)
                ve.memset(consts[:, 1:2], DECAY_SCALE)
                ve.memset(consts[:, 2:3], ATT_SCALE_LNBIAS)
                ve.memset(consts[:, 3:4], -1.0)
                ve.memset(onesf[:, :], 1.0)
                ve.memset(mrv[:, 2:8], -7.25e30)
                s_v0.inc(ve.memset(wt_sb[0:32, :], 1.0))
                ve.wait_ge(s_gpc.sem, 1)
                s_v0.inc(ve.tensor_single_scalar(ident[:, :], ident[:, :], 0.0,
                                                 op=ALU.is_equal))
                ve.wait_ge(s_qx.sem, dmarec["sbd"])
                s_v0.inc(ve.tensor_single_scalar(sbd_st[:, :], sbd_st[:, :], 1.0,
                                                 op=ALU.mult))
                ve.wait_ge(s_v0.sem, 3)
                s_v0.inc(ve.tensor_single_scalar(sbd_st[:, :], sbd_st[:, :], 0.1,
                                                 op=ALU.mult))
                ve.wait_ge(s_v0.sem, 4)
                s_v0.inc(ve.tensor_copy(sbd_bf[:, :], sbd_st[:, :]))

        with ExitStack() as stp1:
            latT = stp1.enter_context(nc.sbuf_tensor("latT", [128, 2 * S], BF16, side="left"))
            with ExitStack() as stp1q:
                xnT = stp1q.enter_context(nc.sbuf_tensor("xnT", [128, NKC * S], BF16, side="right"))
                wq_b = stp1q.enter_context(nc.sbuf_tensor("wq_b", [128, NKC * D], BF16, side="right"))
                with ExitStack() as stp1b:
                    wk_b = stp1b.enter_context(nc.sbuf_tensor("wk_b", [128, NKC * D], BF16, side="right"))
                    wv_b = stp1b.enter_context(nc.sbuf_tensor("wv_b", [128, NKC * D], BF16, side="right"))
                    wdn_b = stp1b.enter_context(nc.sbuf_tensor("wdn_b", [128, NKC * DL], BF16, side="right"))
                    kv_b = stp1b.enter_context(nc.sbuf_tensor("kv_b", [128, NKC * 512], BF16, side="right"))
                    with ExitStack() as stp1a:
                        sb1 = lambda n, sh, dt: stp1a.enter_context(nc.sbuf_tensor(n, sh, dt, side="right"))
                        xst = sb1("xst", [128, 2 * D], F32)
                        stats = sb1("stats", [128, 2 * 12], F32)
                        mv = sb1("mv", [128, 2 * 2], F32)
                        rstd = sb1("rstd", [128, 2], F32)
                        nmr = sb1("nmr", [128, 2], F32)
                        xnbf = sb1("xnbf", [128, 2 * D], BF16)
                        m8 = sb1("m8", [128, 8], F32)
                        killed = sb1("killed", [128, 8], F32)
                        gmask = sb1("gmask", [128, 8], F32)
                        gden = sb1("gden", [128, 1], F32)

                        VQ1, SQ1, TQ1 = 13, 11, 9
                        with ExitStack() as stps, nc.Block() as blk:
                            trp = [stps.enter_context(nc.psum_tensor(f"trp{i}", [128, 128], BF16)) for i in range(2)]
                            gps = [stps.enter_context(nc.psum_tensor(f"gps{i}", [128, NE], F32)) for i in range(2)]

                            @blk.gpsimd
                            def _(gp):
                                for nm, dst, src in (("wk", wk_b, wk_h), ("wv", wv_b, wv_h),
                                                     ("wq", wq_b, wq_h)):
                                    dmarec[nm] = dma_ser(gp, s_gw, dst.ap().rearrange("p (kc d) -> p kc d", kc=NKC),
                                        src.rearrange("(kc p) d -> p kc d", p=128))
                                dmarec["wdn"] = dma_ser(gp, s_gw, wdn_b.ap().rearrange("p (kc l) -> p kc l", kc=NKC),
                                    wdn_h.rearrange("(kc p) l -> p kc l", p=128))

                            @blk.sync
                            def _(sy):
                                for tt in range(NT):
                                    if tt >= 2:
                                        sy.wait_ge(s_v1.sem, VQ1 * (tt - 2) + 1)
                                        sy.wait_ge(s_s1.sem, SQ1 * (tt - 2) + 2)
                                    dmarec[("xt", tt)] = dma_ser(sy, s_qx, xst[:, (tt % 2) * D:(tt % 2 + 1) * D],
                                        x_h[tt * 128:(tt + 1) * 128, :])

                            @blk.tensor
                            def _(te):
                                te.wait_ge(s_v0.sem, 2)
                                te.wait_ge(s_gw.sem, dmarec["gbias"])
                                for tt in range(NT):
                                    te.wait_ge(s_s1.sem, SQ1 * tt + 2)
                                    for i in range(8):
                                        gi = 8 * tt + i
                                        if gi >= 2:
                                            pj, pi = divmod(gi - 2, 8)
                                            te.wait_ge(s_s1.sem, SQ1 * pj + 3 + pi)
                                        s_t1.inc(te.transpose(
                                            trp[gi % 2][:, :],
                                            xnbf[:, (tt % 2) * D + i * 128:(tt % 2) * D + (i + 1) * 128],
                                            ident[:, :]))
                                    te.wait_ge(s_s1.sem, SQ1 * tt + 10)
                                    if tt >= 2:
                                        te.wait_ge(s_s1.sem, SQ1 * (tt - 2) + 11)
                                    for kc in range(NKC):
                                        te.matmul(gps[tt % 2][:, :],
                                                  xnT[:, kc * S + tt * 128:kc * S + (tt + 1) * 128],
                                                  gwt_b[:, kc * NE:(kc + 1) * NE],
                                                  start=(kc == 0), stop=False)
                                    s_t1.inc(te.matmul(gps[tt % 2][:, :], onesf[0:1, 0:128],
                                                       gbias_b[0:1, :], start=False, stop=True))
                                assert s_t1.n == TQ1 * NT

                            @blk.scalar
                            def _(sc):
                                for tt in range(NT):
                                    sc.wait_ge(s_v1.sem, VQ1 * tt + 2)
                                    s_s1.inc(sc.activation(rstd[:, tt % 2:tt % 2 + 1],
                                                           mv[:, (tt % 2) * 2 + 1:(tt % 2) * 2 + 2],
                                                           AF.Sqrt, bias=consts[:, 0:1]))
                                    sc.wait_ge(s_v1.sem, VQ1 * tt + 5)
                                    sc.wait_ge(s_qx.sem, dmarec[("xt", tt)])
                                    if tt >= 2:
                                        sc.wait_ge(s_t1.sem, TQ1 * (tt - 2) + 8)
                                    s_s1.inc(sc.activation(
                                        xnbf[:, (tt % 2) * D:(tt % 2 + 1) * D],
                                        xst[:, (tt % 2) * D:(tt % 2 + 1) * D],
                                        AF.Identity,
                                        bias=nmr[:, tt % 2:tt % 2 + 1],
                                        scale=rstd[:, tt % 2:tt % 2 + 1]))
                                    for i in range(8):
                                        sc.wait_ge(s_t1.sem, TQ1 * tt + i + 1)
                                        s_s1.inc(sc.activation(
                                            xnT[:, i * S + tt * 128:i * S + (tt + 1) * 128],
                                            trp[(8 * tt + i) % 2][:, :], AF.Identity))
                                    sc.wait_ge(s_t1.sem, TQ1 * tt + 9)
                                    s_s1.inc(sc.activation(gsig[:, tt * NE:(tt + 1) * NE],
                                                           gps[tt % 2][:, :], AF.Sigmoid))
                                assert s_s1.n == SQ1 * NT

                            @blk.vector
                            def _(ve):
                                ve.wait_ge(s_v0.sem, 5)
                                for tt in range(NT):
                                    b = tt % 2
                                    ve.wait_ge(s_qx.sem, dmarec[("xt", tt)])
                                    if tt >= 2:
                                        ve.wait_ge(s_v1.sem, VQ1 * (tt - 2) + 2)
                                    ve.bn_stats(stats[:, b * 12:b * 12 + 6],
                                                xst[:, b * D:b * D + 512])
                                    s_v1.inc(ve.bn_stats(stats[:, b * 12 + 6:b * 12 + 12],
                                                         xst[:, b * D + 512:b * D + 1024]))
                                    ve.wait_ge(s_v1.sem, VQ1 * tt + 1)
                                    if tt >= 2:
                                        ve.wait_ge(s_s1.sem, SQ1 * (tt - 2) + 1)
                                    s_v1.inc(ve.bn_aggr(mv[:, b * 2:b * 2 + 2],
                                                        stats[:, b * 12:b * 12 + 12]))
                                    ve.wait_ge(s_s1.sem, SQ1 * tt + 1)
                                    s_v1.inc(ve.reciprocal(rstd[:, b:b + 1], rstd[:, b:b + 1]))
                                    ve.wait_ge(s_v1.sem, VQ1 * tt + 3)
                                    s_v1.inc(ve.tensor_mul(nmr[:, b:b + 1], mv[:, b * 2:b * 2 + 1],
                                                           rstd[:, b:b + 1]))
                                    ve.wait_ge(s_v1.sem, VQ1 * tt + 4)
                                    s_v1.inc(ve.tensor_single_scalar(nmr[:, b:b + 1], nmr[:, b:b + 1],
                                                                     -1.0, op=ALU.mult))
                                    gs = gsig[:, tt * NE:(tt + 1) * NE]
                                    ve.wait_ge(s_s1.sem, SQ1 * tt + 11)
                                    s_v1.inc(ve.max(m8[:, :], gs))
                                    ve.wait_ge(s_v1.sem, VQ1 * tt + 6)
                                    if tt >= 1:
                                        ve.wait_ge(s_v1.sem, VQ1 * (tt - 1) + 8)
                                    s_v1.inc(ve.tensor_copy(mrv[:, 0:2], m8[:, 0:2]))
                                    ve.wait_ge(s_v1.sem, VQ1 * tt + 7)
                                    s_v1.inc(ve.match_replace(killed[:, :], mrv[:, :], gs, -1e30))
                                    ve.wait_ge(s_v1.sem, VQ1 * tt + 8)
                                    s_v1.inc(ve.tensor_tensor(gmask[:, :], gs, killed[:, :],
                                                              op=ALU.is_gt))
                                    s_v1.inc(ve.tensor_add(gden[:, :], m8[:, 0:1], m8[:, 1:2]))
                                    ve.wait_ge(s_v1.sem, VQ1 * tt + 10)
                                    s_v1.inc(ve.tensor_single_scalar(gden[:, :], gden[:, :],
                                                                     GATE_EPS, op=ALU.add))
                                    ve.wait_ge(s_v1.sem, VQ1 * tt + 11)
                                    s_v1.inc(ve.reciprocal(gden[:, :], gden[:, :]))
                                    ve.wait_ge(s_v1.sem, VQ1 * tt + 12)
                                    s_v1.inc(ve.scalar_tensor_tensor(
                                        wgt[:, tt * NE:(tt + 1) * NE], gs, gden[:, :],
                                        gmask[:, :], op0=ALU.mult, op1=ALU.mult))
                                assert s_v1.n == VQ1 * NT

                    with ExitStack() as stps, nc.Block() as blk:
                        kvp = [stps.enter_context(nc.psum_tensor(f"kvp{i}", [128, 512], F32)) for i in range(2)]
                        latp = [stps.enter_context(nc.psum_tensor(f"latp{i}", [128, 512], F32)) for i in range(2)]

                        @blk.sync
                        def _(sy):
                            dmarec["xnt_spill"] = dma_ser(sy, s_qx, xnt_sp, xnT[:, :])

                        @blk.tensor
                        def _(te):
                            te.wait_ge(s_gw.sem, dmarec["wdn"])
                            te.wait_ge(s_s1.sem, SQ1 * NT)
                            for nn in range(NQC):
                                for mt in range(8):
                                    gi = 10 * nn + mt
                                    if gi >= 2:
                                        te.wait_ge(s_v1b.sem, gi - 1)
                                    first = True
                                    last = None
                                    for wsrc in (wk_b, wv_b):
                                        for kc in range(NKC):
                                            last = te.matmul(
                                                kvp[mt % 2][:, :],
                                                wsrc[:, kc * D + mt * 128:kc * D + (mt + 1) * 128],
                                                xnT[:, kc * S + nn * 512:kc * S + (nn + 1) * 512],
                                                start=first,
                                                stop=(wsrc is wv_b and kc == NKC - 1))
                                            first = False
                                    s_t1b.inc(last)
                                te.wait_ge(s_v1b.sem, 10 * nn + 8)
                                for lt in range(2):
                                    gi = 10 * nn + 8 + lt
                                    if gi >= 2:
                                        te.wait_ge(s_v1b.sem, gi - 1)
                                    last = None
                                    for kc in range(NKC):
                                        last = te.matmul(
                                            latp[lt][:, :],
                                            wdn_b[:, kc * DL + lt * 128:kc * DL + (lt + 1) * 128],
                                            kv_b[:, kc * 512:(kc + 1) * 512],
                                            start=(kc == 0), stop=(kc == NKC - 1))
                                    s_t1b.inc(last)
                            assert s_t1b.n == 10 * NQC

                        @blk.vector
                        def _(ve):
                            for nn in range(NQC):
                                for mt in range(8):
                                    ve.wait_ge(s_t1b.sem, 10 * nn + mt + 1)
                                    if mt == 0 and nn >= 1:
                                        ve.wait_ge(s_t1b.sem, 10 * (nn - 1) + 10)
                                    s_v1b.inc(ve.tensor_copy(kv_b[:, mt * 512:(mt + 1) * 512],
                                                             kvp[mt % 2][:, :]))
                                for lt in range(2):
                                    ve.wait_ge(s_t1b.sem, 10 * nn + 9 + lt)
                                    s_v1b.inc(ve.tensor_copy(
                                        latT[:, lt * S + nn * 512:lt * S + (nn + 1) * 512],
                                        latp[lt][:, :]))
                            assert s_v1b.n == 10 * NQC

                qT = stp1.enter_context(nc.sbuf_tensor("qT", [128, NKC * S], BF16, side="left"))
                with ExitStack() as stps, nc.Block() as blk:
                    qp = [stps.enter_context(nc.psum_tensor(f"qp{i}", [128, 512], F32)) for i in range(2)]

                    @blk.tensor
                    def _(te):
                        te.wait_ge(s_gw.sem, dmarec["wq"])
                        for mt in range(8):
                            for nn in range(NQC):
                                gi = mt * NQC + nn
                                if gi >= 2:
                                    te.wait_ge(s_s1c.sem, gi - 1)
                                last = None
                                for kc in range(NKC):
                                    last = te.matmul(
                                        qp[gi % 2][:, :],
                                        wq_b[:, kc * D + mt * 128:kc * D + (mt + 1) * 128],
                                        xnT[:, kc * S + nn * 512:kc * S + (nn + 1) * 512],
                                        start=(kc == 0), stop=(kc == NKC - 1))
                                s_t1c.inc(last)
                        assert s_t1c.n == 8 * NQC

                    @blk.scalar
                    def _(sc):
                        for mt in range(8):
                            for nn in range(NQC):
                                gi = mt * NQC + nn
                                sc.wait_ge(s_t1c.sem, gi + 1)
                                s_s1c.inc(sc.activation(
                                    qT[:, mt * S + nn * 512:mt * S + (nn + 1) * 512],
                                    qp[gi % 2][:, :], AF.Identity))
                        assert s_s1c.n == 8 * NQC

                    @blk.sync
                    def _(sy):
                        sy.wait_ge(s_qx.sem, dmarec["xnt_spill"])

                if debug:
                    with nc.Block() as blk:
                        @blk.sync
                        def _(sy):
                            dma_ser(sy, s_qx, dbg_lat, latT[:, :])
                            dma_ser(sy, s_qx, dbg_q, qT[:, :])
                            dma_ser(sy, s_qx, dbg_gsig, gsig[:, :])
                            dma_ser(sy, s_qx, dbg_w, wgt[:, :])
                            sy.wait_ge(s_qx.sem, s_qx.n)

            with ExitStack() as stp2:
                sb2 = lambda n, sh, dt: stp2.enter_context(nc.sbuf_tensor(n, sh, dt, side="right"))
                decayM = sb2("decayM", [128, 2 * S], BF16)
                with ExitStack() as stp2s:
                    dmraw = stp2s.enter_context(nc.sbuf_tensor("dmraw", [128, 2 * S], F32, side="right"))
                    with nc.Block() as blk:
                        @blk.gpsimd
                        def _(gp):
                            s_gpc.inc(gp.iota(dmraw[:, :], pattern=[[-1, 2 * S]], base=S - 1,
                                              channel_multiplier=1,
                                              allow_small_or_imprecise_dtypes=True))

                        @blk.scalar
                        def _(sc):
                            sc.wait_ge(s_gpc.sem, s_gpc.n)
                            s_s2s.inc(sc.activation(dmraw[:, :], dmraw[:, :], AF.Abs))
                            sc.wait_ge(s_s2s.sem, 1)
                            s_s2s.inc(sc.activation(decayM[:, :], dmraw[:, :], AF.Exp,
                                                    bias=consts[:, 2:3], scale=consts[:, 1:2]))

                krecT = sb2("krecT", [128, 2 * S], BF16)
                vrec = sb2("vrec", [128, NT * 260], BF16)
                sTb = [[sb2(f"sTb{L}{i}", [128, EXPB * 512], BF16) for i in range(2)]
                       for L in range(2)]
                expT = sTb
                pvT = sb2("pvT", [128, 2 * S], BF16)
                pvst = [[sb2(f"pvst{L}{i}", [64, 512], BF16) for i in range(2)]
                        for L in range(2)]
                bcsb = [sb2(f"bcsb{L}", [64, 512], F32) for L in range(2)]
                rr_sb = [sb2(f"rr_sb{L}", [128, 512], BF16) for L in range(2)]
                rln = [sb2(f"rln{L}", [128, 512], F32) for L in range(2)]

                NJ = 4 * NQC
                NJL = NJ // 2
                TQ2 = KT + NKG + 1
                SQ2 = NKG + 2
                VQ2 = KT + 2
                trec = {("ojL", 0): 0, ("ojL", 1): 0}
                oddrec = {}

                for g in range(4):
                    with ExitStack() as stps, nc.Block() as blk:
                        krp = [stps.enter_context(nc.psum_tensor(f"krp{g}{i}", [128, 512], F32)) for i in range(2)]
                        vrp = [stps.enter_context(nc.psum_tensor(f"vrp{g}{i}", [128, 256], F32)) for i in range(2)]
                        T0p, V0p = s_t2p.n, s_v2p.n
                        NKR = 2 * NQC

                        @blk.tensor
                        def _(te):
                            te.wait_ge(s_gw.sem, dmarec["wupv"])
                            if g == 0:
                                te.wait_ge(s_v1b.sem, 10 * NQC)
                            for mt in range(2):
                                for nn in range(NQC):
                                    gi = mt * NQC + nn
                                    if gi >= 2:
                                        te.wait_ge(s_v2p.sem, V0p + 1 + gi - 1)
                                    last = None
                                    for lc in range(2):
                                        last = te.matmul(
                                            krp[gi % 2][:, :],
                                            wupk_b[:, lc * D + g * 256 + mt * 128:lc * D + g * 256 + (mt + 1) * 128],
                                            latT[:, lc * S + nn * 512:lc * S + (nn + 1) * 512],
                                            start=(lc == 0), stop=(lc == 1))
                                    s_t2p.inc(last)
                            for tt in range(NT):
                                if tt >= 2:
                                    te.wait_ge(s_v2p.sem, V0p + 1 + NKR + tt - 1)
                                last = None
                                for lc in range(2):
                                    last = te.matmul(
                                        vrp[tt % 2][:, :],
                                        latT[:, lc * S + tt * 128:lc * S + (tt + 1) * 128],
                                        wupv_b[:, lc * D + g * 256:lc * D + (g + 1) * 256],
                                        start=(lc == 0), stop=(lc == 1))
                                s_t2p.inc(last)
                            assert s_t2p.n == T0p + NKR + NT

                        @blk.vector
                        def _(ve):
                            if g > 0:
                                ve.wait_ge(s_t2m.sem, TQ2 * NJL * g)
                                ve.wait_ge(s_t2mB.sem, TQ2 * NJL * g)
                            s_v2p.inc(ve.memset(vrec[:, :], 1.0))
                            for mt in range(2):
                                for nn in range(NQC):
                                    gi = mt * NQC + nn
                                    ve.wait_ge(s_t2p.sem, T0p + gi + 1)
                                    s_v2p.inc(ve.tensor_copy(
                                        krecT[:, mt * S + nn * 512:mt * S + (nn + 1) * 512],
                                        krp[gi % 2][:, :]))
                            ve.wait_ge(s_v2p.sem, V0p + 1)
                            for tt in range(NT):
                                ve.wait_ge(s_t2p.sem, T0p + NKR + tt + 1)
                                last = None
                                for lam in range(4):
                                    last = ve.tensor_copy(
                                        vrec[:, tt * 260 + lam * 65:tt * 260 + lam * 65 + 64],
                                        vrp[tt % 2][:, lam * 64:(lam + 1) * 64])
                                s_v2p.inc(last)
                            assert s_v2p.n == V0p + 1 + NKR + NT

                    with ExitStack() as stps, nc.Block() as blk:
                        scp = [[stps.enter_context(nc.psum_tensor(f"scp{g}{L}{i}", [128, 512], F32)) for i in range(2)]
                               for L in range(2)]
                        pvp = [stps.enter_context(nc.psum_tensor(f"pvp{g}{L}", [128, 512], F32)) for L in range(2)]
                        bcp = [stps.enter_context(nc.psum_tensor(f"bcp{g}{L}", [64, 512], F32)) for L in range(2)]
                        v2p_done = s_v2p.n
                        s1c_done = s_s1c.n
                        TC2 = [s_t2m, s_t2mB]
                        SC2 = [s_s2m, s_s2mB]
                        VC2 = [s_v2m, s_v2mB]

                        def jinfo(pj, L):
                            j = 2 * pj + L
                            Jl = g * NJL + pj
                            lam, qc = divmod(j, NQC)
                            return j, Jl, lam, qc

                        @blk.tensor
                        def _(te):
                            te.wait_ge(s_v2p.sem, v2p_done)
                            te.wait_ge(s_s1c.sem, s1c_done)

                            def emit_sc(pj, L, kg):
                                j, Jl, lam, qc = jinfo(pj, L)
                                pbase = 64 * (lam % 2)
                                ktile = lam // 2
                                qtile = (g * 4 + lam) // 2
                                for i in range(EXPB):
                                    kt = kg * EXPB + i
                                    m = KT * Jl + kt
                                    if m >= 2:
                                        jj, kk = divmod(m - 2, KT)
                                        te.wait_ge(VC2[L].sem, VQ2 * jj + kk + 1)
                                    trec[(L, Jl, "sc", kt)] = TC2[L].inc(te.matmul(
                                        scp[L][m % 2][:, :],
                                        krecT[pbase:pbase + 64,
                                              ktile * S + kt * 128:ktile * S + (kt + 1) * 128],
                                        qT[pbase:pbase + 64,
                                           qtile * S + qc * 512:qtile * S + (qc + 1) * 512],
                                        start=True, stop=True))

                            def emit_pv(pj, L, kg):
                                j, Jl, lam, qc = jinfo(pj, L)
                                te.wait_ge(SC2[L].sem, SQ2 * Jl + kg + 1)
                                if kg == 0 and Jl >= 1:
                                    te.wait_ge(VC2[L].sem, VQ2 * Jl)
                                last = None
                                for i in range(EXPB):
                                    kt = kg * EXPB + i
                                    last = te.matmul(
                                        pvp[L][0:65, :],
                                        vrec[:, kt * 260 + lam * 65:kt * 260 + lam * 65 + 65],
                                        expT[L][kg % 2][:, i * 512:(i + 1) * 512],
                                        start=(kt == 0), stop=(kt == KT - 1))
                                trec[(L, Jl, "pv", kg)] = TC2[L].inc(last)

                            def emit_bc(pj, L):
                                j, Jl, lam, qc = jinfo(pj, L)
                                te.wait_ge(SC2[L].sem, SQ2 * (Jl + 1))
                                if Jl >= 1:
                                    te.wait_ge(VC2[L].sem, VQ2 * (Jl - 1) + KT + 1)
                                trec[(L, Jl, "bc")] = TC2[L].inc(te.matmul(
                                    bcp[L][:, :], onesf[64:65, 0:64], rr_sb[L][64:65, :],
                                    start=True, stop=True))

                            for pj in range(NJL):
                                for kg in range(NKG):
                                    emit_sc(pj, 0, kg)
                                    emit_sc(pj, 1, kg)
                                    if kg >= 1:
                                        emit_pv(pj, 0, kg - 1)
                                        emit_pv(pj, 1, kg - 1)
                                emit_pv(pj, 0, NKG - 1)
                                emit_pv(pj, 1, NKG - 1)
                                emit_bc(pj, 0)
                                emit_bc(pj, 1)
                            assert s_t2m.n == TQ2 * NJL * (g + 1)
                            assert s_t2mB.n == TQ2 * NJL * (g + 1)

                        @blk.scalar
                        def _(sc):
                            def emit_exp(pj, L, kg):
                                j, Jl, lam, qc = jinfo(pj, L)
                                sc.wait_ge(VC2[L].sem, VQ2 * Jl + EXPB * (kg + 1))
                                SC2[L].inc(sc.activation(expT[L][kg % 2][:, :],
                                                         sTb[L][kg % 2][:, :], AF.Exp))

                            def emit_rr(pj, L):
                                j, Jl, lam, qc = jinfo(pj, L)
                                sc.wait_ge(TC2[L].sem, trec[(L, Jl, "pv", NKG - 1)])
                                if Jl >= 1:
                                    sc.wait_ge(TC2[L].sem, trec[(L, Jl - 1, "bc")])
                                SC2[L].inc(sc.activation(rln[L][64:65, :],
                                                         pvp[L][64:65, :], AF.Ln))
                                sc.wait_ge(SC2[L].sem, SQ2 * Jl + SQ2 - 1)
                                SC2[L].inc(sc.activation(rr_sb[L][64:65, :], rln[L][64:65, :],
                                                         AF.Exp, scale=consts[64:65, 3:4]))

                            for pj in range(NJL):
                                for kg in range(NKG):
                                    emit_exp(pj, 0, kg)
                                    emit_exp(pj, 1, kg)
                                emit_rr(pj, 0)
                                emit_rr(pj, 1)
                            assert s_s2m.n == SQ2 * NJL * (g + 1)
                            assert s_s2mB.n == SQ2 * NJL * (g + 1)

                        @blk.sync
                        def _(sy):
                            ojs = [trec.get(("sync_ojL", 0), 0), trec.get(("sync_ojL", 1), 0)]
                            for pj in range(NJL):
                                for L in range(2):
                                    j, Jl, lam, qc = jinfo(pj, L)
                                    if lam % 2 == 1:
                                        sy.wait_ge(VC2[L].sem, VQ2 * (Jl + 1))
                                        oddrec[("odd", L, ojs[L])] = dma_ser(sy, s_qx,
                                            pvT[64:128, (lam // 2) * S + qc * 512:(lam // 2) * S + (qc + 1) * 512],
                                            pvst[L][ojs[L] % 2][0:64, :])
                                        ojs[L] += 1
                            trec[("sync_ojL", 0)], trec[("sync_ojL", 1)] = ojs
                            sy.wait_ge(s_v2m.sem, VQ2 * NJL * (g + 1))
                            sy.wait_ge(s_v2mB.sem, VQ2 * NJL * (g + 1))
                            dmarec[("pvspill", g)] = dma_ser(sy, s_qx, pv_sp[:, (2 * g) * S:(2 * g + 2) * S], pvT[:, :])

                        @blk.vector
                        def _(ve):
                            if g == 0:
                                ve.wait_ge(s_s2s.sem, 2)
                            else:
                                ve.wait_ge(s_qx.sem, dmarec[("pvspill", g - 1)])

                            def emit_tt(pj, L, kt):
                                j, Jl, lam, qc = jinfo(pj, L)
                                kg = kt // EXPB
                                pjx = NKG * Jl + kg
                                ve.wait_ge(TC2[L].sem, trec[(L, Jl, "sc", kt)])
                                if pjx >= 2:
                                    jj, kk = divmod(pjx - 2, NKG)
                                    ve.wait_ge(TC2[L].sem, trec[(L, jj, "pv", kk)])
                                c0 = (S - 1) - 128 * kt + 512 * qc
                                VC2[L].inc(ve.tensor_mul(
                                    sTb[L][kg % 2][:, (kt % EXPB) * 512:(kt % EXPB + 1) * 512],
                                    scp[L][(KT * Jl + kt) % 2][:, :],
                                    decayM[:, c0:c0 + 512]))

                            def emit_norm(pj, L):
                                j, Jl, lam, qc = jinfo(pj, L)
                                odd = (lam % 2 == 1)
                                ve.wait_ge(TC2[L].sem, trec[(L, Jl, "bc")])
                                if Jl >= 1:
                                    ve.wait_ge(VC2[L].sem, VQ2 * (Jl - 1) + KT + 2)
                                VC2[L].inc(ve.tensor_copy(bcsb[L][:, :], bcp[L][:, :]))
                                ve.wait_ge(VC2[L].sem, VQ2 * Jl + KT + 1)
                                if odd:
                                    ojL = trec[("ojL", L)]
                                    if ("odd", L, ojL - 2) in oddrec:
                                        ve.wait_ge(s_qx.sem, oddrec[("odd", L, ojL - 2)])
                                    dst = pvst[L][ojL % 2][0:64, :]
                                    trec[("ojL", L)] = ojL + 1
                                else:
                                    dst = pvT[0:64, (lam // 2) * S + qc * 512:(lam // 2) * S + (qc + 1) * 512]
                                VC2[L].inc(ve.tensor_mul(dst, pvp[L][0:64, :], bcsb[L][:, :]))

                            for pj in range(NJL):
                                for kg in range(NKG):
                                    for L in range(2):
                                        for i in range(EXPB):
                                            emit_tt(pj, L, kg * EXPB + i)
                                emit_norm(pj, 0)
                                emit_norm(pj, 1)
                            assert s_v2m.n == VQ2 * NJL * (g + 1)
                            assert s_v2mB.n == VQ2 * NJL * (g + 1)

        with ExitStack() as stp3:
            sb3 = lambda n, sh, dt: stp3.enter_context(nc.sbuf_tensor(n, sh, dt, side="right"))
            pvF = sb3("pvF", [128, NKC * S], BF16)
            wo_b = sb3("wo_b", [128, NKC * D], BF16)
            ast = sb3("ast", [128, 2 * D], F32)
            with ExitStack() as stps, nc.Block() as blk:
                wop = [stps.enter_context(nc.psum_tensor(f"wop{i}", [128, 512], F32)) for i in range(2)]

                @blk.gpsimd
                def _(gp):
                    dmarec["wo"] = dma_ser(gp, s_gw, wo_b.ap().rearrange("p (kc d) -> p kc d", kc=NKC),
                        wo_h.rearrange("(kc p) d -> p kc d", p=128))

                @blk.sync
                def _(sy):
                    sy.wait_ge(s_qx.sem, dmarec[("pvspill", 3)])
                    dmarec["pvload"] = dma_ser(sy, s_qx, pvF[:, :], pv_sp)
                    for tt in range(NT):
                        sy.wait_ge(s_v3.sem, 2 * tt + 2)
                        dmarec[("attn", tt)] = dma_ser(sy, s_qx, attn_sp[tt * 128:(tt + 1) * 128, :],
                            ast[:, (tt % 2) * D:(tt % 2 + 1) * D])

                @blk.tensor
                def _(te):
                    te.wait_ge(s_gw.sem, dmarec["wo"])
                    te.wait_ge(s_qx.sem, dmarec["pvload"])
                    for tt in range(NT):
                        for nn in range(2):
                            gi = 2 * tt + nn
                            if gi >= 2:
                                te.wait_ge(s_v3.sem, gi - 1)
                            last = None
                            for kc in range(NKC):
                                last = te.matmul(
                                    wop[gi % 2][:, :],
                                    pvF[:, kc * S + tt * 128:kc * S + (tt + 1) * 128],
                                    wo_b[:, kc * D + nn * 512:kc * D + (nn + 1) * 512],
                                    start=(kc == 0), stop=(kc == NKC - 1))
                            s_t3.inc(last)
                    assert s_t3.n == 2 * NT

                @blk.vector
                def _(ve):
                    for tt in range(NT):
                        for nn in range(2):
                            gi = 2 * tt + nn
                            ve.wait_ge(s_t3.sem, gi + 1)
                            if tt >= 2 and nn == 0:
                                ve.wait_ge(s_qx.sem, dmarec[("attn", tt - 2)])
                            s_v3.inc(ve.tensor_copy(
                                ast[:, (tt % 2) * D + nn * 512:(tt % 2) * D + (nn + 1) * 512],
                                wop[gi % 2][:, :]))
                    assert s_v3.n == 2 * NT

        with ExitStack() as stp4:
            sb4 = lambda n, sh, dt: stp4.enter_context(nc.sbuf_tensor(n, sh, dt, side="right"))
            xnTh = sb4("xnTh", [128, NKC * TH], BF16)
            comb = sb4("comb", [128, NTTH * D], F32)
            wu_sb = sb4("wu_sb", [128, NKC * FCB], BF16)
            wd_sb = sb4("wd_sb", [128, NFT * D], BF16)
            a1s = sb4("a1s", [128, NFT * TC], BF16)
            bu_sb = sb4("bu_sb", [128, 2 * (FF // 128)], F32)
            wbf = sb4("wbf", [128, 8], BF16)
            xst5 = sb4("xst5", [128, 2 * D], F32)
            att5 = sb4("att5", [128, 2 * D], F32)

            TQ4 = NTC * (NFT + NTI)
            SQ4 = NTC * NFT
            VQ4 = NTTH
            H1LAST = (NTC - 1) * (NFT + NTI) + NFT

            def t4_h1_val(bl, tc, ft):
                return bl * TQ4 + tc * (NFT + NTI) + ft + 1

            def t4_y_val(bl, tc, tti):
                return bl * TQ4 + tc * (NFT + NTI) + NFT + tti + 1

            for half in range(2):
                with ExitStack() as stps, nc.Block() as blk:
                    trwp = stps.enter_context(nc.psum_tensor(f"trw{half}", [8, 128], BF16))
                    wbdp = [stps.enter_context(nc.psum_tensor(f"wbd{half}{i}", [128, D], F32)) for i in range(2)]
                    T0w, V0w = s_t4w.n, s_v4w.n

                    @blk.sync
                    def _(sy):
                        dmarec[("xnth", half)] = dma_ser(sy, s_qx, xnTh.ap().rearrange("p (kc t) -> p kc t", kc=NKC),
                            xnt_sp.rearrange("p (kc t) -> p kc t", kc=NKC)[:, :, half * TH:(half + 1) * TH])
                        for tt in range(NTTH):
                            r0 = (half * NTTH + tt) * 128
                            if tt >= 2:
                                sy.wait_ge(s_v4w.sem, V0w + 5 * (tt - 2) + 5)
                            dmarec[("x5", half, tt)] = dma_ser(sy, s_qx, xst5[:, (tt % 2) * D:(tt % 2 + 1) * D],
                                x_h[r0:r0 + 128, :])
                            dmarec[("a5", half, tt)] = dma_ser(sy, s_qx, att5[:, (tt % 2) * D:(tt % 2 + 1) * D],
                                attn_sp[r0:r0 + 128, :])

                    @blk.tensor
                    def _(te):
                        te.wait_ge(s_gw.sem, dmarec["bd"])
                        te.wait_ge(s_qx.sem, dmarec["sbdrow"])
                        for tt in range(NTTH):
                            te.wait_ge(s_v4w.sem, V0w + 5 * tt + 1)
                            if tt >= 1:
                                te.wait_ge(s_v4w.sem, V0w + 5 * (tt - 1) + 2)
                            s_t4w.inc(te.transpose(trwp[:, :], wbf[:, :], ident[:, :]))
                            te.wait_ge(s_v4w.sem, V0w + 5 * tt + 2)
                            if tt >= 2:
                                te.wait_ge(s_v4w.sem, V0w + 5 * (tt - 2) + 3)
                            last = None
                            for nn in range(2):
                                last = te.matmul(wbdp[tt % 2][:, nn * 512:(nn + 1) * 512],
                                                 wt_sb[0:9, :],
                                                 bd_aug[0:9, nn * 512:(nn + 1) * 512],
                                                 start=True, stop=True)
                            s_t4w.inc(last)
                        assert s_t4w.n == T0w + 2 * NTTH

                    @blk.vector
                    def _(ve):
                        if half == 1:
                            ve.wait_ge(s_qx.sem, dmarec[("out", 0, NTTH - 1)])
                        ve.wait_ge(s_v0.sem, 1)
                        for tt in range(NTTH):
                            gt = half * NTTH + tt
                            if tt >= 1:
                                ve.wait_ge(s_t4w.sem, T0w + 2 * (tt - 1) + 1)
                            s_v4w.inc(ve.tensor_copy(wbf[:, :],
                                                     wgt[:, gt * NE:(gt + 1) * NE]))
                            ve.wait_ge(s_t4w.sem, T0w + 2 * tt + 1)
                            s_v4w.inc(ve.tensor_copy(wt_sb[0:8, :], trwp[:, :]))
                            ve.wait_ge(s_t4w.sem, T0w + 2 * tt + 2)
                            s_v4w.inc(ve.tensor_copy(comb[:, tt * D:(tt + 1) * D],
                                                     wbdp[tt % 2][:, :]))
                            ve.wait_ge(s_qx.sem, dmarec[("a5", half, tt)])
                            ve.wait_ge(s_v4w.sem, V0w + 5 * tt + 3)
                            s_v4w.inc(ve.tensor_add(comb[:, tt * D:(tt + 1) * D],
                                                    comb[:, tt * D:(tt + 1) * D],
                                                    xst5[:, (tt % 2) * D:(tt % 2 + 1) * D]))
                            ve.wait_ge(s_v4w.sem, V0w + 5 * tt + 4)
                            s_v4w.inc(ve.tensor_add(comb[:, tt * D:(tt + 1) * D],
                                                    comb[:, tt * D:(tt + 1) * D],
                                                    att5[:, (tt % 2) * D:(tt % 2 + 1) * D]))
                        assert s_v4w.n == V0w + 5 * NTTH

                with ExitStack() as stps, nc.Block() as blk:
                    h1p = [stps.enter_context(nc.psum_tensor(f"h1p{half}{i}", [128, TC], F32)) for i in range(2)]
                    yp = [stps.enter_context(nc.psum_tensor(f"yp{half}{i}", [128, D], F32)) for i in range(2)]
                    T0, S0, V0 = s_t4.n, s_s4.n, s_v4.n
                    B0 = T0 // TQ4
                    v4w_done = s_v4w.n

                    @blk.gpsimd
                    def _(gp):
                        for e in range(9):
                            for fc in range(NFC):
                                b = B0 + e * NFC + fc
                                wsrc = wut_h[e, :, :] if e < NE else swut_h
                                dsrc = wdt_h[e, :, :] if e < NE else swdt_h
                                if b >= 1:
                                    gp.wait_ge(s_t4.sem, TQ4 * (b - 1) + H1LAST)
                                dmarec[("wu", b)] = dma_ser(gp, s_gw, wu_sb.ap().rearrange("p (kc f) -> p kc f", kc=NKC),
                                    wsrc[:, fc * FCB:(fc + 1) * FCB].rearrange(
                                        "(kc p) f -> p kc f", p=128))
                                if b >= 1:
                                    gp.wait_ge(s_t4.sem, TQ4 * b)
                                dmarec[("wd", b)] = dma_ser(gp, s_gw, wd_sb.ap().rearrange("p (fi d) -> p fi d", fi=NFT),
                                    dsrc[fc * FCB:(fc + 1) * FCB, :].rearrange(
                                        "(fi p) d -> p fi d", p=128))

                    @blk.sync
                    def _(sy):
                        for e in range(9):
                            b = B0 + e * NFC
                            if e >= 2:
                                sy.wait_ge(s_s4.sem, SQ4 * (b - NFC))
                            src = bu2_h[e, :, :] if e < NE else sbu2_h
                            dmarec[("bu", half, e)] = dma_ser(sy, s_qx, bu_sb[:, (e % 2) * 32:(e % 2) * 32 + 32], src)

                    @blk.tensor
                    def _(te):
                        te.wait_ge(s_qx.sem, dmarec[("xnth", half)])
                        for e in range(9):
                            for fc in range(NFC):
                                bl = e * NFC + fc
                                b = B0 + bl
                                te.wait_ge(s_gw.sem, dmarec[("wu", b)])
                                for tc in range(NTC):
                                    for ft in range(NFT):
                                        hh = SQ4 * b + tc * NFT + ft
                                        if hh >= 2:
                                            te.wait_ge(s_s4.sem, hh - 1)
                                        last = None
                                        for kc in range(NKC):
                                            for no, nw in nsplit(TC):
                                                last = te.matmul(
                                                    h1p[(tc * NFT + ft) % 2][:, no:no + nw],
                                                    wu_sb[:, kc * FCB + ft * 128:kc * FCB + (ft + 1) * 128],
                                                    xnTh[:, kc * TH + tc * TC + no:kc * TH + tc * TC + no + nw],
                                                    start=(kc == 0), stop=(kc == NKC - 1))
                                        s_t4.inc(last)
                                    te.wait_ge(s_gw.sem, dmarec[("wd", b)])
                                    for tti in range(NTI):
                                        tt = tc * NTI + tti
                                        yy = VQ4 * b + tt
                                        if yy >= 2:
                                            te.wait_ge(s_v4.sem, yy - 1)
                                        te.wait_ge(s_s4.sem, SQ4 * b + NFT * (tc + 1))
                                        last = None
                                        for kc in range(NFT):
                                            for no, nw in nsplit(D):
                                                last = te.matmul(
                                                    yp[tt % 2][:, no:no + nw],
                                                    a1s[:, kc * TC + tti * 128:kc * TC + (tti + 1) * 128],
                                                    wd_sb[:, kc * D + no:kc * D + no + nw],
                                                    start=(kc == 0), stop=(kc == NFT - 1))
                                        s_t4.inc(last)
                        assert s_t4.n == T0 + TQ4 * 9 * NFC

                    @blk.scalar
                    def _(sc):
                        for e in range(9):
                            for fc in range(NFC):
                                bl = e * NFC + fc
                                b = B0 + bl
                                if fc == 0:
                                    sc.wait_ge(s_qx.sem, dmarec[("bu", half, e)])
                                for tc in range(NTC):
                                    for ft in range(NFT):
                                        sc.wait_ge(s_t4.sem, T0 + t4_h1_val(bl, tc, ft))
                                        s_s4.inc(sc.activation(
                                            a1s[:, ft * TC:(ft + 1) * TC],
                                            h1p[(tc * NFT + ft) % 2][:, :], AF.Gelu,
                                            bias=bu_sb[:, (e % 2) * 32 + fc * NFT + ft:
                                                       (e % 2) * 32 + fc * NFT + ft + 1]))
                        assert s_s4.n == S0 + SQ4 * 9 * NFC

                    @blk.vector
                    def _(ve):
                        ve.wait_ge(s_v4w.sem, v4w_done)
                        for e in range(9):
                            for fc in range(NFC):
                                bl = e * NFC + fc
                                b = B0 + bl
                                for tc in range(NTC):
                                    for tti in range(NTI):
                                        tt = tc * NTI + tti
                                        ve.wait_ge(s_t4.sem, T0 + t4_y_val(bl, tc, tti))
                                        gt = half * NTTH + tt
                                        scalar = (wgt[:, gt * NE + e:gt * NE + e + 1]
                                                  if e < NE else 0.1)
                                        yy = VQ4 * bl + tt
                                        if yy >= NTTH:
                                            ve.wait_ge(s_v4.sem, V0 + yy - NTTH + 1)
                                        s_v4.inc(ve.scalar_tensor_tensor(
                                            comb[:, tt * D:(tt + 1) * D],
                                            yp[tt % 2][:, :], scalar,
                                            comb[:, tt * D:(tt + 1) * D],
                                            op0=ALU.mult, op1=ALU.add))
                        assert s_v4.n == V0 + VQ4 * 9 * NFC

                with nc.Block() as blk:
                    v4done = s_v4.n

                    @blk.sync
                    def _(sy):
                        for tt in range(NTTH):
                            r0 = (half * NTTH + tt) * 128
                            sy.wait_ge(s_v4.sem, v4done - VQ4 + tt + 1)
                            dmarec[("out", half, tt)] = dma_ser(sy, s_qx, out_h[r0:r0 + 128, :],
                                comb[:, tt * D:(tt + 1) * D])
                        sy.wait_ge(s_qx.sem, dmarec[("out", half, NTTH - 1)])

    return nc


def _prep_inputs(L0_attn_x, L0_attn_Wq, L0_attn_Wk, L0_attn_Wv, L0_attn_Wo,
                 L0_attn_W_down_kv, L0_attn_W_up_k, L0_attn_W_up_v,
                 L0_ffn_gate_w, L0_ffn_gate_bias, L0_ffn_Wu, L0_ffn_bu,
                 L0_ffn_Wd, L0_ffn_bd, L0_ffn_sWu, L0_ffn_sbu, L0_ffn_sWd,
                 L0_ffn_sbd, S):
    c = np.ascontiguousarray
    f = lambda a: np.asarray(a, dtype=np.float32)
    shared = {
        "wq": c(f(L0_attn_Wq)),
        "wk": c(f(L0_attn_Wk)),
        "wv": c(f(L0_attn_Wv)),
        "wo": c(f(L0_attn_Wo)),
        "wdn": c(f(L0_attn_W_down_kv)),
        "wupk": c(f(L0_attn_W_up_k)),
        "wupv": c(f(L0_attn_W_up_v)),
        "gwt": c(f(L0_ffn_gate_w).T),
        "gbias": c(f(L0_ffn_gate_bias).reshape(1, NE)),
        "wut": c(f(L0_ffn_Wu).transpose(0, 2, 1)),
        "bu2": c(f(L0_ffn_bu).reshape(NE, FF // 128, 128).transpose(0, 2, 1)),
        "wdt": c(f(L0_ffn_Wd).transpose(0, 2, 1)),
        "bd": c(f(L0_ffn_bd)),
        "swut": c(f(L0_ffn_sWu).T),
        "sbu2": c(f(L0_ffn_sbu).reshape(FF // 128, 128).T),
        "swdt": c(f(L0_ffn_sWd).T),
        "sbd": c(f(L0_ffn_sbd).reshape(1, D)),
    }
    x = f(L0_attn_x)
    in_maps = []
    for b in range(x.shape[0]):
        m = dict(shared)
        m["x"] = c(x[b, :S, :])
        in_maps.append(m)
    return in_maps


def kernel(**inputs):
    S = 4096
    in_maps = _prep_inputs(S=S, **inputs)
    nc = build(S=S)
    res = run_bass_kernel_spmd(nc, in_maps, core_ids=list(range(len(in_maps))))
    return np.stack([r["out"] for r in res.results], axis=0)
